# version 10
# baseline (speedup 1.0000x reference)
import numpy as np
import ml_dtypes
import concourse.bass as bass
import concourse.mybir as mybir
from concourse.bass_utils import run_bass_kernel_spmd
from concourse.aot_env import allow_ici_all_to_all

F32 = mybir.dt.float32
BF16 = mybir.dt.bfloat16
AF = mybir.ActivationFunctionType
ALU = mybir.AluOpType
AXX = mybir.AxisListType.X

NCORE = 8
P = 128


class Chain:
    def __init__(self, nc, name, registry=None):
        self._cm = nc.semaphore(name)
        self.sem = self._cm.__enter__()
        self.name = name
        self.count = 0
        if registry is not None:
            registry[name] = self

    def inc(self, instr, amt=1):
        instr.then_inc(self.sem, amt)
        self.count += amt
        return (self, self.count)

    def tok(self):
        return (self, self.count)


class Em:
    def __init__(self):
        self.e = None
        self.waited = {}

    def wait(self, tok):
        if tok is None:
            return
        chain, val = tok
        if self.waited.get(chain.name, 0) < val:
            self.e.wait_ge(chain.sem, val)
            self.waited[chain.name] = val


class CFG:
    def __init__(self, B=2, L=4096, H=2048, I=4096, N=16, R=128, K=4,
                 E=8, FM=1408, FS=5632, debug=False):
        self.B, self.L, self.H, self.I, self.N, self.R, self.K = B, L, H, I, N, R, K
        self.E, self.FM, self.FS = E, FM, FS
        self.IC = I // NCORE
        self.TC = L // NCORE
        self.NIT = self.IC // P
        self.NHT = H // P
        self.NFM = (FM + P - 1) // P
        self.NFS = FS // P
        self.NTT = self.TC // P
        self.ALC = min(512, L)
        self.NALC = L // self.ALC
        self.NAT = self.ALC // P
        self.SLC = min(256, L)
        self.NSLC = L // self.SLC
        self.MODSL = 6 * H // NCORE
        self.debug = debug


class KB:
    def __init__(self, cfg, rec=None):
        self.c = cfg
        self.nc = bass.Bass("TRN2", target_bir_lowering=False)
        self.tok = {}
        self.ems = {}
        self.dbg = {}
        self.rec = rec
        self.chains = {}

    def T(self, name):
        t = self.tok.get(name)
        if t is not None:
            return t
        if self.rec is None:
            return None
        cname, val = self.rec[name]
        return (self.chains[cname], val)

    def on(self, eng, fn):
        em = self.ems.setdefault(eng, Em())

        def wrap(e):
            em.e = e
            fn(e, em)

        getattr(self.block, eng)(wrap)

    def alloc(self):
        c, nc = self.c, self.nc
        B, L, H, I, N, R, K = c.B, c.L, c.H, c.I, c.N, c.R, c.K
        E, FM, FS, IC, TC = c.E, c.FM, c.FS, c.IC, c.TC
        NIT, NHT = c.NIT, c.NHT

        def din(name, shape):
            return nc.dram_tensor(name, list(shape), F32, kind="ExternalInput").ap()

        self.x_in = din("x", [B, L, H])
        self.xq_in = din("xq", [B, TC, H])
        self.ipT_d = din("ipT", [H, 2 * IC])
        self.convw_d = din("convw", [IC, K])
        self.convb_d = din("convb", [IC, 1])
        self.xpT_d = din("xpT", [IC, R + 2 * N])
        self.dtpT_d = din("dtpT", [R, IC])
        self.dtb_d = din("dtb", [IC, 1])
        self.alog_d = din("alog", [IC, N])
        self.dvec_d = din("dvec", [IC, 1])
        self.opT_d = din("opT", [I, H])
        self.fc1T_d = din("fc1T", [E, H, FM])
        self.fc2T_d = din("fc2T", [E, H, FM])
        self.prT_d = din("prT", [E, FM, H])
        self.sf1T_d = din("sf1T", [H, FS])
        self.sf2T_d = din("sf2T", [H, FS])
        self.sprT_d = din("sprT", [FS, H])
        self.gwT_d = din("gwT", [H, E])
        self.sgwT_d = din("sgwT", [H, 1])
        self.cvec_d = din("c_in", [B, H])
        self.wmod_d = din("wmod", [H, c.MODSL])
        self.out_d = nc.dram_tensor("out", [B, TC, H], F32, kind="ExternalOutput").ap()

        def dint(name, shape, dt=F32):
            return nc.dram_tensor(name, list(shape), dt, kind="Internal")

        self.modsl_d = dint("modsl_d", [B, c.MODSL])
        self.modg_d = dint("modg_d", [NCORE * B, c.MODSL])
        self.modflat_d = dint("modflat_d", [B, 6 * H])
        self.ssmp_d = [dint(f"ssmp_d{b}", [R + 2 * N, L]) for b in range(B)]
        self.ssm_d = [dint(f"ssm_d{b}", [R + 2 * N, L]) for b in range(B)]
        self.usp_d = [dint(f"usp_d{b}", [IC, L]) for b in range(B)]
        self.gsp_d = [dint(f"gsp_d{b}", [IC, L]) for b in range(B)]
        self.a2ain_d = [dint(f"a2ain_d{b}", [NCORE * IC, TC], BF16) for b in range(B)]
        self.a2aout_d = [dint(f"a2aout_d{b}", [NCORE * IC, TC], BF16) for b in range(B)]
        self.cmb_d = [dint(f"cmb_d{b}", [E, TC]) for b in range(B)]
        self.x2_d = [dint(f"x2_d{b}", [TC, H]) for b in range(B)]
        self.h2b_d0 = dint("h2b_d0", [128, NHT * TC], BF16)
        self.dtpre_d = [dint(f"dtpre_d{b}", [IC, L]) for b in range(B)]
        self.rowsd = dint("rowsd", [1, 16384])

        if c.debug:
            def dout(name, shape):
                self.dbg[name] = nc.dram_tensor("dbg_" + name, list(shape), F32,
                                                kind="ExternalOutput").ap()
            dout("u", [B, IC, L]); dout("gate", [B, IC, L])
            dout("ssm", [B, R + 2 * N, L]); dout("dt", [B, IC, L])
            dout("y", [B, IC, L]); dout("x2", [B, TC, H])
            dout("h2", [B, H, TC]); dout("cmb", [B, E, TC])

        def sb(name, shape, dt=F32):
            return nc.alloc_sbuf_tensor(name, list(shape), dt)

        self.id_f32 = sb("id_f32", [P, P])
        self.onescol_bf = sb("onescol_bf", [P, 1], BF16)
        self.onesrow_f = sb("onesrow_f", [1, P])
        self.sa_bf = [sb(f"sa_bf{b}", [P, NHT], BF16) for b in range(B)]
        self.ca1_t = [sb(f"ca1_t{b}", [P, NHT]) for b in range(B)]
        self.A_sb = sb("A_sb", [P, NIT * N])
        self.D_sb = sb("D_sb", [P, NIT])
        self.dtb_sb = sb("dtb_sb", [P, NIT])
        self.convb_sb = sb("convb_sb", [P, NIT])
        self.cw_sb = sb("cw_sb", [P, NIT * K])
        self.diag = [[sb(f"diag_{it}_{k}", [P, P], BF16) for k in range(K)]
                     for it in range(NIT)]
        self.xpT_bf = [sb(f"xpT_bf{it}", [P, R + 2 * N], BF16) for it in range(NIT)]
        self.dtpT_bf = sb("dtpT_bf", [P, IC], BF16)
        self.gwT_sb = sb("gwT_sb", [P, NHT * E])
        self.sgwT_sb = sb("sgwT_sb", [P, NHT])
        self.h2rt = [sb(f"h2rt{j}", [P, P]) for j in range(2)]
        self.ubias = sb("ubias", [P, NIT])
        self.gbias = sb("gbias", [P, NIT])
        self.wsum = sb("wsum", [P, 2 * NIT])
        self.carry = sb("carry", [P, B * N * NIT])
        self.csT = sb("csT", [P, NHT * B])
        self.tssl = sb("tssl", [P, 2 * c.ALC], BF16)
        self.dtps = sb("dtps", [P, 1024])
        self.r_tm = sb("r_tm", [P, c.NTT * E])
        self.rt1 = sb("rt1", [P, c.NTT * E])
        self.scol = sb("scol", [P, 16])
        self.sgbc = sb("sgbc", [P, B * TC])
        self.cmbbc = sb("cmbbc", [P, B * TC])
        self.wst = [sb(f"wst{j}", [P, 1024], BF16) for j in range(2)]
        self.tev = sb("tev", [P, 512])
        self.hin = [sb(f"hin{it}", [P, 3 + c.ALC], BF16) for it in range(NIT)]
        self.rows = sb("rows", [1, 4096])
        self.AR1 = sb("AR1", [P, 8192])
        self.AR2 = sb("AR2", [P, 8192])
        self.AR3 = sb("AR3", [P, 8192])
        self.AR4 = sb("AR4", [P, 8192])
        self.AR5 = sb("AR5", [P, 3072])
        self.AR6 = sb("AR6", [P, 4096])
        self.csml = self.AR4[0:B, 0:H]
        self.modrow = self.AR4[0:B, H:H + c.MODSL]
        self.rsb = self.AR1[0:E, 2048:2048 + TC]
        self.cmbsb = self.rsb

        self.psA = nc.alloc_psum_tensor("psA", [P, 1024], F32)
        self.psB = nc.alloc_psum_tensor("psB", [P, 1024], F32)
        self.psC = nc.alloc_psum_tensor("psC", [P, 1024], F32)
        self.psD = nc.alloc_psum_tensor("psD", [P, 1024], F32)
        self.PSL = [self.psA[:, 0:512], self.psA[:, 512:1024],
                    self.psB[:, 0:512], self.psB[:, 512:1024],
                    self.psC[:, 0:512], self.psC[:, 512:1024],
                    self.psD[:, 0:512], self.psD[:, 512:1024]]

    def v32(self, ar, c0, n):
        return ar[:, c0:c0 + n]

    def vbf(self, ar, c0, nbf):
        return ar[:, c0:c0 + nbf // 2].bitcast(BF16)

    def bcast(self, dst, src_dram_row):
        import concourse.bass as _B
        _, a2 = _B.broadcast_tensor_aps(dst, src_dram_row)
        return a2

    def emit(self):
        c = self.c
        nc = self.nc
        with nc.Block() as block:
            self.block = block
            self.cd = Chain(nc, "cd", self.chains)
            self.cg = Chain(nc, "cg", self.chains)
            self.cgM = Chain(nc, "cgM", self.chains)
            self.cdM = Chain(nc, "cdM", self.chains)
            self.st = [Chain(nc, "st0", self.chains), Chain(nc, "st1", self.chains)]
            self.stg = [Chain(nc, f"stg{j}", self.chains) for j in range(4)]
            self.xs = Chain(nc, "xs", self.chains)
            self.spU = Chain(nc, "spU", self.chains)
            self.spG = Chain(nc, "spG", self.chains)
            self.ug = Chain(nc, "ug", self.chains)
            self.br = [Chain(nc, f"br{j}", self.chains) for j in range(3)]
            self.ssmW = [Chain(nc, f"ssmW{j}", self.chains) for j in range(2)]
            self.ygW = Chain(nc, "ygW", self.chains)
            self.xq = [Chain(nc, "xq0", self.chains), Chain(nc, "xq1", self.chains)]
            self.ya = Chain(nc, "ya", self.chains)
            self.cr = [Chain(nc, "cr0", self.chains), Chain(nc, "cr1", self.chains)]
            self.outW = Chain(nc, "outW", self.chains)
            self.dbgc = Chain(nc, "dbgc", self.chains)
            self.dbg2 = Chain(nc, "dbg2", self.chains)
            self.dbgD = Chain(nc, "dbgD", self.chains)
            self.dbgY = Chain(nc, "dbgY", self.chains)
            self.cq = Chain(nc, "cq", self.chains)
            self.cv = Chain(nc, "cv", self.chains)
            self.ca = Chain(nc, "ca", self.chains)
            self.ct = Chain(nc, "ct", self.chains)
            self.cp = Chain(nc, "cp", self.chains)
            self.csc = Chain(nc, "csc", self.chains)
            self.rbc = Chain(nc, "rbc", self.chains)
            self.cx2 = Chain(nc, "cx2", self.chains)
            self.ch2 = Chain(nc, "ch2", self.chains)
            self.cdl = [Chain(nc, f"cdl{j}", self.chains) for j in range(2)]
            self.cdw = [Chain(nc, f"cdw{j}", self.chains) for j in range(2)]
            self.cdl2 = [Chain(nc, f"cdl2{j}", self.chains) for j in range(2)]
            self.cbc = Chain(nc, "cbc", self.chains)
            self.ph_init()
            self.ph_mod()
            self.ph_wprep()
            self.ph_A_head(0)
            for lc in range(c.NALC):
                self.ph_A_chunk(0, lc)
            self.ph_AR(0)
            self.ph_dtpre(0)
            if c.B > 1:
                self.ph_A_head(1)
                for i in range(c.NALC):
                    self.ph_B_chunk(0, 2 * i)
                    self.ph_B_chunk(0, 2 * i + 1)
                    self.ph_A_chunk(1, i)
                self.ph_B_tail(0)
                self.ph_AR(1)
                self.ph_dtpre(1)
                NS = c.NSLC
                t1, t2, t3 = 1, NS // 2, (3 * NS) // 4
                for slc in range(NS):
                    self.ph_B_chunk(1, slc)
                    if slc == t1:
                        self.ph_C_p1(0)
                    elif slc == t2:
                        self.ph_C_p2(0)
                    elif slc == t3:
                        self.ph_C_p3(0)
                self.ph_B_tail(1)
                self.ph_C_p1(1)
                self.ph_C_p2(1)
                self.ph_C_p3(1)
            else:
                for slc in range(c.NSLC):
                    self.ph_B_chunk(0, slc)
                self.ph_B_tail(0)
                self.ph_C_p1(0)
                self.ph_C_p2(0)
                self.ph_C_p3(0)
            self.ph_D()
            self.ph_final()
        return self.nc

    def ph_init(self):
        c, tok = self.c, self.tok
        cd, cg, cv, cp = self.cd, self.cg, self.cv, self.cp
        N, R, K, NIT, NHT, E = c.N, c.R, c.K, c.NIT, c.NHT, c.E
        self.xpT_f = self.v32(self.AR5, 0, NIT * (R + 2 * N))
        self.dtpT_f = self.v32(self.AR5, NIT * (R + 2 * N), c.IC)
        self.gw_f = self.v32(self.AR5, NIT * (R + 2 * N) + c.IC, NHT * E + NHT)

        def s_sync(e, m):
            cd.inc(e.dma_start(out=self.csml[0:c.B, :], in_=self.cvec_d), 16)
            for it in range(NIT):
                s = slice(it * P, (it + 1) * P)
                cd.inc(e.dma_start(out=self.A_sb[:, it * N:(it + 1) * N], in_=self.alog_d[s, :]), 16)
                cd.inc(e.dma_start(out=self.D_sb[:, it:it + 1], in_=self.dvec_d[s, :]), 16)
                cd.inc(e.dma_start(out=self.dtb_sb[:, it:it + 1], in_=self.dtb_d[s, :]), 16)
                cd.inc(e.dma_start(out=self.convb_sb[:, it:it + 1], in_=self.convb_d[s, :]), 16)
                cd.inc(e.dma_start(out=self.cw_sb[:, it * K:(it + 1) * K], in_=self.convw_d[s, :]), 16)
                cd.inc(e.dma_start(out=self.xpT_f[:, it * (R + 2 * N):(it + 1) * (R + 2 * N)],
                                   in_=self.xpT_d[s, :]), 16)
            cd.inc(e.dma_start(out=self.dtpT_f[:, :], in_=self.dtpT_d[0:P, :]), 16)
            for kt in range(NHT):
                s = slice(kt * P, (kt + 1) * P)
                cd.inc(e.dma_start(out=self.gw_f[:, kt * E:(kt + 1) * E], in_=self.gwT_d[s, :]), 16)
                cd.inc(e.dma_start(out=self.gw_f[:, NHT * E + kt:NHT * E + kt + 1],
                                   in_=self.sgwT_d[s, :]), 16)
            tok["init_in"] = cd.tok()
        self.on("sync", s_sync)

        def s_pre(e, m):
            t = cp.inc(e.memset(self.id_f32[:, :], 0.0))
            m.wait(t)
            cp.inc(e.affine_select(self.id_f32[:, :], self.id_f32[:, :], pattern=[[-1, P]],
                                   compare_op=ALU.not_equal, fill=1.0,
                                   base=0, channel_multiplier=1))
            tok["ident"] = cp.tok()
        self.on("gpsimd", s_pre)

        def s_v(e, m):
            cv.inc(e.memset(self.onescol_bf[:, :], 1.0))
            cv.inc(e.memset(self.onesrow_f[:, :], 1.0))
            tok["ones"] = cv.tok()
        self.on("vector", s_v)

    def ph_mod(self):
        c, tok = self.c, self.tok
        cd, cg, cq, cv, ca, ct = self.cd, self.cg, self.cq, self.cv, self.ca, self.ct
        B, H, NHT = c.B, c.H, c.NHT
        wmod_st = [self.v32(self.AR2, j * c.MODSL, c.MODSL) for j in range(2)]

        def s_a(e, m):
            m.wait(self.T("init_in"))
            tok["csil"] = ca.inc(e.activation(self.csml[0:B, :], self.csml[0:B, :], AF.Silu))
        self.on("scalar", s_a)

        def s_t(e, m):
            m.wait(self.T("csil")); m.wait(self.T("ident"))
            for kt in range(NHT):
                ct.inc(e.transpose(self.psA[0:P, kt * B:(kt + 1) * B],
                                   self.csml[0:B, kt * P:(kt + 1) * P], self.id_f32[0:B, 0:B]))
            tok["csT_ps"] = ct.tok()
        self.on("tensor", s_t)

        def s_v(e, m):
            m.wait(self.T("csT_ps"))
            tok["csT"] = cv.inc(e.tensor_copy(self.csT[:, :], self.psA[:, 0:NHT * B]))
        self.on("vector", s_v)

        nmj = (c.MODSL + 511) // 512
        for kt in range(NHT):
            def s_q(e, m, kt=kt):
                if kt >= 2:
                    m.wait(self.T(f"modmm{kt - 2}"))
                self.st[kt % 2].inc(e.dma_start(out=wmod_st[kt % 2][:, :],
                                                in_=self.wmod_d[kt * P:(kt + 1) * P, :]), 16)
                tok[f"wmodld{kt}"] = self.st[kt % 2].tok()
            self.on("scalar", s_q)

            def s_t2(e, m, kt=kt):
                if kt == 0:
                    m.wait(self.T("csT"))
                m.wait(self.T(f"wmodld{kt}"))
                for j in range(nmj):
                    j0, j1 = j * 512, min((j + 1) * 512, c.MODSL)
                    pj = [self.psB[0:B, 0:512], self.psB[0:B, 512:1024],
                          self.psC[0:B, 0:512], self.psC[0:B, 512:1024]][j]
                    ct.inc(e.matmul(pj[:, 0:j1 - j0],
                                    self.csT[:, kt * B:(kt + 1) * B],
                                    wmod_st[kt % 2][:, j0:j1],
                                    start=(kt == 0), stop=(kt == NHT - 1)))
                tok[f"modmm{kt}"] = ct.tok()
            self.on("tensor", s_t2)
        tok["mod_ps"] = None

        def s_v2(e, m):
            m.wait(self.T(f"modmm{NHT - 1}"))
            n1 = min(1024, c.MODSL)
            cv.inc(e.tensor_copy(self.modrow[0:B, 0:n1], self.psB[0:B, 0:n1]))
            if c.MODSL > 1024:
                cv.inc(e.tensor_copy(self.modrow[0:B, 1024:c.MODSL],
                                     self.psC[0:B, 0:c.MODSL - 1024]))
            tok["mod_sb"] = cv.tok()
        self.on("vector", s_v2)

        def s_s(e, m):
            m.wait(self.T("mod_sb"))
            self.cdM.inc(e.dma_start(out=self.modsl_d[:, :], in_=self.modrow[0:B, :]), 16)
            tok["modsl_w"] = self.cdM.tok()
        self.on("sync", s_s)

        def s_p(e, m):
            m.wait(self.T("modsl_w"))
            t = cg.inc(e.collective_compute(
                "AllGather", ALU.bypass, replica_groups=[list(range(NCORE))],
                ins=[self.modsl_d.ap().opt()], outs=[self.modg_d.ap().opt()]))
            m.wait(t)
            for r in range(NCORE):
                for b in range(B):
                    self.cgM.inc(e.dma_start(
                        out=self.modflat_d[b:b + 1, r * c.MODSL:(r + 1) * c.MODSL],
                        in_=self.modg_d[r * B + b:r * B + b + 1, :]), 16)
            tok["modflat"] = self.cgM.tok()
        self.on("gpsimd", s_p)

        def s_s2(e, m):
            m.wait(self.T("modflat"))
            for b in range(B):
                for kt in range(NHT):
                    self.cdM.inc(e.dma_start(
                        out=self.ca1_t[b][:, kt:kt + 1],
                        in_=self.modflat_d[b:b + 1, 1 * H + kt * P:1 * H + (kt + 1) * P]), 16)
                    self.cdM.inc(e.dma_start(
                        out=self.AR1[:, b * NHT + kt:b * NHT + kt + 1],
                        in_=self.modflat_d[b:b + 1, 0 * H + kt * P:0 * H + (kt + 1) * P]), 16)
            tok["modtiles"] = self.cdM.tok()
        self.on("sync", s_s2)

        def s_v3(e, m):
            m.wait(self.T("modtiles"))
            for b in range(B):
                cv.inc(e.tensor_scalar_add(self.ca1_t[b][:, :], self.ca1_t[b][:, :], 1.0))
                cv.inc(e.tensor_copy(self.sa_bf[b][:, :], self.AR1[:, b * NHT:(b + 1) * NHT]))
            tok["mod_done"] = cv.tok()
        self.on("vector", s_v3)

    def emit_clamp(self, b, off, scr, tok_in, tok_out):
        c, cv, ca, tok = self.c, self.cv, self.ca, self.tok
        H = c.H
        g = self.rows[0:1, off:off + H]
        S = self.rows[0:1, scr:scr + H]

        def s_sign(e, m):
            m.wait(tok_in() if callable(tok_in) else self.T(tok_in))
            tok[tok_out + "_sg"] = ca.inc(e.sign(S, g))
        self.on("scalar", s_sign)

        def s_v(e, m):
            m.wait(self.T(tok_out + "_sg"))
            t = cv.inc(e.tensor_mul(g, g, S))
            m.wait(t)
            t = cv.inc(e.tensor_scalar_max(g, g, 0.25))
            m.wait(t)
            tok[tok_out] = cv.inc(e.tensor_mul(g, g, S))
        self.on("vector", s_v)

    def ph_wprep(self):
        c, tok = self.c, self.tok
        cv, ca, cp = self.cv, self.ca, self.cp
        N, R, K, NIT, NHT, E = c.N, c.R, c.K, c.NIT, c.NHT, c.E

        def s_a(e, m):
            m.wait(self.T("init_in"))
            tok["A_exp"] = ca.inc(e.activation(self.A_sb[:, 0:NIT * N],
                                               self.A_sb[:, 0:NIT * N], AF.Exp))
        self.on("scalar", s_a)

        def s_v(e, m):
            m.wait(self.T("A_exp"))
            cv.inc(e.tensor_scalar_mul(self.A_sb[:, 0:NIT * N], self.A_sb[:, 0:NIT * N], -1.0))
            m.wait(self.T("init_in"))
            for it in range(NIT):
                cv.inc(e.tensor_copy(self.xpT_bf[it][:, :],
                                     self.xpT_f[:, it * (R + 2 * N):(it + 1) * (R + 2 * N)]))
            cv.inc(e.tensor_copy(self.dtpT_bf[:, :], self.dtpT_f[:, :]))
            cv.inc(e.tensor_copy(self.gwT_sb[:, :], self.gw_f[:, 0:NHT * E]))
            cv.inc(e.tensor_copy(self.sgwT_sb[:, :], self.gw_f[:, NHT * E:NHT * E + NHT]))
            m.wait(self.T("ones")); m.wait(self.T("ident"))
            for it in range(NIT):
                for k in range(K):
                    cv.inc(e.tensor_scalar(
                        self.diag[it][k][:, :], self.id_f32[:, :],
                        self.cw_sb[:, it * K + k:it * K + k + 1], None, ALU.mult))
            tok["diag"] = cv.tok()
        self.on("vector", s_v)

    def _A_views(self):
        c = self.c
        NIT, NHT, IC, ALC, NAT, N = c.NIT, c.NHT, c.IC, c.ALC, c.NAT, c.N
        v = {}
        v["Wb"] = self.vbf(self.AR1, 0, NHT * 2 * IC)
        v["ipT_st"] = [self.v32(self.AR2, j * 2 * IC, 2 * IC) for j in range(2)]
        v["x_tm"] = self.v32(self.AR2, 0, NAT * c.H)
        v["xb_fm"] = self.vbf(self.AR6, 0, NHT * ALC)
        v["u_f"] = self.v32(self.AR4, 0, NIT * ALC)
        v["g_f"] = self.v32(self.AR4, NIT * ALC, NIT * ALC)
        v["u_bf"] = self.vbf(self.AR4, 2 * NIT * ALC, NIT * ALC)
        o = 2 * NIT * ALC + NIT * ALC // 2
        v["r_bc_s"] = [self.v32(self.AR4, o + j * 2 * ALC, ALC) for j in range(2)]
        v["nmur_bc_s"] = [self.v32(self.AR4, o + j * 2 * ALC + ALC, ALC) for j in range(2)]
        v["sq_bf"] = [self.vbf(self.AR5, 2048 + j * ALC // 2, ALC) for j in range(2)]
        v["ssp_ts"] = [self.v32(self.AR5, j * ALC, ALC) for j in range(2)]
        v["ssp_bc"] = [self.AR5[0:2 * N, (2 + j) * ALC:(3 + j) * ALC] for j in range(2)]
        return v

    def ph_A_head(self, b):
        c, tok = self.c, self.tok
        cv, ca, ct = self.cv, self.ca, self.ct
        NIT, NHT, IC, K = c.NIT, c.NHT, c.IC, c.K
        psB = self.psB
        v = self._A_views()
        Wb, ipT_st = v["Wb"], v["ipT_st"]

        for kt in range(NHT):
            def s_ld(e, m, kt=kt):
                if kt == 0:
                    m.wait(self.T("mod_done"))
                    if b > 0:
                        m.wait(self.T(f"tr{b - 1}_{c.NALC - 1}"))
                        m.wait(self.T(f"ipmm{b - 1}_{c.NALC - 1}_{2 * NIT - 1}"))
                        if c.debug:
                            m.wait((self.dbgc, self.dbgc.count))
                if kt >= 2:
                    m.wait(self.T(f"wbs{b}_{kt - 2}"))
                self.st[kt % 2].inc(e.dma_start(out=ipT_st[kt % 2][:, :],
                                                in_=self.ipT_d[kt * P:(kt + 1) * P, :]), 16)
                tok[f"ipld{b}_{kt}"] = self.st[kt % 2].tok()
            self.on("sync", s_ld)

            def s_sc(e, m, kt=kt):
                m.wait(self.T(f"ipld{b}_{kt}"))
                tok[f"wbs{b}_{kt}"] = cv.inc(e.tensor_scalar(
                    Wb[:, kt * 2 * IC:(kt + 1) * 2 * IC], ipT_st[kt % 2][:, :],
                    self.ca1_t[b][:, kt:kt + 1], None, ALU.mult))
            self.on("vector", s_sc)
        tok[f"Wb{b}"] = self.cv.tok()

        def s_t(e, m):
            m.wait(self.T(f"Wb{b}"))
            if b > 0:
                m.wait(self.T(f"dtpw{b - 1}_{c.NALC - 1}_{NIT - 1}"))
            for mt in range(2 * NIT):
                for kt in range(NHT):
                    ct.inc(e.matmul(psB[0:P, mt:mt + 1],
                                    Wb[:, kt * 2 * IC + mt * P:kt * 2 * IC + (mt + 1) * P],
                                    self.sa_bf[b][:, kt:kt + 1],
                                    start=(kt == 0), stop=(kt == NHT - 1)))
            for mt in range(2 * NIT):
                for kt in range(NHT):
                    ct.inc(e.matmul(psB[0:P, 512 + mt:512 + mt + 1],
                                    Wb[:, kt * 2 * IC + mt * P:kt * 2 * IC + (mt + 1) * P],
                                    self.onescol_bf[:, :],
                                    start=(kt == 0), stop=(kt == NHT - 1)))
            tok[f"bias_mm{b}"] = ct.tok()
        self.on("tensor", s_t)

        def s_v2(e, m):
            m.wait(self.T(f"bias_mm{b}"))
            if b > 0:
                m.wait(self.T(f"ipev{b - 1}_{c.NALC - 1}_{2 * NIT - 1}"))
                m.wait(self.T(f"gs{b - 1}_{c.NALC - 1}_{NIT - 1}"))
                m.wait(self.T(f"uev{b - 1}_{c.NALC - 1}_{NIT - 1}"))
            cv.inc(e.tensor_copy(self.wsum[:, :], psB[:, 512:512 + 2 * NIT]))
            for it in range(NIT):
                t = cv.inc(e.reduce_sum(self.scol[:, 8 + it:9 + it],
                                        self.cw_sb[:, it * K:(it + 1) * K], axis=AXX))
                m.wait(t)
                cv.inc(e.scalar_tensor_tensor(
                    self.ubias[:, it:it + 1], psB[:, it:it + 1],
                    self.scol[:, 8 + it:9 + it], self.convb_sb[:, it:it + 1],
                    ALU.mult, ALU.add))
            tok[f"gb_ev{b}"] = cv.inc(e.tensor_copy(self.gbias[:, :], psB[:, NIT:2 * NIT]))
            for it in range(NIT):
                cv.inc(e.memset(self.hin[it][:, 0:3], 0.0))
            NN = c.N * NIT
            cv.inc(e.memset(self.carry[:, b * NN:(b + 1) * NN], 0.0))
            tok[f"Aprep{b}"] = cv.tok()
        self.on("vector", s_v2)

    def ph_A_chunk(self, b, lc):
        c, tok = self.c, self.tok
        cv, ca, ct = self.cv, self.ca, self.ct
        H, N, R, K = c.H, c.N, c.R, c.K
        NIT, NHT, IC, ALC, NALC, NAT = c.NIT, c.NHT, c.IC, c.ALC, c.NALC, c.NAT
        psA, psB, psC, psD = self.psA, self.psB, self.psC, self.psD
        v = self._A_views()
        Wb, x_tm, xb_fm = v["Wb"], v["x_tm"], v["xb_fm"]
        u_f, g_f, u_bf, sq_bf = v["u_f"], v["g_f"], v["u_bf"], v["sq_bf"]
        rows = self.rows
        RMU, RM2, RT, RR_, RNM = 0, ALC, 2 * ALC, 3 * ALC, 4 * ALC
        par = lc % 2
        lc0 = lc * ALC
        gch = b * NALC + lc
        gpar = gch % 2
        trp = psA[:, 0:512] if par == 0 else psA[:, 512:1024]
        r_bc = v["r_bc_s"][par]
        nmur_bc = v["nmur_bc_s"][par]
        ssp_ts = v["ssp_ts"][gpar]
        ssp_bc = v["ssp_bc"][gpar]

        def gwait(m, name_lc, last):
            if lc >= 2:
                m.wait(self.T(name_lc.format(b=b, lc=lc - 2)))
            elif b > 0:
                m.wait(self.T(name_lc.format(b=b - 1, lc=last + lc - 1)))

        def s_xld(e, m, lc=lc, lc0=lc0):
            if lc == 0:
                m.wait(self.T(f"Aprep{b}"))
            if lc >= 1:
                m.wait(self.T(f"tr{b}_{lc - 1}"))
            for tt in range(NAT):
                self.xs.inc(e.dma_start(out=x_tm[:, tt * H:(tt + 1) * H],
                                        in_=self.x_in[b, lc0 + tt * P:lc0 + (tt + 1) * P, :]), 16)
            tok[f"xld{b}_{lc}"] = self.xs.tok()
        self.on("sync", s_xld)

        for ht in range(NHT):
            def s_tr(e, m, lc=lc, ht=ht, trp=trp):
                if ht == 0:
                    m.wait(self.T(f"xld{b}_{lc}"))
                    if lc >= 2:
                        m.wait(self.T(f"xev{b}_{lc - 2}_{NHT - 1}"))
                    elif b > 0 and lc < 2:
                        m.wait(self.T(f"xev{b - 1}_{NALC - 2 + lc}_{NHT - 1}"))
                if ht >= 2:
                    m.wait(self.T(f"xev{b}_{lc}_{ht - 2}"))
                for tt in range(NAT):
                    ct.inc(e.transpose(trp[:, tt * P:(tt + 1) * P],
                                       x_tm[:, tt * H + ht * P:tt * H + (ht + 1) * P],
                                       self.id_f32[:, :]))
                tok[f"tr{b}_{lc}_{ht}"] = ct.tok()
            self.on("tensor", s_tr)

            def s_xev(e, m, lc=lc, ht=ht, trp=trp):
                m.wait(self.T(f"tr{b}_{lc}_{ht}"))
                if lc >= 1 and ht == 0:
                    m.wait(self.T(f"ipmm{b}_{lc - 1}_{2 * NIT - 1}"))
                elif b > 0 and lc == 0 and ht == 0:
                    m.wait(self.T(f"ipmm{b - 1}_{NALC - 1}_{2 * NIT - 1}"))
                tok[f"xev{b}_{lc}_{ht}"] = ca.inc(e.activation(
                    xb_fm[:, ht * ALC:(ht + 1) * ALC], trp[:, 0:ALC], AF.Copy))
                if ht >= 2:
                    m.wait(self.T(f"sqmm{b}_{lc}_{ht - 2}"))
                tok[f"sq{b}_{lc}_{ht}"] = ca.inc(e.activation(
                    sq_bf[ht % 2][:, :], xb_fm[:, ht * ALC:(ht + 1) * ALC], AF.Square))
            self.on("scalar", s_xev)

            def s_st(e, m, lc=lc, ht=ht):
                if ht == 0:
                    if lc >= 1:
                        m.wait(self.T(f"var{b}_{lc - 1}"))
                        m.wait(self.T(f"xpev{b}_{lc - 1}"))
                    elif b > 0:
                        m.wait(self.T(f"var{b - 1}_{NALC - 1}"))
                        m.wait(self.T(f"xpev{b - 1}_{NALC - 1}"))
                m.wait(self.T(f"xev{b}_{lc}_{ht}"))
                ct.inc(e.matmul(psC[0:1, 0:ALC], self.onescol_bf[:, :],
                                xb_fm[:, ht * ALC:(ht + 1) * ALC],
                                start=(ht == 0), stop=(ht == NHT - 1)))
                m.wait(self.T(f"sq{b}_{lc}_{ht}"))
                ct.inc(e.matmul(psC[0:1, 512:512 + ALC], self.onescol_bf[:, :],
                                sq_bf[ht % 2][:, :],
                                start=(ht == 0), stop=(ht == NHT - 1)))
                tok[f"sqmm{b}_{lc}_{ht}"] = ct.tok()
            self.on("tensor", s_st)
        tok[f"tr{b}_{lc}"] = self.ct.tok()
        tok[f"statmm{b}_{lc}"] = self.ct.tok()

        def s_row(e, m, lc=lc):
            m.wait(self.T(f"statmm{b}_{lc}"))
            if b > 0 and lc == 0:
                m.wait(self.T(f"rnbc{b - 1}_{NALC - 1}"))
            cv.inc(e.tensor_scalar_mul(rows[0:1, RMU:RMU + ALC], psC[0:1, 0:ALC], 1.0 / H))
            t = cv.inc(e.tensor_scalar_mul(rows[0:1, RM2:RM2 + ALC],
                                           psC[0:1, 512:512 + ALC], 1.0 / H))
            m.wait(t)
            t = cv.inc(e.tensor_mul(rows[0:1, RT:RT + ALC],
                                    rows[0:1, RMU:RMU + ALC], rows[0:1, RMU:RMU + ALC]))
            m.wait(t)
            t = cv.inc(e.tensor_sub(
                rows[0:1, RM2:RM2 + ALC], rows[0:1, RM2:RM2 + ALC], rows[0:1, RT:RT + ALC]))
            m.wait(t)
            tok[f"var{b}_{lc}"] = cv.inc(e.tensor_scalar_add(
                rows[0:1, RM2:RM2 + ALC], rows[0:1, RM2:RM2 + ALC], 1e-5))
        self.on("vector", s_row)

        def s_ln(e, m, lc=lc):
            m.wait(self.T(f"var{b}_{lc}"))
            tok[f"lnv{b}_{lc}"] = ca.inc(e.activation(
                rows[0:1, RT:RT + ALC], rows[0:1, RM2:RM2 + ALC], AF.Ln))
        self.on("scalar", s_ln)

        def s_row2(e, m, lc=lc):
            m.wait(self.T(f"lnv{b}_{lc}"))
            tok[f"nl{b}_{lc}"] = cv.inc(e.tensor_scalar_mul(
                rows[0:1, RT:RT + ALC], rows[0:1, RT:RT + ALC], -0.5))
        self.on("vector", s_row2)

        def s_ex(e, m, lc=lc):
            m.wait(self.T(f"nl{b}_{lc}"))
            tok[f"rstd{b}_{lc}"] = ca.inc(e.activation(
                rows[0:1, RR_:RR_ + ALC], rows[0:1, RT:RT + ALC], AF.Exp))
        self.on("scalar", s_ex)

        def s_row3(e, m, lc=lc):
            m.wait(self.T(f"rstd{b}_{lc}"))
            t = cv.inc(e.tensor_mul(rows[0:1, RNM:RNM + ALC],
                                    rows[0:1, RMU:RMU + ALC], rows[0:1, RR_:RR_ + ALC]))
            m.wait(t)
            tok[f"nmur{b}_{lc}"] = cv.inc(e.tensor_scalar_mul(
                rows[0:1, RNM:RNM + ALC], rows[0:1, RNM:RNM + ALC], -1.0))
        self.on("vector", s_row3)

        RD0 = 8192 // 2 + (lc % 2) * 2 * ALC
        def s_rbc(e, m, lc=lc, RD0=RD0):
            m.wait(self.T(f"rstd{b}_{lc}"))
            if lc >= 2:
                m.wait(self.T(f"ipev{b}_{lc - 2}_{2 * NIT - 1}"))
            elif b > 0:
                m.wait(self.T(f"ipev{b - 1}_{NALC - 2 + lc}_{2 * NIT - 1}"))
            self.rbc.inc(e.dma_start(out=self.rowsd[0:1, RD0:RD0 + ALC],
                                     in_=rows[0:1, RR_:RR_ + ALC]), 16)
            m.wait(self.T(f"nmur{b}_{lc}"))
            self.rbc.inc(e.dma_start(out=self.rowsd[0:1, RD0 + ALC:RD0 + 2 * ALC],
                                     in_=rows[0:1, RNM:RNM + ALC]), 16)
            t = (self.rbc, self.rbc.count)
            m.wait(t)
            self.rbc.inc(e.dma_start(out=r_bc[:, :], in_=self.bcast(
                r_bc[:, :], self.rowsd[0:1, RD0:RD0 + ALC])), 16)
            self.rbc.inc(e.dma_start(out=nmur_bc[:, :], in_=self.bcast(
                nmur_bc[:, :], self.rowsd[0:1, RD0 + ALC:RD0 + 2 * ALC])), 16)
            tok[f"rnbc{b}_{lc}"] = self.rbc.tok()
        self.on("sync", s_rbc)

        for mt in range(2 * NIT):
            pp = psB[:, 0:512] if mt % 2 == 0 else psB[:, 512:1024]

            def s_ip(e, m, lc=lc, mt=mt, pp=pp):
                if lc == 0 and mt == 0:
                    m.wait(self.T(f"bias_mm{b}"))
                if mt >= 2:
                    m.wait(self.T(f"ipev{b}_{lc}_{mt - 2}"))
                elif lc >= 1:
                    m.wait(self.T(f"ipev{b}_{lc - 1}_{2 * NIT - 2 + mt}"))
                for kt in range(NHT):
                    ct.inc(e.matmul(pp[:, 0:ALC],
                                    Wb[:, kt * 2 * IC + mt * P:kt * 2 * IC + (mt + 1) * P],
                                    xb_fm[:, kt * ALC:(kt + 1) * ALC],
                                    start=(kt == 0), stop=(kt == NHT - 1)))
                tok[f"ipmm{b}_{lc}_{mt}"] = ct.tok()
            self.on("tensor", s_ip)

            def s_ipe(e, m, lc=lc, mt=mt, pp=pp, r_bc=r_bc, nmur_bc=nmur_bc):
                m.wait(self.T(f"ipmm{b}_{lc}_{mt}"))
                m.wait(self.T(f"rnbc{b}_{lc}"))
                if mt >= 1:
                    m.wait(self.T(f"ipev{b}_{lc}_{mt - 1}"))
                elif lc >= 1:
                    m.wait(self.T(f"ipev{b}_{lc - 1}_{2 * NIT - 1}"))
                elif b > 0:
                    m.wait(self.T(f"ipev{b - 1}_{NALC - 1}_{2 * NIT - 1}"))
                t = cv.inc(e.tensor_mul(self.tev[:, 0:ALC], pp[:, 0:ALC], r_bc[:, :]))
                m.wait(t)
                if mt < NIT:
                    it = mt
                    if lc >= 1:
                        m.wait(self.T(f"usil{b}_{lc - 1}_{it}"))
                        m.wait(self.T(f"halo{b}_{lc - 1}_{it}"))
                    elif b > 0:
                        m.wait(self.T(f"usil{b - 1}_{NALC - 1}_{it}"))
                        m.wait(self.T(f"halo{b - 1}_{NALC - 1}_{it}"))
                    tok[f"ipev{b}_{lc}_{mt}"] = cv.inc(e.scalar_tensor_tensor(
                        self.hin[it][:, 3:3 + ALC], nmur_bc[:, :],
                        self.wsum[:, mt:mt + 1], self.tev[:, 0:ALC],
                        ALU.mult, ALU.add))
                else:
                    it = mt - NIT
                    if lc >= 1:
                        m.wait(self.T(f"gs{b}_{lc - 1}_{it}"))
                        m.wait(self.T(f"gspill{b}_{lc - 1}"))
                    elif b > 0:
                        m.wait(self.T(f"gs{b - 1}_{NALC - 1}_{it}"))
                        m.wait(self.T(f"gspill{b - 1}_{NALC - 1}"))
                    tok[f"ipev{b}_{lc}_{mt}"] = cv.inc(e.scalar_tensor_tensor(
                        g_f[:, it * ALC:(it + 1) * ALC], nmur_bc[:, :],
                        self.wsum[:, mt:mt + 1], self.tev[:, 0:ALC],
                        ALU.mult, ALU.add))
                    tok[f"gpre{b}_{lc}_{it}"] = tok[f"ipev{b}_{lc}_{mt}"]
            self.on("vector", s_ipe)

            if mt >= NIT:
                it = mt - NIT

                def s_gs(e, m, lc=lc, it=it):
                    if lc == 0 and it == 0:
                        m.wait(self.T(f"gb_ev{b}"))
                    m.wait(self.T(f"gpre{b}_{lc}_{it}"))
                    tok[f"gs{b}_{lc}_{it}"] = ca.inc(e.activation(
                        g_f[:, it * ALC:(it + 1) * ALC],
                        g_f[:, it * ALC:(it + 1) * ALC],
                        AF.Silu, bias=self.gbias[:, it:it + 1]))
                self.on("scalar", s_gs)

        for it in range(NIT):
            def s_cv(e, m, lc=lc, it=it):
                if lc == 0 and it == 0:
                    m.wait(self.T("diag"))
                    if b > 0:
                        m.wait(self.T(f"xpev{b - 1}_{NALC - 1}"))
                m.wait(self.T(f"ipev{b}_{lc}_{it}"))
                if it >= 1:
                    m.wait(self.T(f"uev{b}_{lc}_{it - 1}"))
                elif lc >= 1:
                    m.wait(self.T(f"uev{b}_{lc - 1}_{NIT - 1}"))
                elif b > 0:
                    m.wait(self.T(f"uev{b - 1}_{NALC - 1}_{NIT - 1}"))
                for k in range(K):
                    ct.inc(e.matmul(psD[:, 0:ALC], self.diag[it][k][:, :],
                                    self.hin[it][:, k:k + ALC],
                                    start=(k == 0), stop=(k == K - 1)))
                tok[f"conv{b}_{lc}_{it}"] = ct.tok()
            self.on("tensor", s_cv)

            def s_us(e, m, lc=lc, it=it):
                m.wait(self.T(f"conv{b}_{lc}_{it}"))
                if it == 0:
                    if lc >= 1:
                        m.wait(self.T(f"uspill{b}_{lc - 1}"))
                    elif b > 0:
                        m.wait(self.T(f"uspill{b - 1}_{NALC - 1}"))
                t = ca.inc(e.activation(u_f[:, it * ALC:(it + 1) * ALC],
                                        psD[:, 0:ALC], AF.Silu,
                                        bias=self.ubias[:, it:it + 1]))
                tok[f"uev{b}_{lc}_{it}"] = t
                tok[f"usil{b}_{lc}_{it}"] = t
            self.on("scalar", s_us)

            def s_ub(e, m, lc=lc, it=it):
                m.wait(self.T(f"uev{b}_{lc}_{it}"))
                tok[f"ubf{b}_{lc}_{it}"] = cv.inc(e.tensor_copy(
                    u_bf[:, it * ALC:(it + 1) * ALC], u_f[:, it * ALC:(it + 1) * ALC]))
                tok[f"halo{b}_{lc}_{it}"] = cv.inc(e.tensor_copy(
                    self.hin[it][:, 0:3], self.hin[it][:, ALC:ALC + 3]))
            self.on("vector", s_ub)

        def s_xp(e, m, lc=lc):
            if lc >= 1:
                m.wait(self.T(f"xpev{b}_{lc - 1}"))
            elif b > 0:
                m.wait(self.T(f"xpev{b - 1}_{NALC - 1}"))
            m.wait(self.T(f"var{b}_{lc}"))
            for it in range(NIT):
                m.wait(self.T(f"ubf{b}_{lc}_{it}"))
                ct.inc(e.matmul(psD[:, 512:512 + ALC], self.xpT_bf[it][:, 0:R],
                                u_bf[:, it * ALC:(it + 1) * ALC],
                                start=(it == 0), stop=(it == NIT - 1)))
            for it in range(NIT):
                ct.inc(e.matmul(psC[0:2 * N, 512:512 + ALC],
                                self.xpT_bf[it][:, R:R + 2 * N],
                                u_bf[:, it * ALC:(it + 1) * ALC],
                                start=(it == 0), stop=(it == NIT - 1)))
            tok[f"xpmm{b}_{lc}"] = ct.tok()
        self.on("tensor", s_xp)

        def s_xpe(e, m, lc=lc, gch=gch):
            m.wait(self.T(f"xpmm{b}_{lc}"))
            if gch >= 2:
                pb, plc = divmod(gch - 2, NALC)
                m.wait(self.T(f"ssw{pb}_{plc}"))
            cv.inc(e.tensor_copy(ssp_ts[:, :], psD[:, 512:512 + ALC]))
            tok[f"xpev{b}_{lc}"] = cv.inc(e.tensor_copy(
                ssp_bc[:, :], psC[0:2 * N, 512:512 + ALC]))
        self.on("vector", s_xpe)

        def s_ssw(e, m, lc=lc, lc0=lc0, gpar=gpar):
            m.wait(self.T(f"xpev{b}_{lc}"))
            self.ssmW[gpar].inc(e.dma_start(
                out=self.ssmp_d[b][0:R, lc0:lc0 + ALC], in_=ssp_ts[:, :]), 16)
            self.ssmW[gpar].inc(e.dma_start(
                out=self.ssmp_d[b][R:R + 2 * N, lc0:lc0 + ALC], in_=ssp_bc[:, :]), 16)
            tok[f"ssw{b}_{lc}"] = self.ssmW[gpar].tok()
        self.on("sync", s_ssw)

        def s_sp(e, m, lc=lc, lc0=lc0):
            for it in range(NIT):
                m.wait(self.T(f"uev{b}_{lc}_{it}"))
                self.spU.inc(e.dma_start(
                    out=self.usp_d[b][it * P:(it + 1) * P, lc0:lc0 + ALC],
                    in_=u_f[:, it * ALC:(it + 1) * ALC]), 16)
                if c.debug:
                    self.spU.inc(e.dma_start(
                        out=self.dbg["u"][b, it * P:(it + 1) * P, lc0:lc0 + ALC],
                        in_=u_f[:, it * ALC:(it + 1) * ALC]), 16)
            tok[f"uspill{b}_{lc}"] = self.spU.tok()
            for it in range(NIT):
                m.wait(self.T(f"gs{b}_{lc}_{it}"))
                self.spG.inc(e.dma_start(
                    out=self.gsp_d[b][it * P:(it + 1) * P, lc0:lc0 + ALC],
                    in_=g_f[:, it * ALC:(it + 1) * ALC]), 16)
                if c.debug:
                    self.spG.inc(e.dma_start(
                        out=self.dbg["gate"][b, it * P:(it + 1) * P, lc0:lc0 + ALC],
                        in_=g_f[:, it * ALC:(it + 1) * ALC]), 16)
            tok[f"gspill{b}_{lc}"] = self.spG.tok()
        self.on("sync", s_sp)

    def ph_AR(self, b):
        c, tok = self.c, self.tok
        cg = self.cg

        def s_p(e, m):
            m.wait(self.T(f"ssw{b}_{c.NALC - 1}"))
            if c.NALC >= 2:
                m.wait(self.T(f"ssw{b}_{c.NALC - 2}"))
            tok[f"AR{b}"] = cg.inc(e.collective_compute(
                "AllReduce", ALU.add, replica_groups=[list(range(NCORE))],
                ins=[self.ssmp_d[b].ap().opt()], outs=[self.ssm_d[b].ap().opt()]))
        self.on("gpsimd", s_p)

        if c.debug:
            def s_s2(e, m):
                m.wait(self.T(f"AR{b}"))
                self.dbgc.inc(e.dma_start(out=self.dbg["ssm"][b, :, :], in_=self.ssm_d[b][:, :]), 16)
            self.on("sync", s_s2)

    def ph_dtpre(self, b):
        c, tok = self.c, self.tok
        ct = self.ct
        NIT, ALC, NALC, R = c.NIT, c.ALC, c.NALC, c.R
        psB = self.psB
        tsl = [self.vbf(self.AR2, 0, 0), self.vbf(self.AR2, 0, 0)]
        tsl = [self.tssl[:, j * ALC:(j + 1) * ALC] for j in range(2)]

        for lc in range(NALC):
            lc0 = lc * ALC
            gch = b * NALC + lc

            def s_tl(e, m, lc=lc, lc0=lc0, gch=gch):
                m.wait(self.T(f"AR{b}"))
                if gch >= 2:
                    pb, plc = divmod(gch - 2, NALC)
                    m.wait(self.T(f"dtpmm{pb}_{plc}_{NIT - 1}"))
                self.cdl[gch % 2].inc(e.dma_start(out=tsl[gch % 2][:, :],
                                                  in_=self.ssm_d[b][0:R, lc0:lc0 + ALC]), 16)
                tok[f"tsl{b}_{lc}"] = self.cdl[gch % 2].tok()
            self.on("gpsimd", s_tl)

            for it in range(NIT):
                jj = (b * NALC + lc) * NIT + it
                par = jj % 2

                def s_tm(e, m, lc=lc, it=it, jj=jj, par=par, gch=gch):
                    if lc == 0 and it == 0:
                        m.wait(self.T(f"ipev{b}_{NALC - 1}_{2 * NIT - 1}"))
                    m.wait(self.T(f"tsl{b}_{lc}"))
                    if jj >= 2:
                        pj = jj - 2
                        pb2, r2 = divmod(pj, NALC * NIT)
                        plc2, pit2 = divmod(r2, NIT)
                        m.wait(self.T(f"dtpe{pb2}_{plc2}_{pit2}"))
                    ct.inc(e.matmul(psB[:, par * 512:par * 512 + ALC],
                                    self.dtpT_bf[:, it * P:(it + 1) * P],
                                    tsl[gch % 2][:, :], start=True, stop=True))
                    tok[f"dtpmm{b}_{lc}_{it}"] = ct.tok()
                self.on("tensor", s_tm)

                def s_te(e, m, lc=lc, it=it, jj=jj, par=par):
                    m.wait(self.T(f"dtpmm{b}_{lc}_{it}"))
                    if jj >= 2:
                        pj = jj - 2
                        pb2, r2 = divmod(pj, NALC * NIT)
                        plc2, pit2 = divmod(r2, NIT)
                        m.wait(self.T(f"dtpw{pb2}_{plc2}_{pit2}"))
                    tok[f"dtpe{b}_{lc}_{it}"] = self.cv.inc(e.tensor_copy(
                        self.dtps[:, par * 512:par * 512 + ALC],
                        psB[:, par * 512:par * 512 + ALC]))
                self.on("vector", s_te)

                def s_tw(e, m, lc=lc, lc0=lc0, it=it, par=par):
                    m.wait(self.T(f"dtpe{b}_{lc}_{it}"))
                    self.cdw[par].inc(e.dma_start(
                        out=self.dtpre_d[b][it * P:(it + 1) * P, lc0:lc0 + ALC],
                        in_=self.dtps[:, par * 512:par * 512 + ALC]), 16)
                    tok[f"dtpw{b}_{lc}_{it}"] = self.cdw[par].tok()
                self.on("sync", s_tw)

    def _B_views(self):
        c = self.c
        NIT, SLC, N = c.NIT, c.SLC, c.N
        v = {}
        v["u_ch"] = self.v32(self.AR3, 0, NIT * SLC)
        v["g_ch"] = self.v32(self.AR3, NIT * SLC, NIT * SLC)
        v["dt_ch"] = self.v32(self.AR3, 2 * NIT * SLC, NIT * SLC)
        v["dtu_ch"] = self.v32(self.AR3, 3 * NIT * SLC, NIT * SLC)
        v["y_ch"] = self.v32(self.AR3, 4 * NIT * SLC, NIT * SLC)
        base = 5 * NIT * SLC
        v["BC"] = [self.v32(self.AR3, base + j * 2 * SLC, 2 * SLC) for j in range(3)]
        v["Bbc"] = [v["BC"][j][:, 0:SLC] for j in range(3)]
        v["Cbc"] = [v["BC"][j][:, SLC:2 * SLC] for j in range(3)]
        v["h_t"] = self.v32(self.AR3, base + 6 * SLC, SLC)
        v["dA_t"] = [self.v32(self.AR3, base + (7 + j) * SLC, SLC) for j in range(3)]
        v["dBx_t"] = [self.v32(self.AR3, base + (10 + j) * SLC, SLC) for j in range(2)]
        v["yg_ch"] = self.vbf(self.AR5, 2560, NIT * SLC)
        return v

    def ph_B_chunk(self, b, slc):
        c, tok = self.c, self.tok
        cv, ca, ct, cp = self.cv, self.ca, self.ct, self.cp
        N, R, NIT, IC, SLC, NSLC, TC = c.N, c.R, c.NIT, c.IC, c.SLC, c.NSLC, c.TC
        v = self._B_views()
        u_ch, g_ch, dt_ch, dtu_ch, y_ch = v["u_ch"], v["g_ch"], v["dt_ch"], v["dtu_ch"], v["y_ch"]
        BC, Bbc, Cbc, h_t = v["BC"], v["Bbc"], v["Cbc"], v["h_t"]
        dA_t, dBx_t, yg_ch = v["dA_t"], v["dBx_t"], v["yg_ch"]
        CB = b * N * NIT
        csc = self.csc
        if slc == 0:
            self._cscbase = getattr(self, "_cscbase", {})
            self._cscbase[b] = csc.count
        cscbase = self._cscbase[b]

        def scw(slc_, n, it):
            return (csc, cscbase + slc_ * N * NIT + n * NIT + it + 1)

        l0 = slc * SLC

        def s_ld(e, m, slc=slc, l0=l0):
            if slc == 0:
                m.wait(self.T(f"gspill{b}_{c.NALC - 1}"))
                m.wait(self.T(f"uspill{b}_{c.NALC - 1}"))
            if slc >= 1:
                m.wait(self.T(f"ygv{b}_{slc - 1}"))
            for it in range(NIT):
                self.ug.inc(e.dma_start(out=u_ch[:, it * SLC:(it + 1) * SLC],
                                        in_=self.usp_d[b][it * P:(it + 1) * P, l0:l0 + SLC]), 16)
                self.ug.inc(e.dma_start(out=g_ch[:, it * SLC:(it + 1) * SLC],
                                        in_=self.gsp_d[b][it * P:(it + 1) * P, l0:l0 + SLC]), 16)
            tok[f"ugld{b}_{slc}"] = self.ug.tok()
        self.on("sync", s_ld)

        for it in range(NIT):
            ji = slc * NIT + it
            ACR = c.ALC // SLC

            def s_dl(e, m, slc=slc, l0=l0, it=it, ji=ji, ACR=ACR):
                m.wait(self.T(f"dtpw{b}_{slc // ACR}_{it}"))
                if slc >= 1:
                    m.wait(self.T(f"dA{b}_{slc - 1}_{N - 1}_{it}"))
                    m.wait(self.T(f"dtu{b}_{slc - 1}"))
                    if c.debug:
                        m.wait(self.T(f"dtdump{b}_{slc - 1}"))
                elif b > 0:
                    m.wait(self.T(f"dA{b - 1}_{c.NSLC - 1}_{N - 1}_{it}"))
                    m.wait(self.T(f"dtu{b - 1}_{c.NSLC - 1}"))
                    if c.debug:
                        m.wait(self.T(f"dtdump{b - 1}_{c.NSLC - 1}"))
                self.cdl2[ji % 2].inc(e.dma_start(
                    out=dt_ch[:, it * SLC:(it + 1) * SLC],
                    in_=self.dtpre_d[b][it * P:(it + 1) * P, l0:l0 + SLC]), 16)
                tok[f"dtld{b}_{slc}_{it}"] = self.cdl2[ji % 2].tok()
            self.on("sync", s_dl)

            def s_de(e, m, slc=slc, it=it, ji=ji):
                m.wait(self.T(f"dtld{b}_{slc}_{it}"))
                if slc >= 1 and it == 0:
                    m.wait(self.T(f"dtln{b}_{slc - 1}"))
                    m.wait(scw(slc - 1, N - 1, NIT - 1))
                tok[f"dte{b}_{slc}_{it}"] = cv.inc(e.tensor_scalar(
                    dt_ch[:, it * SLC:(it + 1) * SLC],
                    dt_ch[:, it * SLC:(it + 1) * SLC],
                    self.dtb_sb[:, it:it + 1], 85.0, ALU.add, ALU.min))
            self.on("vector", s_de)
        tok[f"dtev{b}_{slc}"] = self.cv.tok()

        def s_sp(e, m, slc=slc):
            m.wait(self.T(f"dtev{b}_{slc}"))
            if slc >= 1:
                m.wait(scw(slc - 1, N - 1, NIT - 1))
            t = ca.inc(e.activation(dtu_ch[:, :], dt_ch[:, :], AF.Exp))
            m.wait(t)
            tok[f"dtln{b}_{slc}"] = ca.inc(e.activation(dt_ch[:, :], dtu_ch[:, :],
                                                        AF.Ln, bias=1.0))
        self.on("scalar", s_sp)

        def s_du(e, m, slc=slc):
            m.wait(self.T(f"dtln{b}_{slc}")); m.wait(self.T(f"ugld{b}_{slc}"))
            tok[f"dtu{b}_{slc}"] = cv.inc(e.tensor_mul(dtu_ch[:, :], dt_ch[:, :], u_ch[:, :]))
        self.on("vector", s_du)

        if c.debug:
            def s_dd(e, m, slc=slc, l0=l0):
                m.wait(self.T(f"dtln{b}_{slc}"))
                for it in range(NIT):
                    self.dbgD.inc(e.dma_start(
                        out=self.dbg["dt"][b, it * P:(it + 1) * P, l0:l0 + SLC],
                        in_=dt_ch[:, it * SLC:(it + 1) * SLC]), 16)
                tok[f"dtdump{b}_{slc}"] = self.dbgD.tok()
            self.on("sync", s_dd)

        def s_dbx(e, m, slc=slc, l0=l0):
            if slc == 0:
                m.wait(self.T(f"AR{b}"))

            def issue_bc(n2):
                g = slc * N + n2
                sl = g % 3
                if g >= 3:
                    ps_, pn_ = divmod(g - 3, N)
                    m.wait(scw(ps_, pn_, NIT - 1))
                elif b > 0:
                    m.wait((csc, cscbase))
                bc2 = BC[sl].rearrange("p (r l) -> p r l", r=2)
                src2 = self.ssm_d[b][R + 2 * n2:R + 2 * n2 + 2,
                                     l0:l0 + SLC].rearrange("(o r) l -> o r l", o=1)
                self.br[sl].inc(e.dma_start(
                    out=bc2[:, :, :],
                    in_=self.bcast(bc2[:, :, :], src2)), 16)
                tok[f"bcd{b}_{slc}_{n2}"] = (self.br[sl], self.br[sl].count)

            issue_bc(0)
            issue_bc(1)
            m.wait(self.T(f"dtu{b}_{slc}"))
            for n in range(N):
                if n + 2 < N:
                    issue_bc(n + 2)
                m.wait(self.T(f"bcd{b}_{slc}_{n}"))
                for it in range(NIT):
                    j = n * NIT + it
                    if j >= 2:
                        m.wait(scw(slc, (j - 2) // NIT, (j - 2) % NIT))
                    elif slc >= 1:
                        m.wait(scw(slc - 1, (N * NIT - 2 + j) // NIT, (N * NIT - 2 + j) % NIT))
                    tok[f"dbx{b}_{slc}_{n}_{it}"] = cp.inc(e.tensor_mul(
                        dBx_t[j % 2][:, :], dtu_ch[:, it * SLC:(it + 1) * SLC],
                        Bbc[(slc * N + n) % 3][:, :]))
        self.on("gpsimd", s_dbx)

        def s_dA(e, m, slc=slc):
            m.wait(self.T(f"dtln{b}_{slc}"))
            for n in range(N):
                for it in range(NIT):
                    j = n * NIT + it
                    u = slc * N * NIT + j
                    if u >= 3:
                        ps_, pj_ = divmod(u - 3, N * NIT)
                        m.wait(scw(ps_, pj_ // NIT, pj_ % NIT))
                    elif b > 0:
                        m.wait((csc, cscbase))
                    tok[f"dA{b}_{slc}_{n}_{it}"] = ca.inc(e.activation(
                        dA_t[u % 3][:, :], dt_ch[:, it * SLC:(it + 1) * SLC], AF.Exp,
                        scale=self.A_sb[:, it * N + n:it * N + n + 1]))
        self.on("scalar", s_dA)

        def s_sc(e, m, slc=slc):
            if c.debug and slc >= 1:
                m.wait(self.T(f"ydump{b}_{slc - 1}"))
            elif c.debug and b > 0 and slc == 0:
                m.wait(self.T(f"ydump{b - 1}_{c.NSLC - 1}"))
            for n in range(N):
                for it in range(NIT):
                    j = n * NIT + it
                    u = slc * N * NIT + j
                    m.wait(self.T(f"dbx{b}_{slc}_{n}_{it}"))
                    m.wait(self.T(f"dA{b}_{slc}_{n}_{it}"))
                    if j >= 1:
                        m.wait(self.T(f"scanh{b}_{slc}_{(j - 1) // NIT}_{(j - 1) % NIT}"))
                    elif slc >= 1:
                        m.wait(self.T(f"scanh{b}_{slc - 1}_{N - 1}_{NIT - 1}"))
                    t = cv.inc(e.tensor_tensor_scan(
                        h_t[:, :], dA_t[u % 3][:, :], dBx_t[j % 2][:, :],
                        self.carry[:, CB + n * NIT + it:CB + n * NIT + it + 1],
                        ALU.mult, ALU.add))
                    m.wait(t)
                    cv.inc(e.tensor_copy(self.carry[:, CB + n * NIT + it:CB + n * NIT + it + 1],
                                         h_t[:, SLC - 1:SLC]))
                    if n == 0:
                        ins = e.tensor_mul(y_ch[:, it * SLC:(it + 1) * SLC],
                                           h_t[:, :], Cbc[(slc * N + n) % 3][:, :])
                    else:
                        t2 = cv.inc(e.tensor_mul(dBx_t[j % 2][:, :], h_t[:, :],
                                                 Cbc[(slc * N + n) % 3][:, :]))
                        m.wait(t2)
                        ins = e.tensor_add(y_ch[:, it * SLC:(it + 1) * SLC],
                                           y_ch[:, it * SLC:(it + 1) * SLC],
                                           dBx_t[j % 2][:, :])
                    tok[f"scanh{b}_{slc}_{n}_{it}"] = csc.inc(ins)
            m.wait(self.T(f"scanh{b}_{slc}_{N - 1}_{NIT - 1}"))
            for it in range(NIT):
                t = cv.inc(e.scalar_tensor_tensor(
                    y_ch[:, it * SLC:(it + 1) * SLC], u_ch[:, it * SLC:(it + 1) * SLC],
                    self.D_sb[:, it:it + 1], y_ch[:, it * SLC:(it + 1) * SLC],
                    ALU.mult, ALU.add))
                m.wait(t)
                if slc >= 1 and it == 0:
                    m.wait(self.T(f"ygw{b}_{slc - 1}"))
                cv.inc(e.tensor_mul(yg_ch[:, it * SLC:(it + 1) * SLC],
                                    y_ch[:, it * SLC:(it + 1) * SLC],
                                    g_ch[:, it * SLC:(it + 1) * SLC]))
            tok[f"ygv{b}_{slc}"] = cv.tok()
        self.on("vector", s_sc)

        def s_yw(e, m, slc=slc, l0=l0):
            m.wait(self.T(f"ygv{b}_{slc}"))
            for q in range(max(1, SLC // TC)):
                rank = (l0 + q * TC) // TC
                q0 = (l0 + q * TC) % TC
                w = min(SLC, TC)
                for it in range(NIT):
                    self.ygW.inc(e.dma_start(
                        out=self.a2ain_d[b][rank * IC + it * P:rank * IC + (it + 1) * P,
                                            q0:q0 + w],
                        in_=yg_ch[:, it * SLC + q * w:it * SLC + q * w + w]), 16)
            if c.debug:
                for it in range(NIT):
                    self.dbgY.inc(e.dma_start(
                        out=self.dbg["y"][b, it * P:(it + 1) * P, l0:l0 + SLC],
                        in_=y_ch[:, it * SLC:(it + 1) * SLC]), 16)
                tok[f"ydump{b}_{slc}"] = self.dbgY.tok()
            tok[f"ygw{b}_{slc}"] = self.ygW.tok()
        self.on("sync", s_yw)

    def ph_B_tail(self, b):
        c, tok = self.c, self.tok
        cg = self.cg
        tok[f"yg_w{b}"] = self.ygW.tok()
        tok[f"dbgB{b}"] = self.dbgc.tok()
        tok[f"dbgBD{b}"] = self.dbgD.tok()
        tok[f"dbgBY{b}"] = self.dbgY.tok()

        def s_a2a(e, m):
            m.wait(self.T(f"yg_w{b}"))
            with allow_ici_all_to_all():
                tok[f"A2A{b}"] = cg.inc(e.collective_compute(
                    "AllToAll", ALU.bypass, replica_groups=[list(range(NCORE))],
                    ins=[self.a2ain_d[b].ap().opt()], outs=[self.a2aout_d[b].ap().opt()]))
        self.on("gpsimd", s_a2a)

    def _C_views(self):
        c = self.c
        H, TC, NHT, NTT = c.H, c.TC, c.NHT, c.NTT
        NKI = c.I // P
        v = {}
        v["y_all"] = self.vbf(self.AR1, 0, NKI * TC)
        v["x2_tm"] = self.v32(self.AR2, 0, NTT * H)
        v["x_q"] = [self.v32(self.AR4, j * 1024, 1024) for j in range(2)]
        v["gabc"] = self.v32(self.AR4, H, H)
        v["cbbc"] = self.v32(self.AR4, 2 * H, H)
        v["sbbc"] = self.v32(self.AR4, 3 * H, H)
        v["xsq"] = self.v32(self.AR1, 0, H)
        v["h2_tm"] = self.v32(self.AR4, 0, H)
        v["h2b"] = self.vbf(self.AR6, 0, NHT * TC)
        return v

    def ph_C_p1(self, b):
        c, tok = self.c, self.tok
        cd, cq, cv, ca, ct = self.cd, self.cq, self.cv, self.ca, self.ct
        H, N, L, IC, TC, NHT, NTT, E = c.H, c.N, c.L, c.IC, c.TC, c.NHT, c.NTT, c.E
        NALC, NIT = c.NALC, c.NIT
        NKI = c.I // P
        psA, psB, psC, psD = self.psA, self.psB, self.psC, self.psD
        PSL = self.PSL
        v = self._C_views()
        y_all, x2_tm, x_q = v["y_all"], v["x2_tm"], v["x_q"]
        gabc, cbbc, sbbc = v["gabc"], v["cbbc"], v["sbbc"]
        rows = self.rows
        R0, RS = 0, 2048

        def s_yld(e, m):
            m.wait(self.T(f"A2A{b}"))
            if b == 0 and c.B > 1:
                m.wait(self.T(f"ipmm{1}_{NALC - 1}_{2 * NIT - 1}"))
            for kt in range(NKI):
                self.ya.inc(e.dma_start(out=y_all[:, kt * TC:(kt + 1) * TC],
                                        in_=self.a2aout_d[b][kt * P:(kt + 1) * P, :]), 16)
            tok[f"yall{b}"] = self.ya.tok()
        self.on("sync", s_yld)

        def s_r1(e, m):
            m.wait(self.T("modflat"))
            m.wait(self.T(f"ygw{b}_{c.NSLC - 1}"))
            if b == 0 and c.B > 1:
                m.wait(self.T(f"rnbc{1}_{NALC - 1}"))
            if b > 0:
                m.wait(self.T(f"cbbc{b - 1}"))
            cd.inc(e.dma_start(out=rows[0:1, R0:R0 + H],
                               in_=self.modflat_d[b:b + 1, 2 * H:3 * H]), 16)
            tok[f"garow{b}"] = cd.tok()
        self.on("sync", s_r1)

        self.emit_clamp(b, R0, RS, f"garow{b}", f"gacl{b}")

        def s_gab(e, m):
            m.wait(self.T(f"gacl{b}"))
            if b == 0 and c.B > 1:
                m.wait(self.T(f"gspill{1}_{NALC - 1}"))
                m.wait(self.T(f"ipev{1}_{NALC - 1}_{2 * NIT - 1}"))
            elif b > 0:
                m.wait(self.T(f"x2{b - 1}"))
                m.wait(self.T(f"h2tm{b - 1}_{c.NTT - 1}"))
            m.wait((self.cbc, self.cbc.count))
            self.cbc.inc(e.dma_start(out=self.rowsd[0:1, 0:H], in_=rows[0:1, R0:R0 + H]), 16)
            t = (self.cbc, self.cbc.count)
            m.wait(t)
            self.cbc.inc(e.dma_start(out=gabc[:, :], in_=self.bcast(
                gabc[:, :], self.rowsd[0:1, 0:H])), 16)
            self.cbc.inc(e.dma_start(out=sbbc[:, :], in_=self.bcast(
                sbbc[:, :], self.modflat_d[b:b + 1, 3 * H:4 * H])), 16)
            tok[f"gabc{b}"] = self.cbc.tok()
        self.on("sync", s_gab)

        def s_cbr(e, m):
            m.wait(self.T(f"gabc{b}"))
            cd.inc(e.dma_start(out=rows[0:1, R0:R0 + H],
                               in_=self.modflat_d[b:b + 1, 4 * H:5 * H]), 16)
            tok[f"cbrow{b}"] = cd.tok()
        self.on("sync", s_cbr)

        def s_cb1(e, m):
            m.wait(self.T(f"cbrow{b}"))
            tok[f"cb1{b}"] = cv.inc(e.tensor_scalar_add(
                rows[0:1, R0:R0 + H], rows[0:1, R0:R0 + H], 1.0))
        self.on("vector", s_cb1)

        def s_cbb(e, m):
            m.wait(self.T(f"cb1{b}"))
            if b == 0 and c.B > 1:
                m.wait(self.T(f"xpev{1}_{NALC - 1}"))
            elif b > 0:
                m.wait(self.T(f"h2tm{b - 1}_{c.NTT - 1}"))
            m.wait((self.cbc, self.cbc.count))
            self.cbc.inc(e.dma_start(out=self.rowsd[0:1, H:2 * H], in_=rows[0:1, R0:R0 + H]), 16)
            t = (self.cbc, self.cbc.count)
            m.wait(t)
            self.cbc.inc(e.dma_start(out=cbbc[:, :], in_=self.bcast(
                cbbc[:, :], self.rowsd[0:1, H:2 * H])), 16)
            tok[f"cbbc{b}"] = self.cbc.tok()
        self.on("sync", s_cbb)
        tok[f"bcC{b}"] = tok[f"cbbc{b}"]

        for g in range(2):
            for kt in range(NKI):
                j = g * NKI + kt

                def s_wl(e, m, g=g, kt=kt, j=j):
                    if j >= 2:
                        m.wait(self.T(f"opmm{b}_{j - 2}"))
                    self.stg[j % 2].inc(e.dma_start(
                        out=self.wst[j % 2][:, :],
                        in_=self.opT_d[kt * P:(kt + 1) * P, g * 1024:(g + 1) * 1024]), 16)
                    tok[f"opw{b}_{j}"] = self.stg[j % 2].tok()
                self.on("gpsimd", s_wl)

                def s_om(e, m, g=g, kt=kt, j=j):
                    if j == 0:
                        m.wait(self.T(f"yall{b}"))
                        m.wait(self.T(f"bcC{b}"))
                        if b == 0 and c.B > 1:
                            m.wait(self.T(f"tr{1}_{NALC - 1}"))
                            m.wait(self.T(f"xev{1}_{NALC - 1}_{NHT - 1}"))
                            m.wait(self.T(f"ipev{1}_{NALC - 1}_{2 * NIT - 1}"))
                            m.wait(self.T(f"xpev{1}_{NALC - 1}"))
                            m.wait(self.T(f"var{1}_{NALC - 1}"))
                            m.wait(self.T(f"dtpw{1}_{NALC - 1}_{NIT - 1}"))
                    if g == 1 and kt == 0:
                        for sl in range(2 * NTT):
                            m.wait(self.T(f"opev{b}_0_{sl}"))
                    m.wait(self.T(f"opw{b}_{j}"))
                    for lt in range(NTT):
                        for hc in range(2):
                            ct.inc(e.matmul(
                                PSL[lt * 2 + hc][:, 0:512],
                                y_all[:, kt * TC + lt * P:kt * TC + (lt + 1) * P],
                                self.wst[j % 2][:, hc * 512:(hc + 1) * 512],
                                start=(kt == 0), stop=(kt == NKI - 1)))
                    tok[f"opmm{b}_{j}"] = ct.tok()
                self.on("tensor", s_om)
            tok[f"opg{b}_{g}"] = self.ct.tok()

            for lt in range(NTT):
                def s_xq(e, m, g=g, lt=lt):
                    if g == 0 and lt == 0:
                        if b == 0 and c.B > 1:
                            m.wait(self.T(f"uspill{1}_{NALC - 1}"))
                            m.wait(self.T(f"ubf{1}_{NALC - 1}_{NIT - 1}"))
                        elif b > 0:
                            m.wait(self.T(f"h2tm{b - 1}_{c.NTT - 1}"))
                    if lt >= 2:
                        m.wait(self.T(f"opev{b}_{g}_{(lt - 2) * 2 + 1}"))
                    elif g == 1 and lt == 0:
                        m.wait(self.T(f"opev{b}_0_{2 * NTT - 1}"))
                    self.xq[lt % 2].inc(e.dma_start(out=x_q[lt % 2][:, :],
                                        in_=self.xq_in[b, lt * P:(lt + 1) * P,
                                                       g * 1024:(g + 1) * 1024]), 16)
                    tok[f"xq{b}_{g}_{lt}"] = self.xq[lt % 2].tok()
                self.on("sync", s_xq)

                def s_oe(e, m, g=g, lt=lt):
                    if b > 0 and g == 0 and lt == 0:
                        m.wait(self.T(f"x2w{b - 1}_{c.NTT - 1}"))
                    m.wait(self.T(f"opg{b}_{g}"))
                    m.wait(self.T(f"xq{b}_{g}_{lt}"))
                    if c.debug:
                        m.wait(self.T(f"dbgB{b}")); m.wait(self.T(f"dbgBD{b}")); m.wait(self.T(f"dbgBY{b}"))
                    for hc in range(2):
                        sl = lt * 2 + hc
                        hcol = g * 1024 + hc * 512
                        if sl >= 1:
                            m.wait(self.T(f"opev{b}_{g}_{sl - 1}"))
                        elif g == 1:
                            m.wait(self.T(f"opev{b}_0_{2 * NTT - 1}"))
                        t = cv.inc(e.tensor_mul(self.tev[:, 0:512], PSL[sl][:, 0:512],
                                                gabc[:, hcol:hcol + 512]))
                        m.wait(t)
                        tok[f"opev{b}_{g}_{sl}"] = cv.inc(e.tensor_add(
                            x2_tm[:, lt * H + hcol:lt * H + hcol + 512],
                            self.tev[:, 0:512], x_q[lt % 2][:, hc * 512:(hc + 1) * 512]))
                self.on("vector", s_oe)
        tok[f"x2{b}"] = self.cv.tok()

        if c.debug:
            def s_xd(e, m):
                m.wait(self.T(f"x2{b}"))
                for lt in range(NTT):
                    self.dbgc.inc(e.dma_start(out=self.dbg["x2"][b, lt * P:(lt + 1) * P, :],
                                              in_=x2_tm[:, lt * H:(lt + 1) * H]), 16)
            self.on("sync", s_xd)

        def s_x2w(e, m):
            m.wait(self.T(f"x2{b}"))
            for lt in range(NTT):
                self.cx2.inc(e.dma_start(out=self.x2_d[b][lt * P:(lt + 1) * P, :],
                                         in_=x2_tm[:, lt * H:(lt + 1) * H]), 16)
                tok[f"x2w{b}_{lt}"] = self.cx2.tok()
        self.on("sync", s_x2w)

    def ph_C_p2(self, b):
        c, tok = self.c, self.tok
        cv, ca, ct = self.cv, self.ca, self.ct
        H, TC, NHT, NTT, E = c.H, c.TC, c.NHT, c.NTT, c.E
        psA, psB, psC, psD = self.psA, self.psB, self.psC, self.psD
        v = self._C_views()
        x2_tm, xsq, cbbc, sbbc = v["x2_tm"], v["xsq"], v["cbbc"], v["sbbc"]
        h2_tm, h2b = v["h2_tm"], v["h2b"]
        h2b3 = h2b.rearrange("p (ht t) -> p ht t", t=TC)

        for lt in range(NTT):
            def s_sq(e, m, lt=lt):
                m.wait(self.T(f"x2{b}"))
                if lt >= 1:
                    m.wait(self.T(f"sqred{b}_{lt - 1}"))
                tok[f"x2sq{b}_{lt}"] = ca.inc(e.activation(
                    xsq[:, :], x2_tm[:, lt * H:(lt + 1) * H], AF.Square))
            self.on("scalar", s_sq)

            def s_v1(e, m, lt=lt):
                m.wait(self.T(f"x2{b}"))
                if lt >= 1:
                    m.wait(self.T(f"h2tm{b}_{lt - 1}"))
                t0 = cv.inc(e.reduce_sum(self.scol[:, 0:1], x2_tm[:, lt * H:(lt + 1) * H], axis=AXX))
                m.wait(self.T(f"x2sq{b}_{lt}"))
                tok[f"sqred{b}_{lt}"] = cv.inc(e.reduce_sum(self.scol[:, 1:2], xsq[:, :], axis=AXX))
                m.wait(t0)
                t = cv.inc(e.tensor_scalar_mul(self.scol[:, 0:1], self.scol[:, 0:1], 1.0 / H))
                m.wait(self.T(f"sqred{b}_{lt}"))
                t = cv.inc(e.tensor_scalar_mul(self.scol[:, 1:2], self.scol[:, 1:2], 1.0 / H))
                m.wait(t)
                t = cv.inc(e.tensor_mul(self.scol[:, 2:3], self.scol[:, 0:1], self.scol[:, 0:1]))
                m.wait(t)
                t = cv.inc(e.tensor_sub(self.scol[:, 1:2],
                                        self.scol[:, 1:2], self.scol[:, 2:3]))
                m.wait(t)
                tok[f"varc{b}_{lt}"] = cv.inc(e.tensor_scalar_add(
                    self.scol[:, 1:2], self.scol[:, 1:2], 1e-5))
            self.on("vector", s_v1)

            def s_a1(e, m, lt=lt):
                m.wait(self.T(f"varc{b}_{lt}"))
                tok[f"lnc{b}_{lt}"] = ca.inc(e.activation(self.scol[:, 3:4],
                                                          self.scol[:, 1:2], AF.Ln))
            self.on("scalar", s_a1)

            def s_vh(e, m, lt=lt):
                m.wait(self.T(f"lnc{b}_{lt}"))
                tok[f"nlc{b}_{lt}"] = cv.inc(e.tensor_scalar_mul(
                    self.scol[:, 3:4], self.scol[:, 3:4], -0.5))
            self.on("vector", s_vh)

            def s_a1b(e, m, lt=lt):
                m.wait(self.T(f"nlc{b}_{lt}"))
                tok[f"rstdc{b}_{lt}"] = ca.inc(e.activation(self.scol[:, 3:4],
                                                            self.scol[:, 3:4], AF.Exp))
            self.on("scalar", s_a1b)

            def s_v2(e, m, lt=lt):
                m.wait(self.T(f"rstdc{b}_{lt}")); m.wait(self.T(f"bcC{b}"))
                if lt >= 1:
                    m.wait(self.T(f"h2tr{b}_{lt - 1}"))
                t = cv.inc(e.tensor_scalar(h2_tm[:, :], x2_tm[:, lt * H:(lt + 1) * H],
                                           self.scol[:, 0:1], self.scol[:, 3:4],
                                           ALU.subtract, ALU.mult))
                m.wait(t)
                t = cv.inc(e.tensor_mul(h2_tm[:, :], h2_tm[:, :], cbbc[:, :]))
                m.wait(t)
                tok[f"h2tm{b}_{lt}"] = cv.inc(e.tensor_add(h2_tm[:, :], h2_tm[:, :], sbbc[:, :]))
            self.on("vector", s_v2)

            def s_t1(e, m, lt=lt):
                m.wait(self.T(f"h2tm{b}_{lt}"))
                if lt == 0:
                    m.wait(self.T(f"x2{b}"))
                if lt >= 1:
                    m.wait(self.T(f"h2ev{b}_{lt - 1}"))
                    m.wait(self.T(f"rtc{b}_{lt - 1}_{NHT - 1}"))
                for ht in range(NHT):
                    dstp = psA if ht < 8 else psB
                    ct.inc(e.transpose(dstp[:, (ht % 8) * P:(ht % 8 + 1) * P],
                                       h2_tm[:, ht * P:(ht + 1) * P], self.id_f32[:, :]))
                tok[f"h2tr{b}_{lt}"] = ct.tok()
            self.on("tensor", s_t1)

            def s_a2(e, m, lt=lt):
                if b > 0 and lt == 0:
                    m.wait(self.T("h2bw0"))
                m.wait(self.T(f"h2tr{b}_{lt}"))
                ca.inc(e.activation(h2b3[:, 0:min(8, NHT), lt * P:(lt + 1) * P],
                                    psA[:, 0:min(8, NHT) * P], AF.Copy))
                if NHT > 8:
                    ca.inc(e.activation(h2b3[:, 8:NHT, lt * P:(lt + 1) * P],
                                        psB[:, 0:(NHT - 8) * P], AF.Copy))
                tok[f"h2ev{b}_{lt}"] = ca.tok()
            self.on("scalar", s_a2)

            for ht in range(NHT):
                def s_rc(e, m, lt=lt, ht=ht):
                    m.wait(self.T(f"h2tr{b}_{lt}"))
                    if lt * NHT + ht >= 2:
                        pj = lt * NHT + ht - 2
                        m.wait(self.T(f"rtm2{b}_{pj // NHT}_{pj % NHT}"))
                    src = psA if ht < 8 else psB
                    tok[f"rtc{b}_{lt}_{ht}"] = ca.inc(e.activation(
                        self.h2rt[ht % 2][:, :], src[:, (ht % 8) * P:(ht % 8 + 1) * P],
                        AF.Copy))
                self.on("scalar", s_rc)

                def s_rm(e, m, lt=lt, ht=ht):
                    m.wait(self.T(f"rtc{b}_{lt}_{ht}"))
                    ct.inc(e.matmul(psC[0:E, lt * P:(lt + 1) * P],
                                    self.gwT_sb[:, ht * E:(ht + 1) * E],
                                    self.h2rt[ht % 2][:, :],
                                    start=(ht == 0), stop=(ht == NHT - 1)))
                    ct.inc(e.matmul(psD[0:1, lt * P:(lt + 1) * P],
                                    self.sgwT_sb[:, ht:ht + 1],
                                    self.h2rt[ht % 2][:, :],
                                    start=(ht == 0), stop=(ht == NHT - 1)))
                    tok[f"rtm2{b}_{lt}_{ht}"] = ct.tok()
                self.on("tensor", s_rm)
                tok[f"h2done{b}"] = self.ca.tok()

        if c.debug:
            for ht in range(NHT):
                def s_h2c(e, m, ht=ht):
                    m.wait(self.T(f"h2done{b}"))
                    if ht >= 1:
                        m.wait(self.T(f"h2dd{b}_{ht - 1}"))
                    tok[f"h2dc{b}_{ht}"] = cv.inc(e.tensor_copy(
                        self.tev[:, 0:TC], h2b[:, ht * TC:(ht + 1) * TC]))
                self.on("vector", s_h2c)

                def s_h2w(e, m, ht=ht):
                    m.wait(self.T(f"h2dc{b}_{ht}"))
                    self.dbg2.inc(e.dma_start(
                        out=self.dbg["h2"][b, ht * P:(ht + 1) * P, :],
                        in_=self.tev[:, 0:TC]), 16)
                    tok[f"h2dd{b}_{ht}"] = self.dbg2.tok()
                self.on("sync", s_h2w)

        tok[f"rmm{b}"] = self.ct.tok()
        tok[f"sgmm{b}"] = self.ct.tok()

    def ph_C_p3(self, b):
        c, tok = self.c, self.tok
        cd, cv, ca, ct = self.cd, self.cv, self.ca, self.ct
        H, TC, NHT, NTT, E = c.H, c.TC, c.NHT, c.NTT, c.E
        psC, psD = self.psC, self.psD
        v = self._C_views()
        h2b = v["h2b"]

        def s_re(e, m):
            m.wait(self.T(f"rmm{b}"))
            if b > 0:
                m.wait(self.T(f"cmbw{b - 1}"))
            tok[f"rsb{b}"] = cv.inc(e.tensor_copy(self.rsb[:, :], psC[0:E, 0:TC]))
        self.on("vector", s_re)

        def s_sg(e, m):
            m.wait(self.T(f"sgmm{b}")); m.wait(self.T(f"gacl{b}"))
            tok[f"sgrow{b}"] = ca.inc(e.activation(self.rows[0:1, 2048:2048 + TC],
                                                   psD[0:1, 0:TC], AF.Sigmoid))
        self.on("scalar", s_sg)

        def s_rtr(e, m):
            m.wait(self.T(f"rsb{b}"))
            m.wait(self.T(f"sgrow{b}"))
            for lt in range(NTT):
                ct.inc(e.transpose(psD[:, lt * E:(lt + 1) * E],
                                   self.rsb[0:E, lt * P:(lt + 1) * P], self.id_f32[0:E, 0:E]))
            tok[f"rtr{b}"] = ct.tok()
        self.on("tensor", s_rtr)

        def s_rc(e, m):
            m.wait(self.T(f"rtr{b}"))
            tok[f"rtm{b}"] = cv.inc(e.tensor_copy(self.r_tm[:, 0:NTT * E],
                                                  psD[:, 0:NTT * E]))
        self.on("vector", s_rc)

        for lt in range(NTT):
            sl = slice(lt * E, (lt + 1) * E)

            def s_m1(e, m, lt=lt, sl=sl):
                m.wait(self.T(f"rtm{b}"))
                tok[f"nmx{b}_{lt}"] = cv.inc(e.reduce_max(self.scol[:, 4:5],
                                                          self.r_tm[:, sl], axis=AXX,
                                                          negate=True))
            self.on("vector", s_m1)

            def s_ex(e, m, lt=lt, sl=sl):
                m.wait(self.T(f"nmx{b}_{lt}"))
                tok[f"smex{b}_{lt}"] = ca.inc(e.activation(
                    self.rt1[:, sl], self.r_tm[:, sl], AF.Exp, bias=self.scol[:, 4:5]))
            self.on("scalar", s_ex)

            def s_m2(e, m, lt=lt, sl=sl):
                m.wait(self.T(f"smex{b}_{lt}"))
                t = cv.inc(e.reduce_sum(self.scol[:, 5:6], self.rt1[:, sl], axis=AXX))
                m.wait(t)
                t = cv.inc(e.reciprocal(self.scol[:, 5:6], self.scol[:, 5:6]))
                m.wait(t)
                t = cv.inc(e.tensor_scalar_mul(self.r_tm[:, sl], self.rt1[:, sl],
                                               self.scol[:, 5:6]))
                m.wait(t)
                t = cv.inc(e.reduce_max(self.scol[:, 6:7], self.r_tm[:, sl], axis=AXX))
                m.wait(t)
                t = cv.inc(e.tensor_scalar(self.rt1[:, sl], self.r_tm[:, sl],
                                           self.scol[:, 6:7], None, ALU.is_equal))
                m.wait(t)
                t = cv.inc(e.scalar_tensor_tensor(self.rt1[:, sl], self.rt1[:, sl],
                                                  -1e30, self.r_tm[:, sl],
                                                  ALU.mult, ALU.add))
                m.wait(t)
                t = cv.inc(e.reduce_max(self.scol[:, 7:8], self.rt1[:, sl], axis=AXX))
                m.wait(t)
                t = cv.inc(e.tensor_scalar(self.rt1[:, sl], self.r_tm[:, sl],
                                           self.scol[:, 7:8], None, ALU.is_ge))
                m.wait(t)
                tok[f"cmb{b}_{lt}"] = cv.inc(e.tensor_mul(self.r_tm[:, sl],
                                                          self.r_tm[:, sl], self.rt1[:, sl]))
            self.on("vector", s_m2)
        tok[f"cmbv{b}"] = self.cv.tok()

        def s_ctr(e, m):
            m.wait(self.T(f"cmbv{b}"))
            for lt in range(NTT):
                ct.inc(e.transpose(psD[0:E, 512 + lt * P:512 + (lt + 1) * P],
                                   self.r_tm[:, lt * E:(lt + 1) * E], self.id_f32[:, :]))
            tok[f"ctr{b}"] = ct.tok()
        self.on("tensor", s_ctr)

        def s_ce(e, m):
            m.wait(self.T(f"ctr{b}"))
            tok[f"cmbsb{b}"] = cv.inc(e.tensor_copy(self.cmbsb[:, :],
                                                    psD[0:E, 512:512 + TC]))
        self.on("vector", s_ce)

        def s_cw(e, m):
            m.wait(self.T(f"cmbsb{b}"))
            self.cr[0].inc(e.dma_start(out=self.cmb_d[b][:, :], in_=self.cmbsb[:, :]), 16)
            tok[f"cmbw{b}"] = self.cr[0].tok()
            if c.debug:
                self.dbgc.inc(e.dma_start(out=self.dbg["cmb"][b, :, :], in_=self.cmbsb[:, :]), 16)
        self.on("sync", s_cw)

        def s_sb(e, m):
            m.wait(self.T(f"sgrow{b}"))
            m.wait((self.cbc, self.cbc.count))
            self.cbc.inc(e.dma_start(out=self.rowsd[0:1, 2 * H:2 * H + TC],
                                     in_=self.rows[0:1, 2048:2048 + TC]), 16)
            t = (self.cbc, self.cbc.count)
            m.wait(t)
            self.cbc.inc(e.dma_start(out=self.sgbc[:, b * TC:(b + 1) * TC], in_=self.bcast(
                self.sgbc[:, b * TC:(b + 1) * TC], self.rowsd[0:1, 2 * H:2 * H + TC])), 16)
            tok[f"sgbc{b}"] = self.cbc.tok()
        self.on("sync", s_sb)

        if b == 0:
            def s_h2w0(e, m):
                m.wait(self.T(f"h2ev0_{NTT - 1}"))
                if c.debug:
                    m.wait(self.T(f"h2dd0_{NHT - 1}"))
                self.ch2.inc(e.dma_start(out=self.h2b_d0[:, :], in_=h2b[:, :]), 16)
                tok["h2bw0"] = self.ch2.tok()
            self.on("sync", s_h2w0)


    def ph_D(self):
        c, tok = self.c, self.tok
        cd, cq, cv, ca, ct = self.cd, self.cq, self.cv, self.ca, self.ct
        B, H, TC, NHT, NTT, E = c.B, c.H, c.TC, c.NHT, c.NTT, c.E
        NFM, NFS = c.NFM, c.NFS
        psA, psB, psC, psD = self.psA, self.psB, self.psC, self.psD
        PSL = self.PSL
        NKI = c.I // P
        ymoe_b = [self.v32(self.AR1, 0, NHT * TC), self.v32(self.AR2, 0, NHT * TC)]
        h2b_b = [self.vbf(self.AR3, 0, NHT * TC), self.vbf(self.AR6, 0, NHT * TC)]
        s1_sb = self.v32(self.AR3, NHT * TC // 2, 8 * TC)
        he_b = [self.vbf(self.AR4, 0, NFM * TC),
                self.vbf(self.AR4, NFM * TC // 2 + 32, NFM * TC)]
        sho = NFM * TC + 64
        shf_b = [self.vbf(self.AR4, sho, 4 * TC),
                 self.vbf(self.AR4, sho + 2 * TC + 16, 4 * TC)]
        gbbc = self.v32(self.AR3, NHT * TC // 2, H)
        x2t = self.v32(self.AR3, NHT * TC // 2 + H, H)
        rows = self.rows
        mg_f = [list(range(g, min(g + 4, NFM))) for g in range(0, NFM, 4)]
        hgrp = [list(range(g, min(g + 4, NHT))) for g in range(0, NHT, 4)]
        NMG, NHG = len(mg_f), len(hgrp)
        lastC = B - 1

        def s_h2l(e, m):
            m.wait(self.T(f"h2tm{lastC}_{NTT - 1}"))
            m.wait(self.T("h2bw0"))
            self.ch2.inc(e.dma_start(out=h2b_b[0][:, :], in_=self.h2b_d0[:, :]), 16)
            tok["h2bl0"] = self.ch2.tok()
        self.on("sync", s_h2l)

        def s_init(e, m):
            m.wait(self.T(f"cmbw{lastC}"))
            if c.debug:
                m.wait((self.dbgc, self.dbgc.count))
            cv.inc(e.memset(ymoe_b[0][:, :], 0.0))
            m.wait(self.T(f"x2w{lastC}_{NTT - 1}"))
            cv.inc(e.memset(ymoe_b[1][:, :], 0.0))
            tok["ymz"] = cv.tok()
        self.on("vector", s_init)

        for b in range(B):
            def s_gr(e, m, b=b):
                m.wait(self.T("modflat"))
                m.wait(self.T(f"cbbc{lastC}"))
                m.wait(self.T(f"sgbc{lastC}"))
                if b > 0:
                    m.wait(self.T("gbw0"))
                self.cdM.inc(e.dma_start(out=rows[0:1, 0:H],
                                         in_=self.modflat_d[b:b + 1, 5 * H:6 * H]), 16)
                tok[f"gbrow{b}"] = self.cdM.tok()
            self.on("sync", s_gr)
            self.emit_clamp(b, 0, 2048, lambda b=b: tok[f"gbrow{b}"], f"gbcl{b}")

            def s_gw(e, m, b=b):
                m.wait(self.T(f"gbcl{b}"))
                self.cdM.inc(e.dma_start(out=self.rowsd[0:1, (3 + b) * H:(4 + b) * H],
                                         in_=rows[0:1, 0:H]), 16)
                tok[f"gbw{b}"] = self.cdM.tok()
            self.on("sync", s_gw)

        self.wstD = [self.wst[0], self.wst[1],
                     self.vbf(self.AR5, 0, 1024), self.vbf(self.AR5, 512, 1024)]
        self.dring = []

        def wload(src_ap, width, cons_name):
            i = len(self.dring)
            sl = i % 4
            war = self.dring[i - 4] if i >= 4 else None
            self.dring.append(cons_name)

            def s_l(e, m, i=i, sl=sl, war=war, src_ap=src_ap, width=width):
                if war is not None:
                    m.wait(self.T(war))
                elif sl >= 2:
                    m.wait(self.T(f"x2{lastC}"))
                else:
                    m.wait(self.T(f"opmm{lastC}_{2 * NKI - 2 + sl}"))
                self.stg[sl].inc(e.dma_start(out=self.wstD[sl][:, 0:width],
                                             in_=src_ap), 16)
                tok[f"dwl{i}"] = self.stg[sl].tok()
            self.on("gpsimd", s_l)
            return i, sl

        def fc_psum_war(sl, groups):
            for gi in reversed(range(len(groups))):
                if sl < 2 * len(groups[gi]):
                    return gi
            return None

        for ee in range(E):
            def s_cbc2(e, m, ee=ee):
                for b in range(B):
                    m.wait(self.T(f"cmbw{b}"))
                if ee >= 1:
                    m.wait(self.T(f"hev{ee - 1}_{NMG - 1}"))
                for b in range(B):
                    self.cr[ee % 2].inc(e.dma_start(
                        out=self.cmbbc[:, b * TC:(b + 1) * TC],
                        in_=self.bcast(self.cmbbc[:, b * TC:(b + 1) * TC],
                                       self.cmb_d[b][ee:ee + 1, :])), 16)
                tok[f"cbc{ee}"] = (self.cr[ee % 2], self.cr[ee % 2].count)
            self.on("sync", s_cbc2)

            for gi, grp in enumerate(mg_f):
                g0, gn = grp[0], len(grp)
                for wi, wsrc in ((0, self.fc1T_d[ee]), (1, self.fc2T_d[ee])):
                    for kt in range(NHT):
                        li, sl_w = wload(
                            wsrc[kt * P:(kt + 1) * P, g0 * P:(g0 + gn) * P],
                            gn * P, f"fmm{ee}_{wi}_{gi}_{kt}")

                        def s_fm(e, m, ee=ee, wi=wi, gi=gi, gn=gn, kt=kt,
                                 li=li, sl_w=sl_w):
                            if kt == 0:
                                for b in range(B):
                                    m.wait(self.T(f"h2done{b}"))
                                m.wait(self.T("h2bl0"))
                                if wi == 0 and gi == 0:
                                    m.wait(self.T(f"cbc{ee}"))
                            m.wait(self.T(f"dwl{li}"))
                            for mi in range(gn):
                                for b in range(B):
                                    sl = mi * 2 + b
                                    if kt == 0:
                                        if wi == 1:
                                            m.wait(self.T(f"silt{ee}_{gi}_{sl}"))
                                        elif ee == 0 and gi == 0:
                                            m.wait(self.T(f"cmbw{lastC}"))
                                        elif gi == 0:
                                            m.wait(self.T(f"ymt{ee - 1}_{NHG - 1}_{sl}"))
                                        else:
                                            m.wait(self.T(f"hepf{ee}_{gi - 1}_{sl}"))
                                    ct.inc(e.matmul(PSL[sl][:, 0:TC],
                                                    self.wstD[sl_w][:, mi * P:(mi + 1) * P],
                                                    h2b_b[b][:, kt * TC:(kt + 1) * TC],
                                                    start=(kt == 0), stop=(kt == NHT - 1)))
                            tok[f"fmm{ee}_{wi}_{gi}_{kt}"] = ct.tok()
                        self.on("tensor", s_fm)

                    if wi == 0:
                        def s_si(e, m, ee=ee, gi=gi, gn=gn):
                            m.wait(self.T(f"fmm{ee}_0_{gi}_{NHT - 1}"))
                            for mi in range(gn):
                                for b in range(B):
                                    sl = mi * 2 + b
                                    tok[f"silt{ee}_{gi}_{sl}"] = ca.inc(
                                        e.activation(s1_sb[:, sl * TC:(sl + 1) * TC],
                                                     PSL[sl][:, 0:TC], AF.Silu))
                            tok[f"sil{ee}_{gi}"] = ca.tok()
                        self.on("scalar", s_si)
                    else:
                        def s_he(e, m, ee=ee, gi=gi, g0=g0, gn=gn):
                            if c.debug:
                                for b in range(B):
                                    m.wait(self.T(f"dbgB{b}")); m.wait(self.T(f"dbgBD{b}"))
                                    m.wait(self.T(f"dbgBY{b}")); m.wait(self.T(f"h2dd{b}_{NHT - 1}"))
                            m.wait(self.T(f"fmm{ee}_1_{gi}_{NHT - 1}"))
                            m.wait(self.T(f"sil{ee}_{gi}"))
                            m.wait(self.T(f"cbc{ee}"))
                            for mi in range(gn):
                                for b in range(B):
                                    sl = mi * 2 + b
                                    m.wait((cv, cv.count))
                                    tok[f"hepf{ee}_{gi}_{sl}"] = cv.inc(
                                        e.tensor_mul(self.tev[:, 0:TC],
                                                     s1_sb[:, sl * TC:(sl + 1) * TC],
                                                     PSL[sl][:, 0:TC]))
                                    m.wait(self.T(f"hepf{ee}_{gi}_{sl}"))
                                    tok[f"hew{ee}_{g0 + mi}_{b}"] = cv.inc(e.tensor_mul(
                                        he_b[b][:, (g0 + mi) * TC:(g0 + mi + 1) * TC],
                                        self.tev[:, 0:TC], self.cmbbc[:, b * TC:(b + 1) * TC]))
                            tok[f"hev{ee}_{gi}"] = cv.tok()
                        self.on("vector", s_he)

            for g2, grp2 in enumerate(hgrp):
                h0, hn = grp2[0], len(grp2)
                for kt in range(NFM):
                    li, sl_w = wload(
                        self.prT_d[ee][kt * P:(kt + 1) * P, h0 * P:(h0 + hn) * P],
                        hn * P, f"pmm{ee}_{g2}_{kt}")

                    def s_pm(e, m, ee=ee, g2=g2, hn=hn, kt=kt, li=li, sl_w=sl_w):
                        for b in range(B):
                            m.wait(self.T(f"hew{ee}_{kt}_{b}"))
                        m.wait(self.T(f"dwl{li}"))
                        for mi in range(hn):
                            for b in range(B):
                                sl = mi * 2 + b
                                if kt == 0:
                                    if g2 >= 1:
                                        m.wait(self.T(f"ymt{ee}_{g2 - 1}_{sl}"))
                                    else:
                                        lg = fc_psum_war(sl, mg_f)
                                        if lg is not None:
                                            m.wait(self.T(f"hepf{ee}_{lg}_{sl}"))
                                        elif ee >= 1:
                                            m.wait(self.T(f"ymt{ee - 1}_{NHG - 1}_{sl}"))
                                        else:
                                            m.wait(self.T(f"cmbw{lastC}"))
                                ct.inc(e.matmul(PSL[sl][:, 0:TC],
                                                self.wstD[sl_w][:, mi * P:(mi + 1) * P],
                                                he_b[b][:, kt * TC:(kt + 1) * TC],
                                                start=(kt == 0), stop=(kt == NFM - 1)))
                        tok[f"pmm{ee}_{g2}_{kt}"] = ct.tok()
                    self.on("tensor", s_pm)

                def s_pe(e, m, ee=ee, g2=g2, h0=h0, hn=hn):
                    m.wait(self.T(f"pmm{ee}_{g2}_{NFM - 1}"))
                    m.wait(self.T("ymz"))
                    for mi in range(hn):
                        for b in range(B):
                            mt = h0 + mi
                            tok[f"ymt{ee}_{g2}_{mi * 2 + b}"] = cv.inc(
                                e.tensor_add(ymoe_b[b][:, mt * TC:(mt + 1) * TC],
                                             ymoe_b[b][:, mt * TC:(mt + 1) * TC],
                                             PSL[mi * 2 + b][:, 0:TC]))
                    tok[f"ymadd{ee}_{g2}"] = cv.tok()
                self.on("vector", s_pe)

        nfc = (NFS + 3) // 4
        for fc in range(nfc):
            f0 = fc * 4
            fn = min(4, NFS - f0)
            sgrp = [list(range(fn))]
            for wi, wsrc in ((0, self.sf1T_d), (1, self.sf2T_d)):
                for kt in range(NHT):
                    li, sl_w = wload(
                        wsrc[kt * P:(kt + 1) * P, f0 * P:(f0 + fn) * P],
                        fn * P, f"smm{fc}_{wi}_{kt}")

                    def s_sm(e, m, fc=fc, fn=fn, wi=wi, kt=kt, li=li, sl_w=sl_w):
                        m.wait(self.T(f"dwl{li}"))
                        for mi in range(fn):
                            for b in range(B):
                                sl = mi * 2 + b
                                if kt == 0:
                                    if wi == 1:
                                        m.wait(self.T(f"ssilt{fc}_{sl}"))
                                    elif fc == 0:
                                        m.wait(self.T(f"ymt{E - 1}_{NHG - 1}_{sl}"))
                                    else:
                                        m.wait(self.T(f"yst{fc - 1}_{NHG - 1}_{sl}"))
                                ct.inc(e.matmul(PSL[sl][:, 0:TC],
                                                self.wstD[sl_w][:, mi * P:(mi + 1) * P],
                                                h2b_b[b][:, kt * TC:(kt + 1) * TC],
                                                start=(kt == 0), stop=(kt == NHT - 1)))
                        tok[f"smm{fc}_{wi}_{kt}"] = ct.tok()
                    self.on("tensor", s_sm)

                if wi == 0:
                    def s_ss(e, m, fc=fc, fn=fn):
                        m.wait(self.T(f"smm{fc}_0_{NHT - 1}"))
                        for mi in range(fn):
                            for b in range(B):
                                sl = mi * 2 + b
                                tok[f"ssilt{fc}_{sl}"] = ca.inc(
                                    e.activation(s1_sb[:, sl * TC:(sl + 1) * TC],
                                                 PSL[sl][:, 0:TC], AF.Silu))
                        tok[f"ssil{fc}"] = ca.tok()
                    self.on("scalar", s_ss)
                else:
                    def s_sh(e, m, fc=fc, fn=fn):
                        m.wait(self.T(f"smm{fc}_1_{NHT - 1}"))
                        m.wait(self.T(f"ssil{fc}"))
                        for mi in range(fn):
                            for b in range(B):
                                sl = mi * 2 + b
                                t = cv.inc(e.tensor_mul(shf_b[b][:, mi * TC:(mi + 1) * TC],
                                                        s1_sb[:, sl * TC:(sl + 1) * TC],
                                                        PSL[sl][:, 0:TC]))
                                tok[f"shpf{fc}_{sl}"] = t
                                tok[f"shw{fc}_{mi}_{b}"] = t
                        tok[f"shev{fc}"] = cv.tok()
                    self.on("vector", s_sh)

            for g2, grp2 in enumerate(hgrp):
                h0, hn = grp2[0], len(grp2)
                for ktc in range(fn):
                    li, sl_w = wload(
                        self.sprT_d[(f0 + ktc) * P:(f0 + ktc + 1) * P,
                                    h0 * P:(h0 + hn) * P],
                        hn * P, f"spm{fc}_{g2}_{ktc}")

                    def s_spm(e, m, fc=fc, fn=fn, g2=g2, hn=hn, ktc=ktc,
                              li=li, sl_w=sl_w):
                        for b in range(B):
                            m.wait(self.T(f"shw{fc}_{ktc}_{b}"))
                        m.wait(self.T(f"dwl{li}"))
                        for mi in range(hn):
                            for b in range(B):
                                sl = mi * 2 + b
                                if ktc == 0:
                                    if g2 >= 1:
                                        m.wait(self.T(f"yst{fc}_{g2 - 1}_{sl}"))
                                    else:
                                        lg = fc_psum_war(sl, [list(range(fn))])
                                        if lg is not None:
                                            m.wait(self.T(f"shpf{fc}_{sl}"))
                                        elif fc >= 1:
                                            m.wait(self.T(f"yst{fc - 1}_{NHG - 1}_{sl}"))
                                        else:
                                            m.wait(self.T(f"ymt{E - 1}_{NHG - 1}_{sl}"))
                                ct.inc(e.matmul(PSL[sl][:, 0:TC],
                                                self.wstD[sl_w][:, mi * P:(mi + 1) * P],
                                                shf_b[b][:, ktc * TC:(ktc + 1) * TC],
                                                start=(ktc == 0), stop=(ktc == fn - 1)))
                        tok[f"spm{fc}_{g2}_{ktc}"] = ct.tok()
                    self.on("tensor", s_spm)

                def s_spe(e, m, fc=fc, fn=fn, g2=g2, h0=h0, hn=hn):
                    m.wait(self.T(f"spm{fc}_{g2}_{fn - 1}"))
                    for b in range(B):
                        m.wait(self.T(f"sgbc{b}"))
                    m.wait(self.T("ymz"))
                    for mi in range(hn):
                        for b in range(B):
                            mt = h0 + mi
                            m.wait((cv, cv.count))
                            t = cv.inc(e.tensor_mul(self.tev[:, 0:TC],
                                                    PSL[mi * 2 + b][:, 0:TC],
                                                    self.sgbc[:, b * TC:(b + 1) * TC]))
                            m.wait(t)
                            tok[f"yst{fc}_{g2}_{mi * 2 + b}"] = cv.inc(
                                e.tensor_add(ymoe_b[b][:, mt * TC:(mt + 1) * TC],
                                             ymoe_b[b][:, mt * TC:(mt + 1) * TC],
                                             self.tev[:, 0:TC]))
                    tok[f"ysadd{fc}_{g2}"] = cv.tok()
                self.on("vector", s_spe)

        tok["ysadd_last"] = tok[f"ysadd{nfc - 1}_{NHG - 1}"]

        for b in range(B):
            def s_gbb(e, m, b=b):
                m.wait(self.T("ysadd_last"))
                m.wait(self.T(f"gbw{b}"))
                m.wait((self.cbc, self.cbc.count))
                self.cbc.inc(e.dma_start(out=gbbc[:, :], in_=self.bcast(
                    gbbc[:, :], self.rowsd[0:1, (3 + b) * H:(4 + b) * H])), 16)
                tok[f"gbbcf{b}"] = self.cbc.tok()
            self.on("sync", s_gbb)

            for lt in range(NTT):
                def s_xl(e, m, b=b, lt=lt):
                    m.wait(self.T("ysadd_last"))
                    if lt > 0:
                        m.wait(self.T(f"x3w{b}_{lt - 1}"))
                    elif b > 0:
                        m.wait(self.T(f"x3w{b - 1}_{NTT - 1}"))
                    self.cx2.inc(e.dma_start(out=x2t[:, :],
                                             in_=self.x2_d[b][lt * P:(lt + 1) * P, :]), 16)
                    tok[f"x2l{b}_{lt}"] = self.cx2.tok()
                self.on("sync", s_xl)

                def s_ft(e, m, b=b, lt=lt):
                    if b == 0 and lt == 0:
                        m.wait(self.T("ysadd_last"))
                    if lt >= 1:
                        m.wait(self.T(f"fev{b}_{lt - 1}"))
                    elif b >= 1:
                        m.wait(self.T(f"fev{b - 1}_{NTT - 1}"))
                    for ctile in range(NHT):
                        dstp = psA if ctile < 8 else psB
                        ct.inc(e.transpose(
                            dstp[:, (ctile % 8) * P:(ctile % 8 + 1) * P],
                            ymoe_b[b][:, ctile * TC + lt * P:ctile * TC + (lt + 1) * P],
                            self.id_f32[:, :]))
                    tok[f"ftr{b}_{lt}"] = ct.tok()
                self.on("tensor", s_ft)

                def s_fe(e, m, b=b, lt=lt):
                    m.wait(self.T(f"gbbcf{b}"))
                    m.wait(self.T(f"ftr{b}_{lt}"))
                    m.wait(self.T(f"x2l{b}_{lt}"))
                    for j in range(H // 512):
                        pp = psA[:, (j % 2) * 512:(j % 2) * 512 + 512] if j < 2 else \
                            psB[:, ((j - 2) % 2) * 512:((j - 2) % 2) * 512 + 512]
                        m.wait((cv, cv.count))
                        t = cv.inc(e.tensor_mul(self.tev[:, 0:512], pp,
                                                gbbc[:, j * 512:(j + 1) * 512]))
                        m.wait(t)
                        cv.inc(e.tensor_add(x2t[:, j * 512:(j + 1) * 512],
                                            self.tev[:, 0:512],
                                            x2t[:, j * 512:(j + 1) * 512]))
                    tok[f"fev{b}_{lt}"] = cv.tok()
                self.on("vector", s_fe)

                def s_fw(e, m, b=b, lt=lt):
                    m.wait(self.T(f"fev{b}_{lt}"))
                    self.outW.inc(e.dma_start(out=self.out_d[b, lt * P:(lt + 1) * P, :],
                                              in_=x2t[:, 0:H]), 16)
                    tok[f"x3w{b}_{lt}"] = self.outW.tok()
                self.on("sync", s_fw)
        tok["out_all"] = self.outW.tok()

    def ph_final(self):
        chains_sync = [self.cd, self.cdM, self.xs, self.spU, self.spG, self.ug,
                       self.ssmW[0], self.ssmW[1], self.ygW, self.ya, self.outW, self.dbgc,
                       self.cr[0], self.cr[1], self.xq[0], self.xq[1], self.dbg2, self.dbgD, self.dbgY,
                       self.rbc, self.cbc, self.cx2, self.ch2, self.cdw[0], self.cdw[1],
                       self.cdl2[0], self.cdl2[1]]
        chains_gp = [self.cg, self.cgM] + self.cdl + self.stg
        chains_sc = [self.st[0], self.st[1], self.cq]

        def s_s(e, m):
            for ch in chains_sync:
                if ch.count:
                    e.wait_ge(ch.sem, ch.count)
        self.on("sync", s_s)

        def s_p(e, m):
            for ch in chains_gp:
                if ch.count:
                    e.wait_ge(ch.sem, ch.count)
        self.on("gpsimd", s_p)

        def s_q(e, m):
            for ch in chains_sc:
                if ch.count:
                    e.wait_ge(ch.sem, ch.count)
        self.on("scalar", s_q)


def build(cfg):
    kb1 = KB(cfg)
    kb1.alloc()
    kb1.emit()
    rec = {k: (v[0].name, v[1]) for k, v in kb1.tok.items() if v is not None}
    kb = KB(cfg, rec=rec)
    kb.alloc()
    nc = kb.emit()
    return nc, kb.dbg


def _stage(inputs, cfg, core):
    c = cfg
    ic0 = core * c.IC
    tc0 = core * c.TC
    T = lambda a: np.ascontiguousarray(np.swapaxes(np.asarray(a), -1, -2))
    ipw = np.asarray(inputs["L0_attn_in_proj_w"])
    ipT = np.ascontiguousarray(
        np.concatenate([ipw[ic0:ic0 + c.IC], ipw[c.I + ic0:c.I + ic0 + c.IC]], 0).T)
    x = np.ascontiguousarray(np.asarray(inputs["L0_attn_x"], dtype=np.float32))
    d = dict(
        x=x,
        xq=x[:, tc0:tc0 + c.TC, :],
        ipT=ipT,
        convw=np.asarray(inputs["L0_attn_conv_w"])[ic0:ic0 + c.IC, 0, :],
        convb=np.asarray(inputs["L0_attn_conv_b"])[ic0:ic0 + c.IC, None],
        xpT=np.asarray(inputs["L0_attn_x_proj_w"]).T[ic0:ic0 + c.IC, :][
            :, list(range(c.R)) + [c.R + (i // 2) + (i % 2) * c.N
                                   for i in range(2 * c.N)]],
        dtpT=np.asarray(inputs["L0_attn_dt_proj_w"]).T[:, ic0:ic0 + c.IC],
        dtb=np.asarray(inputs["L0_attn_dt_proj_b"])[ic0:ic0 + c.IC, None],
        alog=np.asarray(inputs["L0_attn_A_log"])[ic0:ic0 + c.IC, :],
        dvec=np.asarray(inputs["L0_attn_D"])[ic0:ic0 + c.IC, None],
        opT=np.asarray(inputs["L0_attn_out_proj_w"]).T,
        fc1T=T(inputs["L0_ffn_expert_fc1"]),
        fc2T=T(inputs["L0_ffn_expert_fc2"]),
        prT=T(inputs["L0_ffn_expert_proj"]),
        sf1T=np.asarray(inputs["L0_ffn_shared_fc1"]).T,
        sf2T=np.asarray(inputs["L0_ffn_shared_fc2"]).T,
        sprT=np.asarray(inputs["L0_ffn_shared_proj"]).T,
        gwT=np.asarray(inputs["L0_ffn_gate_w"]).T,
        sgwT=np.asarray(inputs["L0_ffn_shared_gate_w"]).T,
        c_in=np.asarray(inputs["L0_c"]),
        wmod=np.asarray(inputs["L0_wmod"])[:, core * c.MODSL:(core + 1) * c.MODSL],
    )
    return {k: np.ascontiguousarray(np.asarray(v), dtype=np.float32) for k, v in d.items()}


def kernel(**inputs):
    cfg = CFG()
    in_maps = [_stage(inputs, cfg, core) for core in range(NCORE)]
    nc, _ = build(cfg)
    res = run_bass_kernel_spmd(nc, in_maps, core_ids=list(range(NCORE)))
    out = np.zeros((cfg.B, cfg.L, cfg.H), dtype=np.float32)
    for core in range(NCORE):
        o = np.asarray(res.results[core]["out"])
        out[:, core * cfg.TC:(core + 1) * cfg.TC, :] = o
    return out


# version 12
# speedup vs baseline: 1.2124x; 1.2124x over previous
import numpy as np
import ml_dtypes
import concourse.bass as bass
import concourse.mybir as mybir
from concourse.bass_utils import run_bass_kernel_spmd
from concourse.aot_env import allow_ici_all_to_all

F32 = mybir.dt.float32
BF16 = mybir.dt.bfloat16
AF = mybir.ActivationFunctionType
ALU = mybir.AluOpType
AXX = mybir.AxisListType.X

NCORE = 8
P = 128


class Chain:
    def __init__(self, nc, name, registry=None):
        self._cm = nc.semaphore(name)
        self.sem = self._cm.__enter__()
        self.name = name
        self.count = 0
        if registry is not None:
            registry[name] = self

    def inc(self, instr, amt=1):
        instr.then_inc(self.sem, amt)
        self.count += amt
        return (self, self.count)

    def tok(self):
        return (self, self.count)


class Em:
    def __init__(self):
        self.e = None
        self.waited = {}

    def wait(self, tok):
        if tok is None:
            return
        chain, val = tok
        if self.waited.get(chain.name, 0) < val:
            self.e.wait_ge(chain.sem, val)
            self.waited[chain.name] = val


class CFG:
    def __init__(self, B=2, L=4096, H=2048, I=4096, N=16, R=128, K=4,
                 E=8, FM=1408, FS=5632, debug=False):
        self.B, self.L, self.H, self.I, self.N, self.R, self.K = B, L, H, I, N, R, K
        self.E, self.FM, self.FS = E, FM, FS
        self.IC = I // NCORE
        self.TC = L // NCORE
        self.NIT = self.IC // P
        self.NHT = H // P
        self.NFM = (FM + P - 1) // P
        self.NFS = FS // P
        self.NTT = self.TC // P
        self.ALC = min(512, L)
        self.NALC = L // self.ALC
        self.NAT = self.ALC // P
        self.SLC = min(512, L)
        self.NSLC = L // self.SLC
        self.MODSL = 6 * H // NCORE
        self.debug = debug


class KB:
    def __init__(self, cfg, rec=None):
        self.c = cfg
        self.nc = bass.Bass("TRN2", target_bir_lowering=False)
        self.tok = {}
        self.ems = {}
        self.dbg = {}
        self.rec = rec
        self.chains = {}

    def T(self, name):
        t = self.tok.get(name)
        if t is not None:
            return t
        if self.rec is None:
            return None
        cname, val = self.rec[name]
        return (self.chains[cname], val)

    def on(self, eng, fn):
        em = self.ems.setdefault(eng, Em())

        def wrap(e):
            em.e = e
            fn(e, em)

        getattr(self.block, eng)(wrap)

    def alloc(self):
        c, nc = self.c, self.nc
        B, L, H, I, N, R, K = c.B, c.L, c.H, c.I, c.N, c.R, c.K
        E, FM, FS, IC, TC = c.E, c.FM, c.FS, c.IC, c.TC
        NIT, NHT = c.NIT, c.NHT

        def din(name, shape):
            return nc.dram_tensor(name, list(shape), F32, kind="ExternalInput").ap()

        self.x_in = din("x", [B, L, H])
        self.xq_in = din("xq", [B, TC, H])
        self.ipT_d = din("ipT", [H, 2 * IC])
        self.convw_d = din("convw", [IC, K])
        self.convb_d = din("convb", [IC, 1])
        self.xpT_d = din("xpT", [IC, R + 2 * N])
        self.dtpT_d = din("dtpT", [R, IC])
        self.dtb_d = din("dtb", [IC, 1])
        self.alog_d = din("alog", [IC, N])
        self.dvec_d = din("dvec", [IC, 1])
        self.opT_d = din("opT", [I, H])
        self.fc1T_d = din("fc1T", [E, H, FM])
        self.fc2T_d = din("fc2T", [E, H, FM])
        self.prT_d = din("prT", [E, FM, H])
        self.sf1T_d = din("sf1T", [H, FS])
        self.sf2T_d = din("sf2T", [H, FS])
        self.sprT_d = din("sprT", [FS, H])
        self.gwT_d = din("gwT", [H, E])
        self.sgwT_d = din("sgwT", [H, 1])
        self.cvec_d = din("c_in", [B, H])
        self.wmod_d = din("wmod", [H, c.MODSL])
        self.out_d = nc.dram_tensor("out", [B, TC, H], F32, kind="ExternalOutput").ap()

        def dint(name, shape, dt=F32):
            return nc.dram_tensor(name, list(shape), dt, kind="Internal")

        self.modsl_d = dint("modsl_d", [B, c.MODSL])
        self.modg_d = dint("modg_d", [NCORE * B, c.MODSL])
        self.modflat_d = dint("modflat_d", [B, 6 * H])
        self.ssmp_d = [dint(f"ssmp_d{b}", [R + 2 * N, L]) for b in range(B)]
        self.ssm_d = [dint(f"ssm_d{b}", [R + 2 * N, L]) for b in range(B)]
        self.usp_d = [dint(f"usp_d{b}", [IC, L]) for b in range(B)]
        self.gsp_d = [dint(f"gsp_d{b}", [IC, L]) for b in range(B)]
        self.a2ain_d = [dint(f"a2ain_d{b}", [NCORE * IC, TC], BF16) for b in range(B)]
        self.a2aout_d = [dint(f"a2aout_d{b}", [NCORE * IC, TC], BF16) for b in range(B)]
        self.cmb_d = [dint(f"cmb_d{b}", [E, TC]) for b in range(B)]
        self.x2_d = [dint(f"x2_d{b}", [TC, H]) for b in range(B)]
        self.h2b_d0 = dint("h2b_d0", [128, NHT * TC], BF16)
        self.rowsd = dint("rowsd", [1, 16384])

        if c.debug:
            def dout(name, shape):
                self.dbg[name] = nc.dram_tensor("dbg_" + name, list(shape), F32,
                                                kind="ExternalOutput").ap()
            dout("u", [B, IC, L]); dout("gate", [B, IC, L])
            dout("ssm", [B, R + 2 * N, L]); dout("dt", [B, IC, L])
            dout("y", [B, IC, L]); dout("x2", [B, TC, H])
            dout("h2", [B, H, TC]); dout("cmb", [B, E, TC])

        def sb(name, shape, dt=F32):
            return nc.alloc_sbuf_tensor(name, list(shape), dt)

        self.id_f32 = sb("id_f32", [P, P])
        self.onescol_bf = sb("onescol_bf", [P, 1], BF16)
        self.onesrow_f = sb("onesrow_f", [1, P])
        self.sa_bf = [sb(f"sa_bf{b}", [P, NHT], BF16) for b in range(B)]
        self.ca1_t = [sb(f"ca1_t{b}", [P, NHT]) for b in range(B)]
        self.A_sb = sb("A_sb", [P, NIT * N])
        self.D_sb = sb("D_sb", [P, NIT])
        self.dtb_sb = sb("dtb_sb", [P, NIT])
        self.convb_sb = sb("convb_sb", [P, NIT])
        self.cw_sb = sb("cw_sb", [P, NIT * K])
        self.diag = [[sb(f"diag_{it}_{k}", [P, P], BF16) for k in range(K)]
                     for it in range(NIT)]
        self.xpT_bf = [sb(f"xpT_bf{it}", [P, R + 2 * N], BF16) for it in range(NIT)]
        self.dtpT_bf = sb("dtpT_bf", [P, IC], BF16)
        self.gwT_sb = sb("gwT_sb", [P, NHT * E])
        self.sgwT_sb = sb("sgwT_sb", [P, NHT])
        self.h2rt = [sb(f"h2rt{j}", [P, P]) for j in range(2)]
        self.ubias = sb("ubias", [P, NIT])
        self.gbias = sb("gbias", [P, NIT])
        self.wsum = sb("wsum", [P, 2 * NIT])
        self.carry = sb("carry", [P, N * NIT])
        self.csT = sb("csT", [P, NHT * B])
        self.ts_bf = sb("ts_bf", [P, L], BF16)
        self.r_tm = sb("r_tm", [P, c.NTT * E])
        self.rt1 = sb("rt1", [P, c.NTT * E])
        self.scol = sb("scol", [P, 16])
        self.sgbc = sb("sgbc", [P, B * TC])
        self.cmbbc = sb("cmbbc", [P, B * TC])
        self.wst = [sb(f"wst{j}", [P, 1024], BF16) for j in range(2)]
        self.tev = sb("tev", [P, 512])
        self.hin = [sb(f"hin{it}", [P, 3 + c.ALC], BF16) for it in range(NIT)]
        self.rows = sb("rows", [1, 4096])
        self.AR1 = sb("AR1", [P, 8192])
        self.AR2 = sb("AR2", [P, 8192])
        self.AR3 = sb("AR3", [P, 8192])
        self.AR4 = sb("AR4", [P, 7808])
        self.AR5 = sb("AR5", [P, 3072])
        self.AR6 = sb("AR6", [P, 4096])
        self.csml = self.AR4[0:B, 0:H]
        self.modrow = self.AR4[0:B, H:H + c.MODSL]
        self.rsb = self.AR5[0:E, 2048:2048 + TC]
        self.cmbsb = self.rsb

        self.psA = nc.alloc_psum_tensor("psA", [P, 1024], F32)
        self.psB = nc.alloc_psum_tensor("psB", [P, 1024], F32)
        self.psC = nc.alloc_psum_tensor("psC", [P, 1024], F32)
        self.psD = nc.alloc_psum_tensor("psD", [P, 1024], F32)
        self.PSL = [self.psA[:, 0:512], self.psA[:, 512:1024],
                    self.psB[:, 0:512], self.psB[:, 512:1024],
                    self.psC[:, 0:512], self.psC[:, 512:1024],
                    self.psD[:, 0:512], self.psD[:, 512:1024]]

    def v32(self, ar, c0, n):
        return ar[:, c0:c0 + n]

    def vbf(self, ar, c0, nbf):
        return ar[:, c0:c0 + nbf // 2].bitcast(BF16)

    def bcast(self, dst, src_dram_row):
        import concourse.bass as _B
        _, a2 = _B.broadcast_tensor_aps(dst, src_dram_row)
        return a2

    def emit(self):
        c = self.c
        nc = self.nc
        with nc.Block() as block:
            self.block = block
            self.cd = Chain(nc, "cd", self.chains)
            self.cg = Chain(nc, "cg", self.chains)
            self.cgM = Chain(nc, "cgM", self.chains)
            self.cdM = Chain(nc, "cdM", self.chains)
            self.st = [Chain(nc, "st0", self.chains), Chain(nc, "st1", self.chains)]
            self.stg = [Chain(nc, f"stg{j}", self.chains) for j in range(4)]
            self.xs = Chain(nc, "xs", self.chains)
            self.spU = Chain(nc, "spU", self.chains)
            self.spG = Chain(nc, "spG", self.chains)
            self.ug = Chain(nc, "ug", self.chains)
            self.br = [Chain(nc, "br0", self.chains), Chain(nc, "br1", self.chains)]
            self.ssmW = Chain(nc, "ssmW", self.chains)
            self.ygW = Chain(nc, "ygW", self.chains)
            self.xq = [Chain(nc, "xq0", self.chains), Chain(nc, "xq1", self.chains)]
            self.ya = Chain(nc, "ya", self.chains)
            self.cr = [Chain(nc, "cr0", self.chains), Chain(nc, "cr1", self.chains)]
            self.outW = Chain(nc, "outW", self.chains)
            self.dbgc = Chain(nc, "dbgc", self.chains)
            self.dbg2 = Chain(nc, "dbg2", self.chains)
            self.dbgD = Chain(nc, "dbgD", self.chains)
            self.dbgY = Chain(nc, "dbgY", self.chains)
            self.cq = Chain(nc, "cq", self.chains)
            self.cv = Chain(nc, "cv", self.chains)
            self.ca = Chain(nc, "ca", self.chains)
            self.ct = Chain(nc, "ct", self.chains)
            self.cp = Chain(nc, "cp", self.chains)
            self.csc = Chain(nc, "csc", self.chains)
            self.rbc = Chain(nc, "rbc", self.chains)
            self.cx2 = Chain(nc, "cx2", self.chains)
            self.ch2 = Chain(nc, "ch2", self.chains)
            self.cbc = Chain(nc, "cbc", self.chains)
            self.ph_init()
            self.ph_mod()
            self.ph_wprep()
            for b in range(c.B):
                self.ph_A(b)
                self.ph_AR(b)
                self.ph_B(b)
                self.ph_C(b)
            self.ph_D()
            self.ph_final()
        return self.nc

    def ph_init(self):
        c, tok = self.c, self.tok
        cd, cg, cv, cp = self.cd, self.cg, self.cv, self.cp
        N, R, K, NIT, NHT, E = c.N, c.R, c.K, c.NIT, c.NHT, c.E
        self.xpT_f = self.v32(self.AR5, 0, NIT * (R + 2 * N))
        self.dtpT_f = self.v32(self.AR5, NIT * (R + 2 * N), c.IC)
        self.gw_f = self.v32(self.AR5, NIT * (R + 2 * N) + c.IC, NHT * E + NHT)

        def s_sync(e, m):
            cd.inc(e.dma_start(out=self.csml[0:c.B, :], in_=self.cvec_d), 16)
            for it in range(NIT):
                s = slice(it * P, (it + 1) * P)
                cd.inc(e.dma_start(out=self.A_sb[:, it * N:(it + 1) * N], in_=self.alog_d[s, :]), 16)
                cd.inc(e.dma_start(out=self.D_sb[:, it:it + 1], in_=self.dvec_d[s, :]), 16)
                cd.inc(e.dma_start(out=self.dtb_sb[:, it:it + 1], in_=self.dtb_d[s, :]), 16)
                cd.inc(e.dma_start(out=self.convb_sb[:, it:it + 1], in_=self.convb_d[s, :]), 16)
                cd.inc(e.dma_start(out=self.cw_sb[:, it * K:(it + 1) * K], in_=self.convw_d[s, :]), 16)
                cd.inc(e.dma_start(out=self.xpT_f[:, it * (R + 2 * N):(it + 1) * (R + 2 * N)],
                                   in_=self.xpT_d[s, :]), 16)
            cd.inc(e.dma_start(out=self.dtpT_f[:, :], in_=self.dtpT_d[0:P, :]), 16)
            for kt in range(NHT):
                s = slice(kt * P, (kt + 1) * P)
                cd.inc(e.dma_start(out=self.gw_f[:, kt * E:(kt + 1) * E], in_=self.gwT_d[s, :]), 16)
                cd.inc(e.dma_start(out=self.gw_f[:, NHT * E + kt:NHT * E + kt + 1],
                                   in_=self.sgwT_d[s, :]), 16)
            tok["init_in"] = cd.tok()
        self.on("sync", s_sync)

        def s_pre(e, m):
            t = cp.inc(e.memset(self.id_f32[:, :], 0.0))
            m.wait(t)
            cp.inc(e.affine_select(self.id_f32[:, :], self.id_f32[:, :], pattern=[[-1, P]],
                                   compare_op=ALU.not_equal, fill=1.0,
                                   base=0, channel_multiplier=1))
            tok["ident"] = cp.tok()
        self.on("gpsimd", s_pre)

        def s_v(e, m):
            cv.inc(e.memset(self.onescol_bf[:, :], 1.0))
            cv.inc(e.memset(self.onesrow_f[:, :], 1.0))
            tok["ones"] = cv.tok()
        self.on("vector", s_v)

    def ph_mod(self):
        c, tok = self.c, self.tok
        cd, cg, cq, cv, ca, ct = self.cd, self.cg, self.cq, self.cv, self.ca, self.ct
        B, H, NHT = c.B, c.H, c.NHT
        wmod_st = [self.v32(self.AR2, j * c.MODSL, c.MODSL) for j in range(2)]

        def s_a(e, m):
            m.wait(self.T("init_in"))
            tok["csil"] = ca.inc(e.activation(self.csml[0:B, :], self.csml[0:B, :], AF.Silu))
        self.on("scalar", s_a)

        def s_t(e, m):
            m.wait(self.T("csil")); m.wait(self.T("ident"))
            for kt in range(NHT):
                ct.inc(e.transpose(self.psA[0:P, kt * B:(kt + 1) * B],
                                   self.csml[0:B, kt * P:(kt + 1) * P], self.id_f32[0:B, 0:B]))
            tok["csT_ps"] = ct.tok()
        self.on("tensor", s_t)

        def s_v(e, m):
            m.wait(self.T("csT_ps"))
            tok["csT"] = cv.inc(e.tensor_copy(self.csT[:, :], self.psA[:, 0:NHT * B]))
        self.on("vector", s_v)

        nmj = (c.MODSL + 511) // 512
        for kt in range(NHT):
            def s_q(e, m, kt=kt):
                if kt >= 2:
                    m.wait(self.T(f"modmm{kt - 2}"))
                self.st[kt % 2].inc(e.dma_start(out=wmod_st[kt % 2][:, :],
                                                in_=self.wmod_d[kt * P:(kt + 1) * P, :]), 16)
                tok[f"wmodld{kt}"] = self.st[kt % 2].tok()
            self.on("scalar", s_q)

            def s_t2(e, m, kt=kt):
                if kt == 0:
                    m.wait(self.T("csT"))
                m.wait(self.T(f"wmodld{kt}"))
                for j in range(nmj):
                    j0, j1 = j * 512, min((j + 1) * 512, c.MODSL)
                    pj = [self.psB[0:B, 0:512], self.psB[0:B, 512:1024],
                          self.psC[0:B, 0:512], self.psC[0:B, 512:1024]][j]
                    ct.inc(e.matmul(pj[:, 0:j1 - j0],
                                    self.csT[:, kt * B:(kt + 1) * B],
                                    wmod_st[kt % 2][:, j0:j1],
                                    start=(kt == 0), stop=(kt == NHT - 1)))
                tok[f"modmm{kt}"] = ct.tok()
            self.on("tensor", s_t2)
        tok["mod_ps"] = None

        def s_v2(e, m):
            m.wait(self.T(f"modmm{NHT - 1}"))
            n1 = min(1024, c.MODSL)
            cv.inc(e.tensor_copy(self.modrow[0:B, 0:n1], self.psB[0:B, 0:n1]))
            if c.MODSL > 1024:
                cv.inc(e.tensor_copy(self.modrow[0:B, 1024:c.MODSL],
                                     self.psC[0:B, 0:c.MODSL - 1024]))
            tok["mod_sb"] = cv.tok()
        self.on("vector", s_v2)

        def s_s(e, m):
            m.wait(self.T("mod_sb"))
            self.cdM.inc(e.dma_start(out=self.modsl_d[:, :], in_=self.modrow[0:B, :]), 16)
            tok["modsl_w"] = self.cdM.tok()
        self.on("sync", s_s)

        def s_p(e, m):
            m.wait(self.T("modsl_w"))
            t = cg.inc(e.collective_compute(
                "AllGather", ALU.bypass, replica_groups=[list(range(NCORE))],
                ins=[self.modsl_d.ap().opt()], outs=[self.modg_d.ap().opt()]))
            m.wait(t)
            for r in range(NCORE):
                for b in range(B):
                    self.cgM.inc(e.dma_start(
                        out=self.modflat_d[b:b + 1, r * c.MODSL:(r + 1) * c.MODSL],
                        in_=self.modg_d[r * B + b:r * B + b + 1, :]), 16)
            tok["modflat"] = self.cgM.tok()
        self.on("gpsimd", s_p)

        def s_s2(e, m):
            m.wait(self.T("modflat"))
            for b in range(B):
                for kt in range(NHT):
                    self.cdM.inc(e.dma_start(
                        out=self.ca1_t[b][:, kt:kt + 1],
                        in_=self.modflat_d[b:b + 1, 1 * H + kt * P:1 * H + (kt + 1) * P]), 16)
                    self.cdM.inc(e.dma_start(
                        out=self.AR1[:, b * NHT + kt:b * NHT + kt + 1],
                        in_=self.modflat_d[b:b + 1, 0 * H + kt * P:0 * H + (kt + 1) * P]), 16)
            tok["modtiles"] = self.cdM.tok()
        self.on("sync", s_s2)

        def s_v3(e, m):
            m.wait(self.T("modtiles"))
            for b in range(B):
                cv.inc(e.tensor_scalar_add(self.ca1_t[b][:, :], self.ca1_t[b][:, :], 1.0))
                cv.inc(e.tensor_copy(self.sa_bf[b][:, :], self.AR1[:, b * NHT:(b + 1) * NHT]))
            tok["mod_done"] = cv.tok()
        self.on("vector", s_v3)

    def emit_clamp(self, b, off, scr, tok_in, tok_out):
        c, cv, ca, tok = self.c, self.cv, self.ca, self.tok
        H = c.H
        g = self.rows[0:1, off:off + H]
        S = self.rows[0:1, scr:scr + H]

        def s_sign(e, m):
            m.wait(tok_in() if callable(tok_in) else self.T(tok_in))
            tok[tok_out + "_sg"] = ca.inc(e.sign(S, g))
        self.on("scalar", s_sign)

        def s_v(e, m):
            m.wait(self.T(tok_out + "_sg"))
            t = cv.inc(e.tensor_mul(g, g, S))
            m.wait(t)
            t = cv.inc(e.tensor_scalar_max(g, g, 0.25))
            m.wait(t)
            tok[tok_out] = cv.inc(e.tensor_mul(g, g, S))
        self.on("vector", s_v)

    def ph_wprep(self):
        c, tok = self.c, self.tok
        cv, ca, cp = self.cv, self.ca, self.cp
        N, R, K, NIT, NHT, E = c.N, c.R, c.K, c.NIT, c.NHT, c.E

        def s_a(e, m):
            m.wait(self.T("init_in"))
            tok["A_exp"] = ca.inc(e.activation(self.A_sb[:, 0:NIT * N],
                                               self.A_sb[:, 0:NIT * N], AF.Exp))
        self.on("scalar", s_a)

        def s_v(e, m):
            m.wait(self.T("A_exp"))
            cv.inc(e.tensor_scalar_mul(self.A_sb[:, 0:NIT * N], self.A_sb[:, 0:NIT * N], -1.0))
            m.wait(self.T("init_in"))
            for it in range(NIT):
                cv.inc(e.tensor_copy(self.xpT_bf[it][:, :],
                                     self.xpT_f[:, it * (R + 2 * N):(it + 1) * (R + 2 * N)]))
            cv.inc(e.tensor_copy(self.dtpT_bf[:, :], self.dtpT_f[:, :]))
            cv.inc(e.tensor_copy(self.gwT_sb[:, :], self.gw_f[:, 0:NHT * E]))
            cv.inc(e.tensor_copy(self.sgwT_sb[:, :], self.gw_f[:, NHT * E:NHT * E + NHT]))
            m.wait(self.T("ones")); m.wait(self.T("ident"))
            for it in range(NIT):
                for k in range(K):
                    cv.inc(e.tensor_scalar(
                        self.diag[it][k][:, :], self.id_f32[:, :],
                        self.cw_sb[:, it * K + k:it * K + k + 1], None, ALU.mult))
            tok["diag"] = cv.tok()
        self.on("vector", s_v)

    def ph_A(self, b):
        c, tok = self.c, self.tok
        cd, cq, cv, ca, ct = self.cd, self.cq, self.cv, self.ca, self.ct
        B, L, H, N, R, K = c.B, c.L, c.H, c.N, c.R, c.K
        NIT, NHT, IC, ALC, NALC, NAT = c.NIT, c.NHT, c.IC, c.ALC, c.NALC, c.NAT
        psA, psB, psC, psD = self.psA, self.psB, self.psC, self.psD
        Wb = self.vbf(self.AR1, 0, NHT * 2 * IC)
        ipT_st = [self.v32(self.AR2, j * 2 * IC, 2 * IC) for j in range(2)]
        x_tm = self.v32(self.AR2, 0, NAT * H)
        xb_fm = self.vbf(self.AR6, 0, NHT * ALC)
        ssm_ts = self.v32(self.AR3, 0, L)
        ssm_bc = self.AR3[0:2 * N, L:2 * L]
        u_f = self.v32(self.AR4, 0, NIT * ALC)
        g_f = self.v32(self.AR4, NIT * ALC, NIT * ALC)
        u_bf = self.vbf(self.AR4, 2 * NIT * ALC, NIT * ALC)
        o = 2 * NIT * ALC + NIT * ALC // 2
        r_bc_s = [self.v32(self.AR4, o + j * 2 * ALC, ALC) for j in range(2)]
        nmur_bc_s = [self.v32(self.AR4, o + j * 2 * ALC + ALC, ALC) for j in range(2)]
        sq_bf = [self.vbf(self.AR5, 1536 + j * ALC // 2, ALC) for j in range(2)]
        self._ssm_views = (ssm_ts, ssm_bc)
        rows = self.rows
        RMU, RM2, RT, RR_, RNM = 0, ALC, 2 * ALC, 3 * ALC, 4 * ALC

        for kt in range(NHT):
            def s_ld(e, m, kt=kt):
                if kt == 0:
                    m.wait(self.T("mod_done"))
                    if b > 0:
                        m.wait(self.T(f"x2w{b - 1}_{c.NTT - 1}"))
                        m.wait(self.T("h2bw0"))
                        m.wait(self.T(f"cmbw{b - 1}"))
                        m.wait(self.T(f"sgbc{b - 1}"))
                        if c.debug:
                            m.wait((self.dbgc, self.dbgc.count))
                if kt >= 2:
                    m.wait(self.T(f"wbs{b}_{kt - 2}"))
                self.st[kt % 2].inc(e.dma_start(out=ipT_st[kt % 2][:, :],
                                                in_=self.ipT_d[kt * P:(kt + 1) * P, :]), 16)
                tok[f"ipld{b}_{kt}"] = self.st[kt % 2].tok()
            self.on("sync", s_ld)

            def s_sc(e, m, kt=kt):
                m.wait(self.T(f"ipld{b}_{kt}"))
                tok[f"wbs{b}_{kt}"] = cv.inc(e.tensor_scalar(
                    Wb[:, kt * 2 * IC:(kt + 1) * 2 * IC], ipT_st[kt % 2][:, :],
                    self.ca1_t[b][:, kt:kt + 1], None, ALU.mult))
            self.on("vector", s_sc)
        tok[f"Wb{b}"] = self.cv.tok()

        def s_t(e, m):
            m.wait(self.T(f"Wb{b}"))
            if b > 0:
                m.wait(self.T(f"cmbw{b - 1}"))
            for mt in range(2 * NIT):
                for kt in range(NHT):
                    ct.inc(e.matmul(psB[0:P, mt:mt + 1],
                                    Wb[:, kt * 2 * IC + mt * P:kt * 2 * IC + (mt + 1) * P],
                                    self.sa_bf[b][:, kt:kt + 1],
                                    start=(kt == 0), stop=(kt == NHT - 1)))
            for mt in range(2 * NIT):
                for kt in range(NHT):
                    ct.inc(e.matmul(psB[0:P, 512 + mt:512 + mt + 1],
                                    Wb[:, kt * 2 * IC + mt * P:kt * 2 * IC + (mt + 1) * P],
                                    self.onescol_bf[:, :],
                                    start=(kt == 0), stop=(kt == NHT - 1)))
            tok[f"bias_mm{b}"] = ct.tok()
        self.on("tensor", s_t)

        def s_v2(e, m):
            m.wait(self.T(f"bias_mm{b}"))
            cv.inc(e.tensor_copy(self.wsum[:, :], psB[:, 512:512 + 2 * NIT]))
            for it in range(NIT):
                t = cv.inc(e.reduce_sum(self.scol[:, 8 + it:9 + it],
                                        self.cw_sb[:, it * K:(it + 1) * K], axis=AXX))
                m.wait(t)
                cv.inc(e.scalar_tensor_tensor(
                    self.ubias[:, it:it + 1], psB[:, it:it + 1],
                    self.scol[:, 8 + it:9 + it], self.convb_sb[:, it:it + 1],
                    ALU.mult, ALU.add))
            tok[f"gb_ev{b}"] = cv.inc(e.tensor_copy(self.gbias[:, :], psB[:, NIT:2 * NIT]))
            for it in range(NIT):
                cv.inc(e.memset(self.hin[it][:, 0:3], 0.0))
            cv.inc(e.memset(self.carry[:, :], 0.0))
            tok[f"Aprep{b}"] = cv.tok()
        self.on("vector", s_v2)

        trps = [psA[:, 0:512], psA[:, 512:1024]]
        for lc in range(NALC):
            par = lc % 2
            lc0 = lc * ALC
            r_bc = r_bc_s[par]
            nmur_bc = nmur_bc_s[par]

            def s_xld(e, m, lc=lc, lc0=lc0):
                if lc == 0:
                    m.wait(self.T(f"Aprep{b}"))
                if lc >= 1:
                    m.wait(self.T(f"tr{b}_{lc - 1}"))
                for tt in range(NAT):
                    self.xs.inc(e.dma_start(out=x_tm[:, tt * H:(tt + 1) * H],
                                            in_=self.x_in[b, lc0 + tt * P:lc0 + (tt + 1) * P, :]), 16)
                tok[f"xld{b}_{lc}"] = self.xs.tok()
            self.on("sync", s_xld)

            def mk_st(lc, ht):
                def s_st(e, m, lc=lc, ht=ht):
                    if ht == 0:
                        if lc >= 1:
                            m.wait(self.T(f"var{b}_{lc - 1}"))
                            m.wait(self.T(f"xpev{b}_{lc - 1}"))
                    m.wait(self.T(f"xev{b}_{lc}_{ht}"))
                    ct.inc(e.matmul(psC[0:1, 0:ALC], self.onescol_bf[:, :],
                                    xb_fm[:, ht * ALC:(ht + 1) * ALC],
                                    start=(ht == 0), stop=(ht == NHT - 1)))
                    m.wait(self.T(f"sq{b}_{lc}_{ht}"))
                    ct.inc(e.matmul(psC[0:1, 512:512 + ALC], self.onescol_bf[:, :],
                                    sq_bf[ht % 2][:, :],
                                    start=(ht == 0), stop=(ht == NHT - 1)))
                    tok[f"sqmm{b}_{lc}_{ht}"] = ct.tok()
                self.on("tensor", s_st)

            for ht in range(NHT):
                g = lc * NHT + ht
                trp = trps[g % 2]

                def s_tr(e, m, lc=lc, ht=ht, trp=trp, g=g):
                    if ht == 0:
                        m.wait(self.T(f"xld{b}_{lc}"))
                    if g >= 2:
                        pl, ph = divmod(g - 2, NHT)
                        m.wait(self.T(f"xev{b}_{pl}_{ph}"))
                    for tt in range(NAT):
                        ct.inc(e.transpose(trp[:, tt * P:(tt + 1) * P],
                                           x_tm[:, tt * H + ht * P:tt * H + (ht + 1) * P],
                                           self.id_f32[:, :]))
                    tok[f"tr{b}_{lc}_{ht}"] = ct.tok()
                self.on("tensor", s_tr)

                def s_xev(e, m, lc=lc, ht=ht, trp=trp):
                    m.wait(self.T(f"tr{b}_{lc}_{ht}"))
                    if lc >= 1 and ht == 0:
                        m.wait(self.T(f"ipmm{b}_{lc - 1}_{2 * NIT - 1}"))
                    tok[f"xev{b}_{lc}_{ht}"] = ca.inc(e.activation(
                        xb_fm[:, ht * ALC:(ht + 1) * ALC], trp[:, 0:ALC], AF.Copy))
                    if ht >= 2:
                        m.wait(self.T(f"sqmm{b}_{lc}_{ht - 2}"))
                    m.wait(self.T(f"xev{b}_{lc}_{ht}"))
                    tok[f"sq{b}_{lc}_{ht}"] = ca.inc(e.activation(
                        sq_bf[ht % 2][:, :], xb_fm[:, ht * ALC:(ht + 1) * ALC], AF.Square))
                self.on("scalar", s_xev)

                if ht >= 1:
                    mk_st(lc, ht - 1)
            mk_st(lc, NHT - 1)
            tok[f"tr{b}_{lc}"] = self.ct.tok()
            tok[f"statmm{b}_{lc}"] = self.ct.tok()

            def s_row(e, m, lc=lc):
                m.wait(self.T(f"statmm{b}_{lc}"))
                cv.inc(e.tensor_scalar_mul(rows[0:1, RMU:RMU + ALC], psC[0:1, 0:ALC], 1.0 / H))
                t = cv.inc(e.tensor_scalar_mul(rows[0:1, RM2:RM2 + ALC],
                                               psC[0:1, 512:512 + ALC], 1.0 / H))
                m.wait(t)
                t = cv.inc(e.tensor_mul(rows[0:1, RT:RT + ALC],
                                        rows[0:1, RMU:RMU + ALC], rows[0:1, RMU:RMU + ALC]))
                m.wait(t)
                t = cv.inc(e.tensor_sub(
                    rows[0:1, RM2:RM2 + ALC], rows[0:1, RM2:RM2 + ALC], rows[0:1, RT:RT + ALC]))
                m.wait(t)
                tok[f"var{b}_{lc}"] = cv.inc(e.tensor_scalar_add(
                    rows[0:1, RM2:RM2 + ALC], rows[0:1, RM2:RM2 + ALC], 1e-5))
            self.on("vector", s_row)

            def s_ln(e, m, lc=lc):
                m.wait(self.T(f"var{b}_{lc}"))
                tok[f"lnv{b}_{lc}"] = ca.inc(e.activation(
                    rows[0:1, RT:RT + ALC], rows[0:1, RM2:RM2 + ALC], AF.Ln))
            self.on("scalar", s_ln)

            def s_row2(e, m, lc=lc):
                m.wait(self.T(f"lnv{b}_{lc}"))
                tok[f"nl{b}_{lc}"] = cv.inc(e.tensor_scalar_mul(
                    rows[0:1, RT:RT + ALC], rows[0:1, RT:RT + ALC], -0.5))
            self.on("vector", s_row2)

            def s_ex(e, m, lc=lc):
                m.wait(self.T(f"nl{b}_{lc}"))
                tok[f"rstd{b}_{lc}"] = ca.inc(e.activation(
                    rows[0:1, RR_:RR_ + ALC], rows[0:1, RT:RT + ALC], AF.Exp))
            self.on("scalar", s_ex)

            def s_row3(e, m, lc=lc):
                m.wait(self.T(f"rstd{b}_{lc}"))
                t = cv.inc(e.tensor_mul(rows[0:1, RNM:RNM + ALC],
                                        rows[0:1, RMU:RMU + ALC], rows[0:1, RR_:RR_ + ALC]))
                m.wait(t)
                tok[f"nmur{b}_{lc}"] = cv.inc(e.tensor_scalar_mul(
                    rows[0:1, RNM:RNM + ALC], rows[0:1, RNM:RNM + ALC], -1.0))
            self.on("vector", s_row3)

            RD0 = 8192 // 2 + (lc % 2) * 2 * ALC
            def s_rbc(e, m, lc=lc, RD0=RD0):
                m.wait(self.T(f"rstd{b}_{lc}"))
                if lc >= 2:
                    m.wait(self.T(f"ipev{b}_{lc - 2}_{2 * NIT - 1}"))
                self.rbc.inc(e.dma_start(out=self.rowsd[0:1, RD0:RD0 + ALC],
                                         in_=rows[0:1, RR_:RR_ + ALC]), 16)
                m.wait(self.T(f"nmur{b}_{lc}"))
                self.rbc.inc(e.dma_start(out=self.rowsd[0:1, RD0 + ALC:RD0 + 2 * ALC],
                                         in_=rows[0:1, RNM:RNM + ALC]), 16)
                t = (self.rbc, self.rbc.count)
                m.wait(t)
                self.rbc.inc(e.dma_start(out=r_bc[:, :], in_=self.bcast(
                    r_bc[:, :], self.rowsd[0:1, RD0:RD0 + ALC])), 16)
                self.rbc.inc(e.dma_start(out=nmur_bc[:, :], in_=self.bcast(
                    nmur_bc[:, :], self.rowsd[0:1, RD0 + ALC:RD0 + 2 * ALC])), 16)
                tok[f"rnbc{b}_{lc}"] = self.rbc.tok()
            self.on("sync", s_rbc)

            for mt in range(2 * NIT):
                pp = psB[:, 0:512] if mt % 2 == 0 else psB[:, 512:1024]

                def s_ip(e, m, lc=lc, mt=mt, pp=pp):
                    if lc == 0 and mt == 0:
                        m.wait(self.T(f"bias_mm{b}"))
                    if mt >= 2:
                        m.wait(self.T(f"ipev{b}_{lc}_{mt - 2}"))
                    elif lc >= 1:
                        m.wait(self.T(f"ipev{b}_{lc - 1}_{2 * NIT - 2 + mt}"))
                    for kt in range(NHT):
                        ct.inc(e.matmul(pp[:, 0:ALC],
                                        Wb[:, kt * 2 * IC + mt * P:kt * 2 * IC + (mt + 1) * P],
                                        xb_fm[:, kt * ALC:(kt + 1) * ALC],
                                        start=(kt == 0), stop=(kt == NHT - 1)))
                    tok[f"ipmm{b}_{lc}_{mt}"] = ct.tok()
                self.on("tensor", s_ip)

                def s_ipe(e, m, lc=lc, mt=mt, pp=pp, r_bc=r_bc, nmur_bc=nmur_bc):
                    m.wait(self.T(f"ipmm{b}_{lc}_{mt}"))
                    m.wait(self.T(f"rnbc{b}_{lc}"))
                    if mt >= 1:
                        m.wait(self.T(f"ipev{b}_{lc}_{mt - 1}"))
                    elif lc >= 1:
                        m.wait(self.T(f"ipev{b}_{lc - 1}_{2 * NIT - 1}"))
                    t = cv.inc(e.tensor_mul(self.tev[:, 0:ALC], pp[:, 0:ALC], r_bc[:, :]))
                    m.wait(t)
                    if mt < NIT:
                        it = mt
                        if lc >= 1:
                            m.wait(self.T(f"usil{b}_{lc - 1}_{it}"))
                            m.wait(self.T(f"halo{b}_{lc - 1}_{it}"))
                        tok[f"ipev{b}_{lc}_{mt}"] = cv.inc(e.scalar_tensor_tensor(
                            self.hin[it][:, 3:3 + ALC], nmur_bc[:, :],
                            self.wsum[:, mt:mt + 1], self.tev[:, 0:ALC],
                            ALU.mult, ALU.add))
                    else:
                        it = mt - NIT
                        if lc >= 1:
                            m.wait(self.T(f"gs{b}_{lc - 1}_{it}"))
                        if lc >= 1:
                            m.wait(self.T(f"gspill{b}_{lc - 1}"))
                        tok[f"ipev{b}_{lc}_{mt}"] = cv.inc(e.scalar_tensor_tensor(
                            g_f[:, it * ALC:(it + 1) * ALC], nmur_bc[:, :],
                            self.wsum[:, mt:mt + 1], self.tev[:, 0:ALC],
                            ALU.mult, ALU.add))
                        tok[f"gpre{b}_{lc}_{it}"] = tok[f"ipev{b}_{lc}_{mt}"]
                self.on("vector", s_ipe)

                if mt >= NIT:
                    it = mt - NIT

                    def s_gs(e, m, lc=lc, it=it):
                        if lc == 0 and it == 0:
                            m.wait(self.T(f"gb_ev{b}"))
                        m.wait(self.T(f"gpre{b}_{lc}_{it}"))
                        tok[f"gs{b}_{lc}_{it}"] = ca.inc(e.activation(
                            g_f[:, it * ALC:(it + 1) * ALC],
                            g_f[:, it * ALC:(it + 1) * ALC],
                            AF.Silu, bias=self.gbias[:, it:it + 1]))
                    self.on("scalar", s_gs)

            for it in range(NIT):
                gc = lc * NIT + it
                cpp = psD[:, 0:ALC] if gc % 2 == 0 else psD[:, 512:512 + ALC]

                def s_cv(e, m, lc=lc, it=it, gc=gc, cpp=cpp):
                    if lc == 0 and it == 0:
                        m.wait(self.T("diag"))
                    m.wait(self.T(f"ipev{b}_{lc}_{it}"))
                    if gc >= 2:
                        pl, pi = divmod(gc - 2, NIT)
                        m.wait(self.T(f"uev{b}_{pl}_{pi}"))
                    if gc % 2 == 1 and lc >= 1 and (it == 1 or NIT == 1):
                        m.wait(self.T(f"xpev{b}_{lc - 1}"))
                    for k in range(K):
                        ct.inc(e.matmul(cpp, self.diag[it][k][:, :],
                                        self.hin[it][:, k:k + ALC],
                                        start=(k == 0), stop=(k == K - 1)))
                    tok[f"conv{b}_{lc}_{it}"] = ct.tok()
                self.on("tensor", s_cv)

                def s_us(e, m, lc=lc, it=it, cpp=cpp):
                    m.wait(self.T(f"conv{b}_{lc}_{it}"))
                    if lc >= 1 and it == 0:
                        m.wait(self.T(f"uspill{b}_{lc - 1}"))
                    t = ca.inc(e.activation(u_f[:, it * ALC:(it + 1) * ALC],
                                            cpp, AF.Silu,
                                            bias=self.ubias[:, it:it + 1]))
                    tok[f"uev{b}_{lc}_{it}"] = t
                    tok[f"usil{b}_{lc}_{it}"] = t
                self.on("scalar", s_us)

                def s_ub(e, m, lc=lc, it=it):
                    m.wait(self.T(f"uev{b}_{lc}_{it}"))
                    tok[f"ubf{b}_{lc}_{it}"] = cv.inc(e.tensor_copy(
                        u_bf[:, it * ALC:(it + 1) * ALC], u_f[:, it * ALC:(it + 1) * ALC]))
                    tok[f"halo{b}_{lc}_{it}"] = cv.inc(e.tensor_copy(
                        self.hin[it][:, 0:3], self.hin[it][:, ALC:ALC + 3]))
                self.on("vector", s_ub)

            def s_xp(e, m, lc=lc):
                if lc >= 1:
                    m.wait(self.T(f"xpev{b}_{lc - 1}"))
                m.wait(self.T(f"var{b}_{lc}"))
                for it in range(NIT):
                    m.wait(self.T(f"ubf{b}_{lc}_{it}"))
                    m.wait(self.T(f"uev{b}_{lc}_{it}"))
                for it in range(NIT):
                    ct.inc(e.matmul(psD[:, 512:512 + ALC], self.xpT_bf[it][:, 0:R],
                                    u_bf[:, it * ALC:(it + 1) * ALC],
                                    start=(it == 0), stop=(it == NIT - 1)))
                for it in range(NIT):
                    ct.inc(e.matmul(psC[0:2 * N, 512:512 + ALC],
                                    self.xpT_bf[it][:, R:R + 2 * N],
                                    u_bf[:, it * ALC:(it + 1) * ALC],
                                    start=(it == 0), stop=(it == NIT - 1)))
                tok[f"xpmm{b}_{lc}"] = ct.tok()
            self.on("tensor", s_xp)


            def s_xpe(e, m, lc=lc, lc0=lc0):
                m.wait(self.T(f"xpmm{b}_{lc}"))
                cv.inc(e.tensor_copy(ssm_ts[:, lc0:lc0 + ALC], psD[:, 512:512 + ALC]))
                tok[f"xpev{b}_{lc}"] = cv.inc(e.tensor_copy(
                    ssm_bc[:, lc0:lc0 + ALC], psC[0:2 * N, 512:512 + ALC]))
            self.on("vector", s_xpe)

            def s_sp(e, m, lc=lc, lc0=lc0):
                for it in range(NIT):
                    m.wait(self.T(f"uev{b}_{lc}_{it}"))
                    self.spU.inc(e.dma_start(
                        out=self.usp_d[b][it * P:(it + 1) * P, lc0:lc0 + ALC],
                        in_=u_f[:, it * ALC:(it + 1) * ALC]), 16)
                    if c.debug:
                        self.spU.inc(e.dma_start(
                            out=self.dbg["u"][b, it * P:(it + 1) * P, lc0:lc0 + ALC],
                            in_=u_f[:, it * ALC:(it + 1) * ALC]), 16)
                tok[f"uspill{b}_{lc}"] = self.spU.tok()
                for it in range(NIT):
                    m.wait(self.T(f"gs{b}_{lc}_{it}"))
                    self.spG.inc(e.dma_start(
                        out=self.gsp_d[b][it * P:(it + 1) * P, lc0:lc0 + ALC],
                        in_=g_f[:, it * ALC:(it + 1) * ALC]), 16)
                    if c.debug:
                        self.spG.inc(e.dma_start(
                            out=self.dbg["gate"][b, it * P:(it + 1) * P, lc0:lc0 + ALC],
                            in_=g_f[:, it * ALC:(it + 1) * ALC]), 16)
                tok[f"gspill{b}_{lc}"] = self.spG.tok()
            self.on("sync", s_sp)

    def ph_AR(self, b):
        c, tok = self.c, self.tok
        cd, cg, cv = self.cd, self.cg, self.cv
        N, R, L = c.N, c.R, c.L
        ssm_ts, ssm_bc = self._ssm_views

        def s_s(e, m):
            m.wait(self.T(f"xpev{b}_{c.NALC - 1}"))
            self.ssmW.inc(e.dma_start(out=self.ssmp_d[b][0:R, :], in_=ssm_ts[:, :]), 16)
            self.ssmW.inc(e.dma_start(out=self.ssmp_d[b][R:R + 2 * N, :], in_=ssm_bc[:, :]), 16)
            tok[f"ssmp_w{b}"] = self.ssmW.tok()
        self.on("sync", s_s)

        def s_p(e, m):
            m.wait(self.T(f"ssmp_w{b}"))
            tok[f"AR{b}"] = cg.inc(e.collective_compute(
                "AllReduce", ALU.add, replica_groups=[list(range(NCORE))],
                ins=[self.ssmp_d[b].ap().opt()], outs=[self.ssm_d[b].ap().opt()]))
        self.on("gpsimd", s_p)

        def s_s2(e, m):
            m.wait(self.T(f"AR{b}"))
            self.ssmW.inc(e.dma_start(out=ssm_ts[:, :], in_=self.ssm_d[b][0:R, :]), 16)
            tok[f"ssm_ld{b}"] = self.ssmW.tok()
            if c.debug:
                self.dbgc.inc(e.dma_start(out=self.dbg["ssm"][b, :, :], in_=self.ssm_d[b][:, :]), 16)
        self.on("sync", s_s2)

        def s_v(e, m):
            m.wait(self.T(f"ssm_ld{b}"))
            tok[f"tsbf{b}"] = cv.inc(e.tensor_copy(self.ts_bf[:, :], ssm_ts[:, :]))
        self.on("vector", s_v)

    def ph_B(self, b):
        c, tok = self.c, self.tok
        cd, cq, cv, ca, ct, cg, cp = self.cd, self.cq, self.cv, self.ca, self.ct, self.cg, self.cp
        N, R, L, NIT, IC, SLC, NSLC, TC = c.N, c.R, c.L, c.NIT, c.IC, c.SLC, c.NSLC, c.TC
        psA, psB, psC, psD = self.psA, self.psB, self.psC, self.psD
        dt_ch = self.v32(self.AR2, 0, NIT * SLC)
        dtu_ch = self.v32(self.AR2, NIT * SLC, NIT * SLC)
        y_ch = self.v32(self.AR2, 2 * NIT * SLC, NIT * SLC)
        u_ch = self.v32(self.AR3, 0, NIT * SLC)
        g_ch = self.v32(self.AR3, NIT * SLC, NIT * SLC)
        Bbc = [self.v32(self.AR3, 2 * NIT * SLC + j * SLC, SLC) for j in range(2)]
        Cbc = [self.v32(self.AR3, 2 * NIT * SLC + 2 * SLC + j * SLC, SLC) for j in range(2)]
        h_t = self.v32(self.AR3, 2 * NIT * SLC + 4 * SLC, SLC)
        ytmp = self.v32(self.AR3, 2 * NIT * SLC + 5 * SLC, SLC)
        dA_t = [self.v32(self.AR5, j * SLC, SLC) for j in range(2)]
        dBx_t = [self.v32(self.AR5, (2 + j) * SLC, SLC) for j in range(2)]
        yg_ch = self.vbf(self.AR5, 4 * SLC, NIT * SLC)
        csc = self.csc
        cscbase = csc.count

        def scw(slc, n, it):
            return (csc, cscbase + slc * N * NIT + n * NIT + it + 1)

        for slc in range(NSLC):
            l0 = slc * SLC

            def s_ld(e, m, slc=slc, l0=l0):
                if slc == 0:
                    m.wait(self.T(f"gspill{b}_{c.NALC - 1}"))
                    m.wait(self.T(f"uspill{b}_{c.NALC - 1}"))
                    m.wait(self.T(f"tsbf{b}"))
                if slc >= 1:
                    m.wait(self.T(f"ygv{b}_{slc - 1}"))
                for it in range(NIT):
                    self.ug.inc(e.dma_start(out=u_ch[:, it * SLC:(it + 1) * SLC],
                                            in_=self.usp_d[b][it * P:(it + 1) * P, l0:l0 + SLC]), 16)
                    self.ug.inc(e.dma_start(out=g_ch[:, it * SLC:(it + 1) * SLC],
                                            in_=self.gsp_d[b][it * P:(it + 1) * P, l0:l0 + SLC]), 16)
                tok[f"ugld{b}_{slc}"] = self.ug.tok()
            self.on("sync", s_ld)

            for it in range(NIT):
                jg0 = slc * NIT + it
                pp = psB[:, 0:512] if jg0 % 2 == 0 else psB[:, 512:1024]

                def s_dm(e, m, slc=slc, l0=l0, it=it, pp=pp):
                    if slc == 0 and it == 0:
                        m.wait(self.T(f"tsbf{b}"))
                    jg = slc * NIT + it
                    if jg >= 2:
                        ps_, pi_ = divmod(jg - 2, NIT)
                        m.wait(self.T(f"dte{b}_{ps_}_{pi_}"))
                    ct.inc(e.matmul(pp[:, 0:SLC],
                                    self.dtpT_bf[:, it * P:(it + 1) * P],
                                    self.ts_bf[:, l0:l0 + SLC],
                                    start=True, stop=True))
                    tok[f"dtmm{b}_{slc}_{it}"] = ct.tok()
                self.on("tensor", s_dm)

                def s_de(e, m, slc=slc, it=it, pp=pp):
                    m.wait(self.T(f"dtmm{b}_{slc}_{it}"))
                    if slc >= 1 and it == 0:
                        m.wait(self.T(f"dtln{b}_{slc - 1}"))
                        m.wait(scw(slc - 1, N - 1, NIT - 1))
                        if c.debug:
                            m.wait(self.T(f"dtdump{b}_{slc - 1}"))
                    tok[f"dte{b}_{slc}_{it}"] = cv.inc(e.tensor_scalar(
                        dt_ch[:, it * SLC:(it + 1) * SLC],
                        pp[:, 0:SLC], self.dtb_sb[:, it:it + 1], 85.0, ALU.add, ALU.min))
                self.on("vector", s_de)
            tok[f"dtev{b}_{slc}"] = self.cv.tok()

            def s_sp(e, m, slc=slc):
                m.wait(self.T(f"dtev{b}_{slc}"))
                if slc >= 1:
                    m.wait(scw(slc - 1, N - 1, NIT - 1))
                t = ca.inc(e.activation(dtu_ch[:, :], dt_ch[:, :], AF.Exp))
                m.wait(t)
                tok[f"dtln{b}_{slc}"] = ca.inc(e.activation(dt_ch[:, :], dtu_ch[:, :],
                                                            AF.Ln, bias=1.0))
            self.on("scalar", s_sp)

            def s_du(e, m, slc=slc):
                m.wait(self.T(f"dtln{b}_{slc}")); m.wait(self.T(f"ugld{b}_{slc}"))
                tok[f"dtu{b}_{slc}"] = cv.inc(e.tensor_mul(dtu_ch[:, :], dt_ch[:, :], u_ch[:, :]))
            self.on("vector", s_du)

            if c.debug:
                def s_dd(e, m, slc=slc, l0=l0):
                    m.wait(self.T(f"dtln{b}_{slc}"))
                    for it in range(NIT):
                        self.dbgD.inc(e.dma_start(
                            out=self.dbg["dt"][b, it * P:(it + 1) * P, l0:l0 + SLC],
                            in_=dt_ch[:, it * SLC:(it + 1) * SLC]), 16)
                    tok[f"dtdump{b}_{slc}"] = self.dbgD.tok()
                self.on("sync", s_dd)

            def s_bc(e, m, slc=slc, l0=l0):
                if slc == 0:
                    m.wait(self.T(f"AR{b}"))
                for n in range(N):
                    if n >= 2:
                        m.wait(scw(slc, n - 2, NIT - 1))
                    elif slc >= 1:
                        m.wait(scw(slc - 1, N - 2 + n, NIT - 1))
                    elif b > 0 and n < 2:
                        m.wait((csc, cscbase))
                    self.br[n % 2].inc(e.dma_start(
                        out=Bbc[n % 2][:, :],
                        in_=self.bcast(Bbc[n % 2][:, :],
                                       self.ssm_d[b][R + n:R + n + 1, l0:l0 + SLC])), 16)
                    self.br[n % 2].inc(e.dma_start(
                        out=Cbc[n % 2][:, :],
                        in_=self.bcast(Cbc[n % 2][:, :],
                                       self.ssm_d[b][R + N + n:R + N + n + 1, l0:l0 + SLC])), 16)
                    tok[f"bcd{b}_{slc}_{n}"] = (self.br[n % 2], self.br[n % 2].count)
            self.on("sync", s_bc)

            def s_dbx(e, m, slc=slc):
                m.wait(self.T(f"dtu{b}_{slc}"))
                for n in range(N):
                    m.wait(self.T(f"bcd{b}_{slc}_{n}"))
                    for it in range(NIT):
                        j = n * NIT + it
                        if j >= 2:
                            m.wait(scw(slc, (j - 2) // NIT, (j - 2) % NIT))
                        elif slc >= 1:
                            m.wait(scw(slc - 1, (N * NIT - 2 + j) // NIT, (N * NIT - 2 + j) % NIT))
                        tok[f"dbx{b}_{slc}_{n}_{it}"] = cp.inc(e.tensor_mul(
                            dBx_t[j % 2][:, :], dtu_ch[:, it * SLC:(it + 1) * SLC],
                            Bbc[n % 2][:, :]))
            self.on("gpsimd", s_dbx)

            def s_dA(e, m, slc=slc):
                m.wait(self.T(f"dtln{b}_{slc}"))
                for n in range(N):
                    for it in range(NIT):
                        j = n * NIT + it
                        if j >= 2:
                            m.wait(scw(slc, (j - 2) // NIT, (j - 2) % NIT))
                        elif slc >= 1:
                            m.wait(scw(slc - 1, (N * NIT - 2 + j) // NIT, (N * NIT - 2 + j) % NIT))
                        tok[f"dA{b}_{slc}_{n}_{it}"] = ca.inc(e.activation(
                            dA_t[j % 2][:, :], dt_ch[:, it * SLC:(it + 1) * SLC], AF.Exp,
                            scale=self.A_sb[:, it * N + n:it * N + n + 1]))
            self.on("scalar", s_dA)

            def s_sc(e, m, slc=slc):
                if c.debug and slc >= 1:
                    m.wait(self.T(f"ydump{b}_{slc - 1}"))
                for n in range(N):
                    for it in range(NIT):
                        j = n * NIT + it
                        m.wait(self.T(f"dbx{b}_{slc}_{n}_{it}"))
                        m.wait(self.T(f"dA{b}_{slc}_{n}_{it}"))
                        if j >= 1:
                            m.wait(self.T(f"scanh{b}_{slc}_{(j - 1) // NIT}_{(j - 1) % NIT}"))
                        elif slc >= 1:
                            m.wait(self.T(f"scanh{b}_{slc - 1}_{N - 1}_{NIT - 1}"))
                        t = cv.inc(e.tensor_tensor_scan(
                            h_t[:, :], dA_t[j % 2][:, :], dBx_t[j % 2][:, :],
                            self.carry[:, n * NIT + it:n * NIT + it + 1],
                            ALU.mult, ALU.add))
                        m.wait(t)
                        cv.inc(e.tensor_copy(self.carry[:, n * NIT + it:n * NIT + it + 1],
                                             h_t[:, SLC - 1:SLC]))
                        if n == 0:
                            ins = e.tensor_mul(y_ch[:, it * SLC:(it + 1) * SLC],
                                               h_t[:, :], Cbc[n % 2][:, :])
                        else:
                            t2 = cv.inc(e.tensor_mul(ytmp[:, :], h_t[:, :], Cbc[n % 2][:, :]))
                            m.wait(t2)
                            ins = e.tensor_add(y_ch[:, it * SLC:(it + 1) * SLC],
                                               y_ch[:, it * SLC:(it + 1) * SLC],
                                               ytmp[:, :])
                        tok[f"scanh{b}_{slc}_{n}_{it}"] = csc.inc(ins)
                m.wait(self.T(f"scanh{b}_{slc}_{N - 1}_{NIT - 1}"))
                for it in range(NIT):
                    t = cv.inc(e.scalar_tensor_tensor(
                        y_ch[:, it * SLC:(it + 1) * SLC], u_ch[:, it * SLC:(it + 1) * SLC],
                        self.D_sb[:, it:it + 1], y_ch[:, it * SLC:(it + 1) * SLC],
                        ALU.mult, ALU.add))
                    m.wait(t)
                    if slc >= 1 and it == 0:
                        m.wait(self.T(f"ygw{b}_{slc - 1}"))
                    cv.inc(e.tensor_mul(yg_ch[:, it * SLC:(it + 1) * SLC],
                                        y_ch[:, it * SLC:(it + 1) * SLC],
                                        g_ch[:, it * SLC:(it + 1) * SLC]))
                tok[f"ygv{b}_{slc}"] = cv.tok()
            self.on("vector", s_sc)

            def s_yw(e, m, slc=slc, l0=l0):
                m.wait(self.T(f"ygv{b}_{slc}"))
                for q in range(max(1, SLC // TC)):
                    rank = (l0 + q * TC) // TC
                    q0 = (l0 + q * TC) % TC
                    w = min(SLC, TC)
                    for it in range(NIT):
                        self.ygW.inc(e.dma_start(
                            out=self.a2ain_d[b][rank * IC + it * P:rank * IC + (it + 1) * P,
                                                q0:q0 + w],
                            in_=yg_ch[:, it * SLC + q * w:it * SLC + q * w + w]), 16)
                if c.debug:
                    for it in range(NIT):
                        self.dbgY.inc(e.dma_start(
                            out=self.dbg["y"][b, it * P:(it + 1) * P, l0:l0 + SLC],
                            in_=y_ch[:, it * SLC:(it + 1) * SLC]), 16)
                    tok[f"ydump{b}_{slc}"] = self.dbgY.tok()
                tok[f"ygw{b}_{slc}"] = self.ygW.tok()
            self.on("sync", s_yw)

        tok[f"yg_w{b}"] = self.ygW.tok()
        tok[f"dbgB{b}"] = self.dbgc.tok()
        tok[f"dbgBD{b}"] = self.dbgD.tok()
        tok[f"dbgBY{b}"] = self.dbgY.tok()

        def s_a2a(e, m):
            m.wait(self.T(f"yg_w{b}"))
            with allow_ici_all_to_all():
                tok[f"A2A{b}"] = cg.inc(e.collective_compute(
                    "AllToAll", ALU.bypass, replica_groups=[list(range(NCORE))],
                    ins=[self.a2ain_d[b].ap().opt()], outs=[self.a2aout_d[b].ap().opt()]))
        self.on("gpsimd", s_a2a)

    def ph_C(self, b):
        c, tok = self.c, self.tok
        cd, cq, cv, ca, ct = self.cd, self.cq, self.cv, self.ca, self.ct
        H, N, L, IC, TC, NHT, NTT, E = c.H, c.N, c.L, c.IC, c.TC, c.NHT, c.NTT, c.E
        NKI = c.I // P
        psA, psB, psC, psD = self.psA, self.psB, self.psC, self.psD
        PSL = self.PSL
        y_all = self.vbf(self.AR1, 0, NKI * TC)
        x2_tm = self.v32(self.AR2, 0, NTT * H)
        x_q = [self.v32(self.AR5, j * 1024, 1024) for j in range(2)]
        gabc = self.v32(self.AR3, 0, H)
        cbbc = self.v32(self.AR3, H, H)
        sbbc = self.v32(self.AR3, 2 * H, H)
        xsq = self.v32(self.AR3, 3 * H, H)
        h2_tm = self.v32(self.AR4, 0, H)
        h2b = self.vbf(self.AR6, 0, NHT * TC)
        h2b3 = h2b.rearrange("p (ht t) -> p ht t", t=TC)
        rows = self.rows
        R0, RS = 0, 2048

        def s_yld(e, m):
            m.wait(self.T(f"A2A{b}"))
            for kt in range(NKI):
                self.ya.inc(e.dma_start(out=y_all[:, kt * TC:(kt + 1) * TC],
                                        in_=self.a2aout_d[b][kt * P:(kt + 1) * P, :]), 16)
            tok[f"yall{b}"] = self.ya.tok()
        self.on("sync", s_yld)

        def s_r1(e, m):
            m.wait(self.T("modflat"))
            m.wait(self.T(f"ygw{b}_{c.NSLC - 1}"))
            if b > 0:
                m.wait(self.T(f"cbbc{b - 1}"))
            cd.inc(e.dma_start(out=rows[0:1, R0:R0 + H],
                               in_=self.modflat_d[b:b + 1, 2 * H:3 * H]), 16)
            tok[f"garow{b}"] = cd.tok()
        self.on("sync", s_r1)

        self.emit_clamp(b, R0, RS, f"garow{b}", f"gacl{b}")

        def s_gab(e, m):
            m.wait(self.T(f"gacl{b}"))
            m.wait((self.cbc, self.cbc.count))
            self.cbc.inc(e.dma_start(out=self.rowsd[0:1, 0:H], in_=rows[0:1, R0:R0 + H]), 16)
            t = (self.cbc, self.cbc.count)
            m.wait(t)
            self.cbc.inc(e.dma_start(out=gabc[:, :], in_=self.bcast(
                gabc[:, :], self.rowsd[0:1, 0:H])), 16)
            self.cbc.inc(e.dma_start(out=sbbc[:, :], in_=self.bcast(
                sbbc[:, :], self.modflat_d[b:b + 1, 3 * H:4 * H])), 16)
            tok[f"gabc{b}"] = self.cbc.tok()
        self.on("sync", s_gab)

        def s_cbr(e, m):
            m.wait(self.T(f"gabc{b}"))
            cd.inc(e.dma_start(out=rows[0:1, R0:R0 + H],
                               in_=self.modflat_d[b:b + 1, 4 * H:5 * H]), 16)
            tok[f"cbrow{b}"] = cd.tok()
        self.on("sync", s_cbr)

        def s_cb1(e, m):
            m.wait(self.T(f"cbrow{b}"))
            tok[f"cb1{b}"] = cv.inc(e.tensor_scalar_add(
                rows[0:1, R0:R0 + H], rows[0:1, R0:R0 + H], 1.0))
        self.on("vector", s_cb1)

        def s_cbb(e, m):
            m.wait(self.T(f"cb1{b}"))
            m.wait((self.cbc, self.cbc.count))
            self.cbc.inc(e.dma_start(out=self.rowsd[0:1, H:2 * H], in_=rows[0:1, R0:R0 + H]), 16)
            t = (self.cbc, self.cbc.count)
            m.wait(t)
            self.cbc.inc(e.dma_start(out=cbbc[:, :], in_=self.bcast(
                cbbc[:, :], self.rowsd[0:1, H:2 * H])), 16)
            tok[f"cbbc{b}"] = self.cbc.tok()
        self.on("sync", s_cbb)
        tok[f"bcC{b}"] = tok[f"cbbc{b}"]

        for g in range(2):
            for kt in range(NKI):
                j = g * NKI + kt

                def s_wl(e, m, g=g, kt=kt, j=j):
                    if j >= 2:
                        m.wait(self.T(f"opmm{b}_{j - 2}"))
                    self.stg[j % 2].inc(e.dma_start(
                        out=self.wst[j % 2][:, :],
                        in_=self.opT_d[kt * P:(kt + 1) * P, g * 1024:(g + 1) * 1024]), 16)
                    tok[f"opw{b}_{j}"] = self.stg[j % 2].tok()
                self.on("gpsimd", s_wl)

                def s_om(e, m, g=g, kt=kt, j=j):
                    if j == 0:
                        m.wait(self.T(f"yall{b}"))
                        m.wait(self.T(f"bcC{b}"))
                    if g == 1 and kt == 0:
                        for sl in range(2 * NTT):
                            m.wait(self.T(f"opev{b}_0_{sl}"))
                    m.wait(self.T(f"opw{b}_{j}"))
                    for lt in range(NTT):
                        for hc in range(2):
                            ct.inc(e.matmul(
                                PSL[lt * 2 + hc][:, 0:512],
                                y_all[:, kt * TC + lt * P:kt * TC + (lt + 1) * P],
                                self.wst[j % 2][:, hc * 512:(hc + 1) * 512],
                                start=(kt == 0), stop=(kt == NKI - 1)))
                    tok[f"opmm{b}_{j}"] = ct.tok()
                self.on("tensor", s_om)
            tok[f"opg{b}_{g}"] = self.ct.tok()

            for lt in range(NTT):
                def s_xq(e, m, g=g, lt=lt):
                    if lt >= 2:
                        m.wait(self.T(f"opev{b}_{g}_{(lt - 2) * 2 + 1}"))
                    elif g == 1 and lt == 0:
                        m.wait(self.T(f"opev{b}_0_{2 * NTT - 1}"))
                    self.xq[lt % 2].inc(e.dma_start(out=x_q[lt % 2][:, :],
                                        in_=self.xq_in[b, lt * P:(lt + 1) * P,
                                                       g * 1024:(g + 1) * 1024]), 16)
                    tok[f"xq{b}_{g}_{lt}"] = self.xq[lt % 2].tok()
                self.on("sync", s_xq)

                def s_oe(e, m, g=g, lt=lt):
                    m.wait(self.T(f"opg{b}_{g}"))
                    m.wait(self.T(f"xq{b}_{g}_{lt}"))
                    if c.debug:
                        m.wait(self.T(f"dbgB{b}")); m.wait(self.T(f"dbgBD{b}")); m.wait(self.T(f"dbgBY{b}"))
                    for hc in range(2):
                        sl = lt * 2 + hc
                        hcol = g * 1024 + hc * 512
                        if sl >= 1:
                            m.wait(self.T(f"opev{b}_{g}_{sl - 1}"))
                        elif g == 1:
                            m.wait(self.T(f"opev{b}_0_{2 * NTT - 1}"))
                        t = cv.inc(e.tensor_mul(self.tev[:, 0:512], PSL[sl][:, 0:512],
                                                gabc[:, hcol:hcol + 512]))
                        m.wait(t)
                        tok[f"opev{b}_{g}_{sl}"] = cv.inc(e.tensor_add(
                            x2_tm[:, lt * H + hcol:lt * H + hcol + 512],
                            self.tev[:, 0:512], x_q[lt % 2][:, hc * 512:(hc + 1) * 512]))
                self.on("vector", s_oe)
        tok[f"x2{b}"] = self.cv.tok()

        if c.debug:
            def s_xd(e, m):
                m.wait(self.T(f"x2{b}"))
                for lt in range(NTT):
                    self.dbgc.inc(e.dma_start(out=self.dbg["x2"][b, lt * P:(lt + 1) * P, :],
                                              in_=x2_tm[:, lt * H:(lt + 1) * H]), 16)
            self.on("sync", s_xd)

        for lt in range(NTT):
            def s_sq(e, m, lt=lt):
                m.wait(self.T(f"x2{b}"))
                if lt >= 1:
                    m.wait(self.T(f"sqred{b}_{lt - 1}"))
                tok[f"x2sq{b}_{lt}"] = ca.inc(e.activation(
                    xsq[:, :], x2_tm[:, lt * H:(lt + 1) * H], AF.Square))
            self.on("scalar", s_sq)

            def s_v1(e, m, lt=lt):
                m.wait(self.T(f"x2{b}"))
                if lt >= 1:
                    m.wait(self.T(f"h2tm{b}_{lt - 1}"))
                t0 = cv.inc(e.reduce_sum(self.scol[:, 0:1], x2_tm[:, lt * H:(lt + 1) * H], axis=AXX))
                m.wait(self.T(f"x2sq{b}_{lt}"))
                tok[f"sqred{b}_{lt}"] = cv.inc(e.reduce_sum(self.scol[:, 1:2], xsq[:, :], axis=AXX))
                m.wait(t0)
                t = cv.inc(e.tensor_scalar_mul(self.scol[:, 0:1], self.scol[:, 0:1], 1.0 / H))
                m.wait(self.T(f"sqred{b}_{lt}"))
                t = cv.inc(e.tensor_scalar_mul(self.scol[:, 1:2], self.scol[:, 1:2], 1.0 / H))
                m.wait(t)
                t = cv.inc(e.tensor_mul(self.scol[:, 2:3], self.scol[:, 0:1], self.scol[:, 0:1]))
                m.wait(t)
                t = cv.inc(e.tensor_sub(self.scol[:, 1:2],
                                        self.scol[:, 1:2], self.scol[:, 2:3]))
                m.wait(t)
                tok[f"varc{b}_{lt}"] = cv.inc(e.tensor_scalar_add(
                    self.scol[:, 1:2], self.scol[:, 1:2], 1e-5))
            self.on("vector", s_v1)

            def s_a1(e, m, lt=lt):
                m.wait(self.T(f"varc{b}_{lt}"))
                tok[f"lnc{b}_{lt}"] = ca.inc(e.activation(self.scol[:, 3:4],
                                                          self.scol[:, 1:2], AF.Ln))
            self.on("scalar", s_a1)

            def s_vh(e, m, lt=lt):
                m.wait(self.T(f"lnc{b}_{lt}"))
                tok[f"nlc{b}_{lt}"] = cv.inc(e.tensor_scalar_mul(
                    self.scol[:, 3:4], self.scol[:, 3:4], -0.5))
            self.on("vector", s_vh)

            def s_a1b(e, m, lt=lt):
                m.wait(self.T(f"nlc{b}_{lt}"))
                tok[f"rstdc{b}_{lt}"] = ca.inc(e.activation(self.scol[:, 3:4],
                                                            self.scol[:, 3:4], AF.Exp))
            self.on("scalar", s_a1b)

            def s_v2(e, m, lt=lt):
                m.wait(self.T(f"rstdc{b}_{lt}")); m.wait(self.T(f"bcC{b}"))
                if lt >= 1:
                    m.wait(self.T(f"h2tr{b}_{lt - 1}"))
                t = cv.inc(e.tensor_scalar(h2_tm[:, :], x2_tm[:, lt * H:(lt + 1) * H],
                                           self.scol[:, 0:1], self.scol[:, 3:4],
                                           ALU.subtract, ALU.mult))
                m.wait(t)
                t = cv.inc(e.tensor_mul(h2_tm[:, :], h2_tm[:, :], cbbc[:, :]))
                m.wait(t)
                tok[f"h2tm{b}_{lt}"] = cv.inc(e.tensor_add(h2_tm[:, :], h2_tm[:, :], sbbc[:, :]))
            self.on("vector", s_v2)

            def s_t1(e, m, lt=lt):
                m.wait(self.T(f"h2tm{b}_{lt}"))
                if lt == 0:
                    m.wait(self.T(f"x2{b}"))
                if lt >= 1:
                    m.wait(self.T(f"h2ev{b}_{lt - 1}"))
                    m.wait(self.T(f"rtc{b}_{lt - 1}_{NHT - 1}"))
                for ht in range(NHT):
                    dstp = psA if ht < 8 else psB
                    ct.inc(e.transpose(dstp[:, (ht % 8) * P:(ht % 8 + 1) * P],
                                       h2_tm[:, ht * P:(ht + 1) * P], self.id_f32[:, :]))
                tok[f"h2tr{b}_{lt}"] = ct.tok()
            self.on("tensor", s_t1)

            def s_a2(e, m, lt=lt):
                m.wait(self.T(f"h2tr{b}_{lt}"))
                ca.inc(e.activation(h2b3[:, 0:min(8, NHT), lt * P:(lt + 1) * P],
                                    psA[:, 0:min(8, NHT) * P], AF.Copy))
                if NHT > 8:
                    ca.inc(e.activation(h2b3[:, 8:NHT, lt * P:(lt + 1) * P],
                                        psB[:, 0:(NHT - 8) * P], AF.Copy))
                tok[f"h2ev{b}_{lt}"] = ca.tok()
            self.on("scalar", s_a2)

            for ht in range(NHT):
                def s_rc(e, m, lt=lt, ht=ht):
                    m.wait(self.T(f"h2tr{b}_{lt}"))
                    if lt * NHT + ht >= 2:
                        pj = lt * NHT + ht - 2
                        m.wait(self.T(f"rtm2{b}_{pj // NHT}_{pj % NHT}"))
                    src = psA if ht < 8 else psB
                    tok[f"rtc{b}_{lt}_{ht}"] = ca.inc(e.activation(
                        self.h2rt[ht % 2][:, :], src[:, (ht % 8) * P:(ht % 8 + 1) * P],
                        AF.Copy))
                self.on("scalar", s_rc)

                def s_rm(e, m, lt=lt, ht=ht):
                    m.wait(self.T(f"rtc{b}_{lt}_{ht}"))
                    ct.inc(e.matmul(psC[0:E, lt * P:(lt + 1) * P],
                                    self.gwT_sb[:, ht * E:(ht + 1) * E],
                                    self.h2rt[ht % 2][:, :],
                                    start=(ht == 0), stop=(ht == NHT - 1)))
                    ct.inc(e.matmul(psD[0:1, lt * P:(lt + 1) * P],
                                    self.sgwT_sb[:, ht:ht + 1],
                                    self.h2rt[ht % 2][:, :],
                                    start=(ht == 0), stop=(ht == NHT - 1)))
                    tok[f"rtm2{b}_{lt}_{ht}"] = ct.tok()
                self.on("tensor", s_rm)
                tok[f"h2done{b}"] = self.ca.tok()

        if c.debug:
            for ht in range(NHT):
                def s_h2c(e, m, ht=ht):
                    m.wait(self.T(f"h2done{b}"))
                    if ht >= 1:
                        m.wait(self.T(f"h2dd{b}_{ht - 1}"))
                    tok[f"h2dc{b}_{ht}"] = cv.inc(e.tensor_copy(
                        self.tev[:, 0:TC], h2b[:, ht * TC:(ht + 1) * TC]))
                self.on("vector", s_h2c)

                def s_h2w(e, m, ht=ht):
                    m.wait(self.T(f"h2dc{b}_{ht}"))
                    self.dbg2.inc(e.dma_start(
                        out=self.dbg["h2"][b, ht * P:(ht + 1) * P, :],
                        in_=self.tev[:, 0:TC]), 16)
                    tok[f"h2dd{b}_{ht}"] = self.dbg2.tok()
                self.on("sync", s_h2w)

        tok[f"rmm{b}"] = self.ct.tok()
        tok[f"sgmm{b}"] = self.ct.tok()

        def s_re(e, m):
            m.wait(self.T(f"rmm{b}"))
            tok[f"rsb{b}"] = cv.inc(e.tensor_copy(self.rsb[:, :], psC[0:E, 0:TC]))
        self.on("vector", s_re)

        def s_sg(e, m):
            m.wait(self.T(f"sgmm{b}")); m.wait(self.T(f"gacl{b}"))
            tok[f"sgrow{b}"] = ca.inc(e.activation(self.rows[0:1, 2048:2048 + TC],
                                                   psD[0:1, 0:TC], AF.Sigmoid))
        self.on("scalar", s_sg)

        def s_rtr(e, m):
            m.wait(self.T(f"rsb{b}"))
            m.wait(self.T(f"sgrow{b}"))
            for lt in range(NTT):
                ct.inc(e.transpose(psD[:, lt * E:(lt + 1) * E],
                                   self.rsb[0:E, lt * P:(lt + 1) * P], self.id_f32[0:E, 0:E]))
            tok[f"rtr{b}"] = ct.tok()
        self.on("tensor", s_rtr)

        def s_rc(e, m):
            m.wait(self.T(f"rtr{b}"))
            tok[f"rtm{b}"] = cv.inc(e.tensor_copy(self.r_tm[:, 0:NTT * E],
                                                  psD[:, 0:NTT * E]))
        self.on("vector", s_rc)

        for lt in range(NTT):
            sl = slice(lt * E, (lt + 1) * E)

            def s_m1(e, m, lt=lt, sl=sl):
                m.wait(self.T(f"rtm{b}"))
                tok[f"nmx{b}_{lt}"] = cv.inc(e.reduce_max(self.scol[:, 4:5],
                                                          self.r_tm[:, sl], axis=AXX,
                                                          negate=True))
            self.on("vector", s_m1)

            def s_ex(e, m, lt=lt, sl=sl):
                m.wait(self.T(f"nmx{b}_{lt}"))
                tok[f"smex{b}_{lt}"] = ca.inc(e.activation(
                    self.rt1[:, sl], self.r_tm[:, sl], AF.Exp, bias=self.scol[:, 4:5]))
            self.on("scalar", s_ex)

            def s_m2(e, m, lt=lt, sl=sl):
                m.wait(self.T(f"smex{b}_{lt}"))
                t = cv.inc(e.reduce_sum(self.scol[:, 5:6], self.rt1[:, sl], axis=AXX))
                m.wait(t)
                t = cv.inc(e.reciprocal(self.scol[:, 5:6], self.scol[:, 5:6]))
                m.wait(t)
                t = cv.inc(e.tensor_scalar_mul(self.r_tm[:, sl], self.rt1[:, sl],
                                               self.scol[:, 5:6]))
                m.wait(t)
                t = cv.inc(e.reduce_max(self.scol[:, 6:7], self.r_tm[:, sl], axis=AXX))
                m.wait(t)
                t = cv.inc(e.tensor_scalar(self.rt1[:, sl], self.r_tm[:, sl],
                                           self.scol[:, 6:7], None, ALU.is_equal))
                m.wait(t)
                t = cv.inc(e.scalar_tensor_tensor(self.rt1[:, sl], self.rt1[:, sl],
                                                  -1e30, self.r_tm[:, sl],
                                                  ALU.mult, ALU.add))
                m.wait(t)
                t = cv.inc(e.reduce_max(self.scol[:, 7:8], self.rt1[:, sl], axis=AXX))
                m.wait(t)
                t = cv.inc(e.tensor_scalar(self.rt1[:, sl], self.r_tm[:, sl],
                                           self.scol[:, 7:8], None, ALU.is_ge))
                m.wait(t)
                tok[f"cmb{b}_{lt}"] = cv.inc(e.tensor_mul(self.r_tm[:, sl],
                                                          self.r_tm[:, sl], self.rt1[:, sl]))
            self.on("vector", s_m2)
        tok[f"cmbv{b}"] = self.cv.tok()

        def s_ctr(e, m):
            m.wait(self.T(f"cmbv{b}"))
            for lt in range(NTT):
                ct.inc(e.transpose(psD[0:E, 512 + lt * P:512 + (lt + 1) * P],
                                   self.r_tm[:, lt * E:(lt + 1) * E], self.id_f32[:, :]))
            tok[f"ctr{b}"] = ct.tok()
        self.on("tensor", s_ctr)

        def s_ce(e, m):
            m.wait(self.T(f"ctr{b}"))
            tok[f"cmbsb{b}"] = cv.inc(e.tensor_copy(self.cmbsb[:, :],
                                                    psD[0:E, 512:512 + TC]))
        self.on("vector", s_ce)

        def s_cw(e, m):
            m.wait(self.T(f"cmbsb{b}"))
            self.cr[0].inc(e.dma_start(out=self.cmb_d[b][:, :], in_=self.cmbsb[:, :]), 16)
            tok[f"cmbw{b}"] = self.cr[0].tok()
            if c.debug:
                self.dbgc.inc(e.dma_start(out=self.dbg["cmb"][b, :, :], in_=self.cmbsb[:, :]), 16)
        self.on("sync", s_cw)

        def s_sb(e, m):
            m.wait(self.T(f"sgrow{b}"))
            m.wait((self.cbc, self.cbc.count))
            self.cbc.inc(e.dma_start(out=self.rowsd[0:1, 2 * H:2 * H + TC],
                                     in_=self.rows[0:1, 2048:2048 + TC]), 16)
            t = (self.cbc, self.cbc.count)
            m.wait(t)
            self.cbc.inc(e.dma_start(out=self.sgbc[:, b * TC:(b + 1) * TC], in_=self.bcast(
                self.sgbc[:, b * TC:(b + 1) * TC], self.rowsd[0:1, 2 * H:2 * H + TC])), 16)
            tok[f"sgbc{b}"] = self.cbc.tok()
        self.on("sync", s_sb)

        def s_x2w(e, m):
            m.wait(self.T(f"x2{b}"))
            for lt in range(NTT):
                self.cx2.inc(e.dma_start(out=self.x2_d[b][lt * P:(lt + 1) * P, :],
                                         in_=x2_tm[:, lt * H:(lt + 1) * H]), 16)
                tok[f"x2w{b}_{lt}"] = self.cx2.tok()
        self.on("sync", s_x2w)

        if b == 0:
            def s_h2w0(e, m):
                m.wait(self.T(f"h2ev0_{NTT - 1}"))
                if c.debug:
                    m.wait(self.T(f"h2dd0_{NHT - 1}"))
                self.ch2.inc(e.dma_start(out=self.h2b_d0[:, :], in_=h2b[:, :]), 16)
                tok["h2bw0"] = self.ch2.tok()
            self.on("sync", s_h2w0)

    def ph_D(self):
        c, tok = self.c, self.tok
        cd, cq, cv, ca, ct = self.cd, self.cq, self.cv, self.ca, self.ct
        B, H, TC, NHT, NTT, E = c.B, c.H, c.TC, c.NHT, c.NTT, c.E
        NFM, NFS = c.NFM, c.NFS
        psA, psB, psC, psD = self.psA, self.psB, self.psC, self.psD
        PSL = self.PSL
        NKI = c.I // P
        ymoe_b = [self.v32(self.AR1, 0, NHT * TC), self.v32(self.AR2, 0, NHT * TC)]
        h2b_b = [self.vbf(self.AR3, 0, NHT * TC), self.vbf(self.AR6, 0, NHT * TC)]
        s1_sb = self.v32(self.AR3, NHT * TC // 2, 8 * TC)
        he_b = [self.vbf(self.AR4, 0, NFM * TC),
                self.vbf(self.AR4, NFM * TC // 2 + 32, NFM * TC)]
        sho = NFM * TC + 64
        shf_b = [self.vbf(self.AR4, sho, 4 * TC),
                 self.vbf(self.AR4, sho + 2 * TC + 16, 4 * TC)]
        gbbc = self.v32(self.AR3, NHT * TC // 2, H)
        x2t = self.v32(self.AR3, NHT * TC // 2 + H, H)
        rows = self.rows
        mg_f = [list(range(g, min(g + 4, NFM))) for g in range(0, NFM, 4)]
        hgrp = [list(range(g, min(g + 4, NHT))) for g in range(0, NHT, 4)]
        NMG, NHG = len(mg_f), len(hgrp)
        lastC = B - 1

        def s_h2l(e, m):
            m.wait(self.T(f"h2tm{lastC}_{NTT - 1}"))
            m.wait(self.T("h2bw0"))
            self.ch2.inc(e.dma_start(out=h2b_b[0][:, :], in_=self.h2b_d0[:, :]), 16)
            tok["h2bl0"] = self.ch2.tok()
        self.on("sync", s_h2l)

        def s_init(e, m):
            m.wait(self.T(f"x2{lastC}"))
            cv.inc(e.memset(ymoe_b[0][:, :], 0.0))
            m.wait(self.T(f"x2w{lastC}_{NTT - 1}"))
            if c.debug:
                m.wait((self.dbgc, self.dbgc.count))
            cv.inc(e.memset(ymoe_b[1][:, :], 0.0))
            tok["ymz"] = cv.tok()
        self.on("vector", s_init)

        for b in range(B):
            def s_gr(e, m, b=b):
                m.wait(self.T("modflat"))
                m.wait(self.T(f"cbbc{lastC}"))
                m.wait(self.T(f"sgbc{lastC}"))
                if b > 0:
                    m.wait(self.T("gbw0"))
                self.cdM.inc(e.dma_start(out=rows[0:1, 0:H],
                                         in_=self.modflat_d[b:b + 1, 5 * H:6 * H]), 16)
                tok[f"gbrow{b}"] = self.cdM.tok()
            self.on("sync", s_gr)
            self.emit_clamp(b, 0, 2048, lambda b=b: tok[f"gbrow{b}"], f"gbcl{b}")

            def s_gw(e, m, b=b):
                m.wait(self.T(f"gbcl{b}"))
                self.cdM.inc(e.dma_start(out=self.rowsd[0:1, (3 + b) * H:(4 + b) * H],
                                         in_=rows[0:1, 0:H]), 16)
                tok[f"gbw{b}"] = self.cdM.tok()
            self.on("sync", s_gw)

        self.wstD = [self.wst[0], self.wst[1],
                     self.vbf(self.AR5, 0, 1024), self.vbf(self.AR5, 512, 1024)]
        self.dring = []

        def wload(src_ap, width, cons_name):
            i = len(self.dring)
            sl = i % 4
            war = self.dring[i - 4] if i >= 4 else None
            self.dring.append(cons_name)

            def s_l(e, m, i=i, sl=sl, war=war, src_ap=src_ap, width=width):
                if war is not None:
                    m.wait(self.T(war))
                elif sl >= 2:
                    m.wait(self.T(f"x2{lastC}"))
                else:
                    m.wait(self.T(f"opmm{lastC}_{2 * NKI - 2 + sl}"))
                self.stg[sl].inc(e.dma_start(out=self.wstD[sl][:, 0:width],
                                             in_=src_ap), 16)
                tok[f"dwl{i}"] = self.stg[sl].tok()
            self.on("gpsimd", s_l)
            return i, sl

        def fc_psum_war(sl, groups):
            for gi in reversed(range(len(groups))):
                if sl < 2 * len(groups[gi]):
                    return gi
            return None

        for ee in range(E):
            def s_cbc2(e, m, ee=ee):
                for b in range(B):
                    m.wait(self.T(f"cmbw{b}"))
                if ee >= 1:
                    m.wait(self.T(f"hev{ee - 1}_{NMG - 1}"))
                for b in range(B):
                    self.cr[ee % 2].inc(e.dma_start(
                        out=self.cmbbc[:, b * TC:(b + 1) * TC],
                        in_=self.bcast(self.cmbbc[:, b * TC:(b + 1) * TC],
                                       self.cmb_d[b][ee:ee + 1, :])), 16)
                tok[f"cbc{ee}"] = (self.cr[ee % 2], self.cr[ee % 2].count)
            self.on("sync", s_cbc2)

            for gi, grp in enumerate(mg_f):
                g0, gn = grp[0], len(grp)
                for wi, wsrc in ((0, self.fc1T_d[ee]), (1, self.fc2T_d[ee])):
                    for kt in range(NHT):
                        li, sl_w = wload(
                            wsrc[kt * P:(kt + 1) * P, g0 * P:(g0 + gn) * P],
                            gn * P, f"fmm{ee}_{wi}_{gi}_{kt}")

                        def s_fm(e, m, ee=ee, wi=wi, gi=gi, gn=gn, kt=kt,
                                 li=li, sl_w=sl_w):
                            if kt == 0:
                                for b in range(B):
                                    m.wait(self.T(f"h2done{b}"))
                                m.wait(self.T("h2bl0"))
                                if wi == 0 and gi == 0:
                                    m.wait(self.T(f"cbc{ee}"))
                            m.wait(self.T(f"dwl{li}"))
                            for mi in range(gn):
                                for b in range(B):
                                    sl = mi * 2 + b
                                    if kt == 0:
                                        if wi == 1:
                                            m.wait(self.T(f"silt{ee}_{gi}_{sl}"))
                                        elif ee == 0 and gi == 0:
                                            m.wait(self.T(f"cmbw{lastC}"))
                                        elif gi == 0:
                                            m.wait(self.T(f"ymt{ee - 1}_{NHG - 1}_{sl}"))
                                        else:
                                            m.wait(self.T(f"hepf{ee}_{gi - 1}_{sl}"))
                                    ct.inc(e.matmul(PSL[sl][:, 0:TC],
                                                    self.wstD[sl_w][:, mi * P:(mi + 1) * P],
                                                    h2b_b[b][:, kt * TC:(kt + 1) * TC],
                                                    start=(kt == 0), stop=(kt == NHT - 1)))
                            tok[f"fmm{ee}_{wi}_{gi}_{kt}"] = ct.tok()
                        self.on("tensor", s_fm)

                    if wi == 0:
                        def s_si(e, m, ee=ee, gi=gi, gn=gn):
                            m.wait(self.T(f"fmm{ee}_0_{gi}_{NHT - 1}"))
                            for mi in range(gn):
                                for b in range(B):
                                    sl = mi * 2 + b
                                    tok[f"silt{ee}_{gi}_{sl}"] = ca.inc(
                                        e.activation(s1_sb[:, sl * TC:(sl + 1) * TC],
                                                     PSL[sl][:, 0:TC], AF.Silu))
                            tok[f"sil{ee}_{gi}"] = ca.tok()
                        self.on("scalar", s_si)
                    else:
                        def s_he(e, m, ee=ee, gi=gi, g0=g0, gn=gn):
                            if c.debug:
                                for b in range(B):
                                    m.wait(self.T(f"dbgB{b}")); m.wait(self.T(f"dbgBD{b}"))
                                    m.wait(self.T(f"dbgBY{b}")); m.wait(self.T(f"h2dd{b}_{NHT - 1}"))
                            m.wait(self.T(f"fmm{ee}_1_{gi}_{NHT - 1}"))
                            m.wait(self.T(f"sil{ee}_{gi}"))
                            m.wait(self.T(f"cbc{ee}"))
                            for mi in range(gn):
                                for b in range(B):
                                    sl = mi * 2 + b
                                    m.wait((cv, cv.count))
                                    tok[f"hepf{ee}_{gi}_{sl}"] = cv.inc(
                                        e.tensor_mul(self.tev[:, 0:TC],
                                                     s1_sb[:, sl * TC:(sl + 1) * TC],
                                                     PSL[sl][:, 0:TC]))
                                    m.wait(self.T(f"hepf{ee}_{gi}_{sl}"))
                                    tok[f"hew{ee}_{g0 + mi}_{b}"] = cv.inc(e.tensor_mul(
                                        he_b[b][:, (g0 + mi) * TC:(g0 + mi + 1) * TC],
                                        self.tev[:, 0:TC], self.cmbbc[:, b * TC:(b + 1) * TC]))
                            tok[f"hev{ee}_{gi}"] = cv.tok()
                        self.on("vector", s_he)

            for g2, grp2 in enumerate(hgrp):
                h0, hn = grp2[0], len(grp2)
                for kt in range(NFM):
                    li, sl_w = wload(
                        self.prT_d[ee][kt * P:(kt + 1) * P, h0 * P:(h0 + hn) * P],
                        hn * P, f"pmm{ee}_{g2}_{kt}")

                    def s_pm(e, m, ee=ee, g2=g2, hn=hn, kt=kt, li=li, sl_w=sl_w):
                        for b in range(B):
                            m.wait(self.T(f"hew{ee}_{kt}_{b}"))
                        m.wait(self.T(f"dwl{li}"))
                        for mi in range(hn):
                            for b in range(B):
                                sl = mi * 2 + b
                                if kt == 0:
                                    if g2 >= 1:
                                        m.wait(self.T(f"ymt{ee}_{g2 - 1}_{sl}"))
                                    else:
                                        lg = fc_psum_war(sl, mg_f)
                                        if lg is not None:
                                            m.wait(self.T(f"hepf{ee}_{lg}_{sl}"))
                                        elif ee >= 1:
                                            m.wait(self.T(f"ymt{ee - 1}_{NHG - 1}_{sl}"))
                                        else:
                                            m.wait(self.T(f"cmbw{lastC}"))
                                ct.inc(e.matmul(PSL[sl][:, 0:TC],
                                                self.wstD[sl_w][:, mi * P:(mi + 1) * P],
                                                he_b[b][:, kt * TC:(kt + 1) * TC],
                                                start=(kt == 0), stop=(kt == NFM - 1)))
                        tok[f"pmm{ee}_{g2}_{kt}"] = ct.tok()
                    self.on("tensor", s_pm)

                def s_pe(e, m, ee=ee, g2=g2, h0=h0, hn=hn):
                    m.wait(self.T(f"pmm{ee}_{g2}_{NFM - 1}"))
                    m.wait(self.T("ymz"))
                    for mi in range(hn):
                        for b in range(B):
                            mt = h0 + mi
                            tok[f"ymt{ee}_{g2}_{mi * 2 + b}"] = cv.inc(
                                e.tensor_add(ymoe_b[b][:, mt * TC:(mt + 1) * TC],
                                             ymoe_b[b][:, mt * TC:(mt + 1) * TC],
                                             PSL[mi * 2 + b][:, 0:TC]))
                    tok[f"ymadd{ee}_{g2}"] = cv.tok()
                self.on("vector", s_pe)

        nfc = (NFS + 3) // 4
        for fc in range(nfc):
            f0 = fc * 4
            fn = min(4, NFS - f0)
            sgrp = [list(range(fn))]
            for wi, wsrc in ((0, self.sf1T_d), (1, self.sf2T_d)):
                for kt in range(NHT):
                    li, sl_w = wload(
                        wsrc[kt * P:(kt + 1) * P, f0 * P:(f0 + fn) * P],
                        fn * P, f"smm{fc}_{wi}_{kt}")

                    def s_sm(e, m, fc=fc, fn=fn, wi=wi, kt=kt, li=li, sl_w=sl_w):
                        m.wait(self.T(f"dwl{li}"))
                        for mi in range(fn):
                            for b in range(B):
                                sl = mi * 2 + b
                                if kt == 0:
                                    if wi == 1:
                                        m.wait(self.T(f"ssilt{fc}_{sl}"))
                                    elif fc == 0:
                                        m.wait(self.T(f"ymt{E - 1}_{NHG - 1}_{sl}"))
                                    else:
                                        m.wait(self.T(f"yst{fc - 1}_{NHG - 1}_{sl}"))
                                ct.inc(e.matmul(PSL[sl][:, 0:TC],
                                                self.wstD[sl_w][:, mi * P:(mi + 1) * P],
                                                h2b_b[b][:, kt * TC:(kt + 1) * TC],
                                                start=(kt == 0), stop=(kt == NHT - 1)))
                        tok[f"smm{fc}_{wi}_{kt}"] = ct.tok()
                    self.on("tensor", s_sm)

                if wi == 0:
                    def s_ss(e, m, fc=fc, fn=fn):
                        m.wait(self.T(f"smm{fc}_0_{NHT - 1}"))
                        for mi in range(fn):
                            for b in range(B):
                                sl = mi * 2 + b
                                tok[f"ssilt{fc}_{sl}"] = ca.inc(
                                    e.activation(s1_sb[:, sl * TC:(sl + 1) * TC],
                                                 PSL[sl][:, 0:TC], AF.Silu))
                        tok[f"ssil{fc}"] = ca.tok()
                    self.on("scalar", s_ss)
                else:
                    def s_sh(e, m, fc=fc, fn=fn):
                        m.wait(self.T(f"smm{fc}_1_{NHT - 1}"))
                        m.wait(self.T(f"ssil{fc}"))
                        for mi in range(fn):
                            for b in range(B):
                                sl = mi * 2 + b
                                t = cv.inc(e.tensor_mul(shf_b[b][:, mi * TC:(mi + 1) * TC],
                                                        s1_sb[:, sl * TC:(sl + 1) * TC],
                                                        PSL[sl][:, 0:TC]))
                                tok[f"shpf{fc}_{sl}"] = t
                                tok[f"shw{fc}_{mi}_{b}"] = t
                        tok[f"shev{fc}"] = cv.tok()
                    self.on("vector", s_sh)

            for g2, grp2 in enumerate(hgrp):
                h0, hn = grp2[0], len(grp2)
                for ktc in range(fn):
                    li, sl_w = wload(
                        self.sprT_d[(f0 + ktc) * P:(f0 + ktc + 1) * P,
                                    h0 * P:(h0 + hn) * P],
                        hn * P, f"spm{fc}_{g2}_{ktc}")

                    def s_spm(e, m, fc=fc, fn=fn, g2=g2, hn=hn, ktc=ktc,
                              li=li, sl_w=sl_w):
                        for b in range(B):
                            m.wait(self.T(f"shw{fc}_{ktc}_{b}"))
                        m.wait(self.T(f"dwl{li}"))
                        for mi in range(hn):
                            for b in range(B):
                                sl = mi * 2 + b
                                if ktc == 0:
                                    if g2 >= 1:
                                        m.wait(self.T(f"yst{fc}_{g2 - 1}_{sl}"))
                                    else:
                                        lg = fc_psum_war(sl, [list(range(fn))])
                                        if lg is not None:
                                            m.wait(self.T(f"shpf{fc}_{sl}"))
                                        elif fc >= 1:
                                            m.wait(self.T(f"yst{fc - 1}_{NHG - 1}_{sl}"))
                                        else:
                                            m.wait(self.T(f"ymt{E - 1}_{NHG - 1}_{sl}"))
                                ct.inc(e.matmul(PSL[sl][:, 0:TC],
                                                self.wstD[sl_w][:, mi * P:(mi + 1) * P],
                                                shf_b[b][:, ktc * TC:(ktc + 1) * TC],
                                                start=(ktc == 0), stop=(ktc == fn - 1)))
                        tok[f"spm{fc}_{g2}_{ktc}"] = ct.tok()
                    self.on("tensor", s_spm)

                def s_spe(e, m, fc=fc, fn=fn, g2=g2, h0=h0, hn=hn):
                    m.wait(self.T(f"spm{fc}_{g2}_{fn - 1}"))
                    for b in range(B):
                        m.wait(self.T(f"sgbc{b}"))
                    m.wait(self.T("ymz"))
                    for mi in range(hn):
                        for b in range(B):
                            mt = h0 + mi
                            m.wait((cv, cv.count))
                            t = cv.inc(e.tensor_mul(self.tev[:, 0:TC],
                                                    PSL[mi * 2 + b][:, 0:TC],
                                                    self.sgbc[:, b * TC:(b + 1) * TC]))
                            m.wait(t)
                            tok[f"yst{fc}_{g2}_{mi * 2 + b}"] = cv.inc(
                                e.tensor_add(ymoe_b[b][:, mt * TC:(mt + 1) * TC],
                                             ymoe_b[b][:, mt * TC:(mt + 1) * TC],
                                             self.tev[:, 0:TC]))
                    tok[f"ysadd{fc}_{g2}"] = cv.tok()
                self.on("vector", s_spe)

        tok["ysadd_last"] = tok[f"ysadd{nfc - 1}_{NHG - 1}"]

        for b in range(B):
            def s_gbb(e, m, b=b):
                m.wait(self.T("ysadd_last"))
                m.wait(self.T(f"gbw{b}"))
                m.wait((self.cbc, self.cbc.count))
                self.cbc.inc(e.dma_start(out=gbbc[:, :], in_=self.bcast(
                    gbbc[:, :], self.rowsd[0:1, (3 + b) * H:(4 + b) * H])), 16)
                tok[f"gbbcf{b}"] = self.cbc.tok()
            self.on("sync", s_gbb)

            for lt in range(NTT):
                def s_xl(e, m, b=b, lt=lt):
                    m.wait(self.T("ysadd_last"))
                    if lt > 0:
                        m.wait(self.T(f"x3w{b}_{lt - 1}"))
                    elif b > 0:
                        m.wait(self.T(f"x3w{b - 1}_{NTT - 1}"))
                    self.cx2.inc(e.dma_start(out=x2t[:, :],
                                             in_=self.x2_d[b][lt * P:(lt + 1) * P, :]), 16)
                    tok[f"x2l{b}_{lt}"] = self.cx2.tok()
                self.on("sync", s_xl)

                def s_ft(e, m, b=b, lt=lt):
                    if b == 0 and lt == 0:
                        m.wait(self.T("ysadd_last"))
                    if lt >= 1:
                        m.wait(self.T(f"fev{b}_{lt - 1}"))
                    elif b >= 1:
                        m.wait(self.T(f"fev{b - 1}_{NTT - 1}"))
                    for ctile in range(NHT):
                        dstp = psA if ctile < 8 else psB
                        ct.inc(e.transpose(
                            dstp[:, (ctile % 8) * P:(ctile % 8 + 1) * P],
                            ymoe_b[b][:, ctile * TC + lt * P:ctile * TC + (lt + 1) * P],
                            self.id_f32[:, :]))
                    tok[f"ftr{b}_{lt}"] = ct.tok()
                self.on("tensor", s_ft)

                def s_fe(e, m, b=b, lt=lt):
                    m.wait(self.T(f"gbbcf{b}"))
                    m.wait(self.T(f"ftr{b}_{lt}"))
                    m.wait(self.T(f"x2l{b}_{lt}"))
                    for j in range(H // 512):
                        pp = psA[:, (j % 2) * 512:(j % 2) * 512 + 512] if j < 2 else \
                            psB[:, ((j - 2) % 2) * 512:((j - 2) % 2) * 512 + 512]
                        m.wait((cv, cv.count))
                        t = cv.inc(e.tensor_mul(self.tev[:, 0:512], pp,
                                                gbbc[:, j * 512:(j + 1) * 512]))
                        m.wait(t)
                        cv.inc(e.tensor_add(x2t[:, j * 512:(j + 1) * 512],
                                            self.tev[:, 0:512],
                                            x2t[:, j * 512:(j + 1) * 512]))
                    tok[f"fev{b}_{lt}"] = cv.tok()
                self.on("vector", s_fe)

                def s_fw(e, m, b=b, lt=lt):
                    m.wait(self.T(f"fev{b}_{lt}"))
                    self.outW.inc(e.dma_start(out=self.out_d[b, lt * P:(lt + 1) * P, :],
                                              in_=x2t[:, 0:H]), 16)
                    tok[f"x3w{b}_{lt}"] = self.outW.tok()
                self.on("sync", s_fw)
        tok["out_all"] = self.outW.tok()

    def ph_final(self):
        chains_sync = [self.cd, self.cdM, self.xs, self.spU, self.spG, self.ug,
                       self.ssmW, self.ygW, self.ya, self.outW, self.dbgc,
                       self.cr[0], self.cr[1], self.xq[0], self.xq[1], self.dbg2, self.dbgD, self.dbgY,
                       self.rbc, self.cbc, self.cx2, self.ch2]
        chains_gp = [self.cg, self.cgM] + self.stg
        chains_sc = [self.st[0], self.st[1], self.cq]

        def s_s(e, m):
            for ch in chains_sync:
                if ch.count:
                    e.wait_ge(ch.sem, ch.count)
        self.on("sync", s_s)

        def s_p(e, m):
            for ch in chains_gp:
                if ch.count:
                    e.wait_ge(ch.sem, ch.count)
        self.on("gpsimd", s_p)

        def s_q(e, m):
            for ch in chains_sc:
                if ch.count:
                    e.wait_ge(ch.sem, ch.count)
        self.on("scalar", s_q)


def build(cfg):
    kb1 = KB(cfg)
    kb1.alloc()
    kb1.emit()
    rec = {k: (v[0].name, v[1]) for k, v in kb1.tok.items() if v is not None}
    kb = KB(cfg, rec=rec)
    kb.alloc()
    nc = kb.emit()
    return nc, kb.dbg


def _stage(inputs, cfg, core):
    c = cfg
    ic0 = core * c.IC
    tc0 = core * c.TC
    T = lambda a: np.ascontiguousarray(np.swapaxes(np.asarray(a), -1, -2))
    ipw = np.asarray(inputs["L0_attn_in_proj_w"])
    ipT = np.ascontiguousarray(
        np.concatenate([ipw[ic0:ic0 + c.IC], ipw[c.I + ic0:c.I + ic0 + c.IC]], 0).T)
    x = np.ascontiguousarray(np.asarray(inputs["L0_attn_x"], dtype=np.float32))
    d = dict(
        x=x,
        xq=x[:, tc0:tc0 + c.TC, :],
        ipT=ipT,
        convw=np.asarray(inputs["L0_attn_conv_w"])[ic0:ic0 + c.IC, 0, :],
        convb=np.asarray(inputs["L0_attn_conv_b"])[ic0:ic0 + c.IC, None],
        xpT=np.asarray(inputs["L0_attn_x_proj_w"]).T[ic0:ic0 + c.IC, :],
        dtpT=np.asarray(inputs["L0_attn_dt_proj_w"]).T[:, ic0:ic0 + c.IC],
        dtb=np.asarray(inputs["L0_attn_dt_proj_b"])[ic0:ic0 + c.IC, None],
        alog=np.asarray(inputs["L0_attn_A_log"])[ic0:ic0 + c.IC, :],
        dvec=np.asarray(inputs["L0_attn_D"])[ic0:ic0 + c.IC, None],
        opT=np.asarray(inputs["L0_attn_out_proj_w"]).T,
        fc1T=T(inputs["L0_ffn_expert_fc1"]),
        fc2T=T(inputs["L0_ffn_expert_fc2"]),
        prT=T(inputs["L0_ffn_expert_proj"]),
        sf1T=np.asarray(inputs["L0_ffn_shared_fc1"]).T,
        sf2T=np.asarray(inputs["L0_ffn_shared_fc2"]).T,
        sprT=np.asarray(inputs["L0_ffn_shared_proj"]).T,
        gwT=np.asarray(inputs["L0_ffn_gate_w"]).T,
        sgwT=np.asarray(inputs["L0_ffn_shared_gate_w"]).T,
        c_in=np.asarray(inputs["L0_c"]),
        wmod=np.asarray(inputs["L0_wmod"])[:, core * c.MODSL:(core + 1) * c.MODSL],
    )
    return {k: np.ascontiguousarray(np.asarray(v), dtype=np.float32) for k, v in d.items()}


def kernel(**inputs):
    cfg = CFG()
    in_maps = [_stage(inputs, cfg, core) for core in range(NCORE)]
    nc, _ = build(cfg)
    res = run_bass_kernel_spmd(nc, in_maps, core_ids=list(range(NCORE)))
    out = np.zeros((cfg.B, cfg.L, cfg.H), dtype=np.float32)
    for core in range(NCORE):
        o = np.asarray(res.results[core]["out"])
        out[:, core * cfg.TC:(core + 1) * cfg.TC, :] = o
    return out


# version 14
# speedup vs baseline: 1.2775x; 1.0536x over previous
import numpy as np
import ml_dtypes
import concourse.bass as bass
import concourse.mybir as mybir
from concourse.bass_utils import run_bass_kernel_spmd
from concourse.aot_env import allow_ici_all_to_all

F32 = mybir.dt.float32
BF16 = mybir.dt.bfloat16
AF = mybir.ActivationFunctionType
ALU = mybir.AluOpType
AXX = mybir.AxisListType.X

NCORE = 8
P = 128


class Chain:
    def __init__(self, nc, name, registry=None):
        self._cm = nc.semaphore(name)
        self.sem = self._cm.__enter__()
        self.name = name
        self.count = 0
        if registry is not None:
            registry[name] = self

    def inc(self, instr, amt=1):
        instr.then_inc(self.sem, amt)
        self.count += amt
        return (self, self.count)

    def tok(self):
        return (self, self.count)


class Em:
    def __init__(self):
        self.e = None
        self.waited = {}

    def wait(self, tok):
        if tok is None:
            return
        chain, val = tok
        if self.waited.get(chain.name, 0) < val:
            self.e.wait_ge(chain.sem, val)
            self.waited[chain.name] = val


class CFG:
    def __init__(self, B=2, L=4096, H=2048, I=4096, N=16, R=128, K=4,
                 E=8, FM=1408, FS=5632, debug=False):
        self.B, self.L, self.H, self.I, self.N, self.R, self.K = B, L, H, I, N, R, K
        self.E, self.FM, self.FS = E, FM, FS
        self.IC = I // NCORE
        self.TC = L // NCORE
        self.NIT = self.IC // P
        self.NHT = H // P
        self.NFM = (FM + P - 1) // P
        self.NFS = FS // P
        self.NTT = self.TC // P
        self.ALC = min(512, L)
        self.NALC = L // self.ALC
        self.NAT = self.ALC // P
        self.SLC = min(512, L)
        self.NSLC = L // self.SLC
        self.MODSL = 6 * H // NCORE
        self.debug = debug


class KB:
    def __init__(self, cfg, rec=None):
        self.c = cfg
        self.nc = bass.Bass("TRN2", target_bir_lowering=False)
        self.tok = {}
        self.ems = {}
        self.dbg = {}
        self.rec = rec
        self.chains = {}

    def T(self, name):
        t = self.tok.get(name)
        if t is not None:
            return t
        if self.rec is None:
            return None
        cname, val = self.rec[name]
        return (self.chains[cname], val)

    def on(self, eng, fn):
        em = self.ems.setdefault(eng, Em())

        def wrap(e):
            em.e = e
            fn(e, em)

        getattr(self.block, eng)(wrap)

    def alloc(self):
        c, nc = self.c, self.nc
        B, L, H, I, N, R, K = c.B, c.L, c.H, c.I, c.N, c.R, c.K
        E, FM, FS, IC, TC = c.E, c.FM, c.FS, c.IC, c.TC
        NIT, NHT = c.NIT, c.NHT

        def din(name, shape):
            return nc.dram_tensor(name, list(shape), F32, kind="ExternalInput").ap()

        self.x_in = din("x", [B, L, H])
        self.xq_in = din("xq", [B, TC, H])
        self.ipT_d = din("ipT", [H, 2 * IC])
        self.convw_d = din("convw", [IC, K])
        self.convb_d = din("convb", [IC, 1])
        self.xpT_d = din("xpT", [IC, R + 2 * N])
        self.dtpT_d = din("dtpT", [R, IC])
        self.dtb_d = din("dtb", [IC, 1])
        self.alog_d = din("alog", [IC, N])
        self.dvec_d = din("dvec", [IC, 1])
        self.opT_d = din("opT", [I, H])
        self.fc1T_d = din("fc1T", [E, H, FM])
        self.fc2T_d = din("fc2T", [E, H, FM])
        self.prT_d = din("prT", [E, FM, H])
        self.sf1T_d = din("sf1T", [H, FS])
        self.sf2T_d = din("sf2T", [H, FS])
        self.sprT_d = din("sprT", [FS, H])
        self.gwT_d = din("gwT", [H, E])
        self.sgwT_d = din("sgwT", [H, 1])
        self.cvec_d = din("c_in", [B, H])
        self.wmod_d = din("wmod", [H, c.MODSL])
        self.out_d = nc.dram_tensor("out", [B, TC, H], F32, kind="ExternalOutput").ap()

        def dint(name, shape, dt=F32):
            return nc.dram_tensor(name, list(shape), dt, kind="Internal")

        self.modsl_d = dint("modsl_d", [B, c.MODSL])
        self.modg_d = dint("modg_d", [NCORE * B, c.MODSL])
        self.modflat_d = dint("modflat_d", [B, 6 * H])
        self.ssmp_d = [dint(f"ssmp_d{b}", [R + 2 * N, L]) for b in range(B)]
        self.ssm_d = [dint(f"ssm_d{b}", [R + 2 * N, L]) for b in range(B)]
        self.usp_d = [dint(f"usp_d{b}", [IC, L]) for b in range(B)]
        self.gsp_d = [dint(f"gsp_d{b}", [IC, L]) for b in range(B)]
        self.a2ain_d = [dint(f"a2ain_d{b}", [NCORE * IC, TC], BF16) for b in range(B)]
        self.a2aout_d = [dint(f"a2aout_d{b}", [NCORE * IC, TC], BF16) for b in range(B)]
        self.cmb_d = [dint(f"cmb_d{b}", [E, TC]) for b in range(B)]
        self.x2_d = [dint(f"x2_d{b}", [TC, H]) for b in range(B)]
        self.h2b_d0 = dint("h2b_d0", [128, NHT * TC], BF16)
        self.rowsd = dint("rowsd", [1, 16384])

        if c.debug:
            def dout(name, shape):
                self.dbg[name] = nc.dram_tensor("dbg_" + name, list(shape), F32,
                                                kind="ExternalOutput").ap()
            dout("u", [B, IC, L]); dout("gate", [B, IC, L])
            dout("ssm", [B, R + 2 * N, L]); dout("dt", [B, IC, L])
            dout("y", [B, IC, L]); dout("x2", [B, TC, H])
            dout("h2", [B, H, TC]); dout("cmb", [B, E, TC])

        def sb(name, shape, dt=F32):
            return nc.alloc_sbuf_tensor(name, list(shape), dt)

        self.id_f32 = sb("id_f32", [P, P])
        self.onescol_bf = sb("onescol_bf", [P, 1], BF16)
        self.onesrow_f = sb("onesrow_f", [1, P])
        self.sa_bf = [sb(f"sa_bf{b}", [P, NHT], BF16) for b in range(B)]
        self.ca1_t = [sb(f"ca1_t{b}", [P, NHT]) for b in range(B)]
        self.A_sb = sb("A_sb", [P, NIT * N])
        self.D_sb = sb("D_sb", [P, NIT])
        self.dtb_sb = sb("dtb_sb", [P, NIT])
        self.convb_sb = sb("convb_sb", [P, NIT])
        self.cw_sb = sb("cw_sb", [P, NIT * K])
        self.diag = [[sb(f"diag_{it}_{k}", [P, P], BF16) for k in range(K)]
                     for it in range(NIT)]
        self.xpT_bf = [sb(f"xpT_bf{it}", [P, R + 2 * N], BF16) for it in range(NIT)]
        self.dtpT_bf = sb("dtpT_bf", [P, IC], BF16)
        self.gwT_sb = sb("gwT_sb", [P, NHT * E])
        self.sgwT_sb = sb("sgwT_sb", [P, NHT])
        self.h2rt = [sb(f"h2rt{j}", [P, P]) for j in range(4)]
        self.ubias = sb("ubias", [P, NIT])
        self.gbias = sb("gbias", [P, NIT])
        self.wsum = sb("wsum", [P, 2 * NIT])
        self.carry = sb("carry", [P, N * NIT])
        self.csT = sb("csT", [P, NHT * B])
        self.ts_bf = sb("ts_bf", [P, L], BF16)
        self.r_tm = sb("r_tm", [P, c.NTT * E])
        self.rt1 = sb("rt1", [P, c.NTT * E])
        self.scol = sb("scol", [P, 16])
        self.sgbc = sb("sgbc", [P, B * TC])
        self.cmbbc = sb("cmbbc", [P, B * TC])
        self.wst = [sb(f"wst{j}", [P, 1024], BF16) for j in range(2)]
        self.tev = sb("tev", [P, 512])
        self.hin = [sb(f"hin{it}", [P, 3 + c.ALC], BF16) for it in range(NIT)]
        self.rows = sb("rows", [1, 4096])
        self.AR1 = sb("AR1", [P, 8192])
        self.AR2 = sb("AR2", [P, 8192])
        self.AR3 = sb("AR3", [P, 8192])
        self.AR4 = sb("AR4", [P, 7808])
        self.AR5 = sb("AR5", [P, 3072])
        self.AR6 = sb("AR6", [P, 4096])
        self.csml = self.AR4[0:B, 0:H]
        self.modrow = self.AR4[0:B, H:H + c.MODSL]
        self.rsb = self.AR5[0:E, 2048:2048 + TC]
        self.cmbsb = self.rsb

        self.psA = nc.alloc_psum_tensor("psA", [P, 1024], F32)
        self.psB = nc.alloc_psum_tensor("psB", [P, 1024], F32)
        self.psC = nc.alloc_psum_tensor("psC", [P, 1024], F32)
        self.psD = nc.alloc_psum_tensor("psD", [P, 1024], F32)
        self.PSL = [self.psA[:, 0:512], self.psA[:, 512:1024],
                    self.psB[:, 0:512], self.psB[:, 512:1024],
                    self.psC[:, 0:512], self.psC[:, 512:1024],
                    self.psD[:, 0:512], self.psD[:, 512:1024]]

    def v32(self, ar, c0, n):
        return ar[:, c0:c0 + n]

    def vbf(self, ar, c0, nbf):
        return ar[:, c0:c0 + nbf // 2].bitcast(BF16)

    def bcast(self, dst, src_dram_row):
        import concourse.bass as _B
        _, a2 = _B.broadcast_tensor_aps(dst, src_dram_row)
        return a2

    def emit(self):
        c = self.c
        nc = self.nc
        with nc.Block() as block:
            self.block = block
            self.cd = Chain(nc, "cd", self.chains)
            self.cg = Chain(nc, "cg", self.chains)
            self.cgM = Chain(nc, "cgM", self.chains)
            self.cdM = Chain(nc, "cdM", self.chains)
            self.st = [Chain(nc, "st0", self.chains), Chain(nc, "st1", self.chains)]
            self.stg = [Chain(nc, f"stg{j}", self.chains) for j in range(6)]
            self.xs = Chain(nc, "xs", self.chains)
            self.spU = Chain(nc, "spU", self.chains)
            self.spG = Chain(nc, "spG", self.chains)
            self.ug = Chain(nc, "ug", self.chains)
            self.br = [Chain(nc, "br0", self.chains), Chain(nc, "br1", self.chains)]
            self.ssmW = Chain(nc, "ssmW", self.chains)
            self.ygW = Chain(nc, "ygW", self.chains)
            self.xq = [Chain(nc, "xq0", self.chains), Chain(nc, "xq1", self.chains)]
            self.ya = Chain(nc, "ya", self.chains)
            self.cr = [Chain(nc, "cr0", self.chains), Chain(nc, "cr1", self.chains)]
            self.outW = Chain(nc, "outW", self.chains)
            self.dbgc = Chain(nc, "dbgc", self.chains)
            self.dbg2 = Chain(nc, "dbg2", self.chains)
            self.dbgD = Chain(nc, "dbgD", self.chains)
            self.dbgY = Chain(nc, "dbgY", self.chains)
            self.cq = Chain(nc, "cq", self.chains)
            self.cv = Chain(nc, "cv", self.chains)
            self.ca = Chain(nc, "ca", self.chains)
            self.ct = Chain(nc, "ct", self.chains)
            self.cp = Chain(nc, "cp", self.chains)
            self.csc = Chain(nc, "csc", self.chains)
            self.rbc = Chain(nc, "rbc", self.chains)
            self.cx2 = Chain(nc, "cx2", self.chains)
            self.ch2 = Chain(nc, "ch2", self.chains)
            self.cbc = Chain(nc, "cbc", self.chains)
            self.ph_init()
            self.ph_mod()
            self.ph_wprep()
            for b in range(c.B):
                self.ph_A(b)
                self.ph_AR(b)
                self.ph_B(b)
                self.ph_C(b)
            self.ph_D()
            self.ph_final()
        return self.nc

    def ph_init(self):
        c, tok = self.c, self.tok
        cd, cg, cv, cp = self.cd, self.cg, self.cv, self.cp
        N, R, K, NIT, NHT, E = c.N, c.R, c.K, c.NIT, c.NHT, c.E
        self.xpT_f = self.v32(self.AR5, 0, NIT * (R + 2 * N))
        self.dtpT_f = self.v32(self.AR5, NIT * (R + 2 * N), c.IC)
        self.gw_f = self.v32(self.AR5, NIT * (R + 2 * N) + c.IC, NHT * E + NHT)

        def s_sync(e, m):
            cd.inc(e.dma_start(out=self.csml[0:c.B, :], in_=self.cvec_d), 16)
            for it in range(NIT):
                s = slice(it * P, (it + 1) * P)
                cd.inc(e.dma_start(out=self.A_sb[:, it * N:(it + 1) * N], in_=self.alog_d[s, :]), 16)
                cd.inc(e.dma_start(out=self.D_sb[:, it:it + 1], in_=self.dvec_d[s, :]), 16)
                cd.inc(e.dma_start(out=self.dtb_sb[:, it:it + 1], in_=self.dtb_d[s, :]), 16)
                cd.inc(e.dma_start(out=self.convb_sb[:, it:it + 1], in_=self.convb_d[s, :]), 16)
                cd.inc(e.dma_start(out=self.cw_sb[:, it * K:(it + 1) * K], in_=self.convw_d[s, :]), 16)
                cd.inc(e.dma_start(out=self.xpT_f[:, it * (R + 2 * N):(it + 1) * (R + 2 * N)],
                                   in_=self.xpT_d[s, :]), 16)
            cd.inc(e.dma_start(out=self.dtpT_f[:, :], in_=self.dtpT_d[0:P, :]), 16)
            for kt in range(NHT):
                s = slice(kt * P, (kt + 1) * P)
                cd.inc(e.dma_start(out=self.gw_f[:, kt * E:(kt + 1) * E], in_=self.gwT_d[s, :]), 16)
                cd.inc(e.dma_start(out=self.gw_f[:, NHT * E + kt:NHT * E + kt + 1],
                                   in_=self.sgwT_d[s, :]), 16)
            tok["init_in"] = cd.tok()
        self.on("sync", s_sync)

        def s_pre(e, m):
            t = cp.inc(e.memset(self.id_f32[:, :], 0.0))
            m.wait(t)
            cp.inc(e.affine_select(self.id_f32[:, :], self.id_f32[:, :], pattern=[[-1, P]],
                                   compare_op=ALU.not_equal, fill=1.0,
                                   base=0, channel_multiplier=1))
            tok["ident"] = cp.tok()
        self.on("gpsimd", s_pre)

        def s_v(e, m):
            cv.inc(e.memset(self.onescol_bf[:, :], 1.0))
            cv.inc(e.memset(self.onesrow_f[:, :], 1.0))
            tok["ones"] = cv.tok()
        self.on("vector", s_v)

    def ph_mod(self):
        c, tok = self.c, self.tok
        cd, cg, cq, cv, ca, ct = self.cd, self.cg, self.cq, self.cv, self.ca, self.ct
        B, H, NHT = c.B, c.H, c.NHT
        wmod_st = [self.v32(self.AR2, j * c.MODSL, c.MODSL) for j in range(2)]

        def s_a(e, m):
            m.wait(self.T("init_in"))
            tok["csil"] = ca.inc(e.activation(self.csml[0:B, :], self.csml[0:B, :], AF.Silu))
        self.on("scalar", s_a)

        def s_t(e, m):
            m.wait(self.T("csil")); m.wait(self.T("ident"))
            for kt in range(NHT):
                ct.inc(e.transpose(self.psA[0:P, kt * B:(kt + 1) * B],
                                   self.csml[0:B, kt * P:(kt + 1) * P], self.id_f32[0:B, 0:B]))
            tok["csT_ps"] = ct.tok()
        self.on("tensor", s_t)

        def s_v(e, m):
            m.wait(self.T("csT_ps"))
            tok["csT"] = cv.inc(e.tensor_copy(self.csT[:, :], self.psA[:, 0:NHT * B]))
        self.on("vector", s_v)

        nmj = (c.MODSL + 511) // 512
        for kt in range(NHT):
            def s_q(e, m, kt=kt):
                if kt >= 2:
                    m.wait(self.T(f"modmm{kt - 2}"))
                self.st[kt % 2].inc(e.dma_start(out=wmod_st[kt % 2][:, :],
                                                in_=self.wmod_d[kt * P:(kt + 1) * P, :]), 16)
                tok[f"wmodld{kt}"] = self.st[kt % 2].tok()
            self.on("scalar", s_q)

            def s_t2(e, m, kt=kt):
                if kt == 0:
                    m.wait(self.T("csT"))
                m.wait(self.T(f"wmodld{kt}"))
                for j in range(nmj):
                    j0, j1 = j * 512, min((j + 1) * 512, c.MODSL)
                    pj = [self.psB[0:B, 0:512], self.psB[0:B, 512:1024],
                          self.psC[0:B, 0:512], self.psC[0:B, 512:1024]][j]
                    ct.inc(e.matmul(pj[:, 0:j1 - j0],
                                    self.csT[:, kt * B:(kt + 1) * B],
                                    wmod_st[kt % 2][:, j0:j1],
                                    start=(kt == 0), stop=(kt == NHT - 1)))
                tok[f"modmm{kt}"] = ct.tok()
            self.on("tensor", s_t2)
        tok["mod_ps"] = None

        def s_v2(e, m):
            m.wait(self.T(f"modmm{NHT - 1}"))
            n1 = min(1024, c.MODSL)
            cv.inc(e.tensor_copy(self.modrow[0:B, 0:n1], self.psB[0:B, 0:n1]))
            if c.MODSL > 1024:
                cv.inc(e.tensor_copy(self.modrow[0:B, 1024:c.MODSL],
                                     self.psC[0:B, 0:c.MODSL - 1024]))
            tok["mod_sb"] = cv.tok()
        self.on("vector", s_v2)

        def s_s(e, m):
            m.wait(self.T("mod_sb"))
            self.cdM.inc(e.dma_start(out=self.modsl_d[:, :], in_=self.modrow[0:B, :]), 16)
            tok["modsl_w"] = self.cdM.tok()
        self.on("sync", s_s)

        def s_p(e, m):
            m.wait(self.T("modsl_w"))
            t = cg.inc(e.collective_compute(
                "AllGather", ALU.bypass, replica_groups=[list(range(NCORE))],
                ins=[self.modsl_d.ap().opt()], outs=[self.modg_d.ap().opt()]))
            m.wait(t)
            for r in range(NCORE):
                for b in range(B):
                    self.cgM.inc(e.dma_start(
                        out=self.modflat_d[b:b + 1, r * c.MODSL:(r + 1) * c.MODSL],
                        in_=self.modg_d[r * B + b:r * B + b + 1, :]), 16)
            tok["modflat"] = self.cgM.tok()
        self.on("gpsimd", s_p)

        def s_s2(e, m):
            m.wait(self.T("modflat"))
            for b in range(B):
                for kt in range(NHT):
                    self.cdM.inc(e.dma_start(
                        out=self.ca1_t[b][:, kt:kt + 1],
                        in_=self.modflat_d[b:b + 1, 1 * H + kt * P:1 * H + (kt + 1) * P]), 16)
                    self.cdM.inc(e.dma_start(
                        out=self.AR1[:, b * NHT + kt:b * NHT + kt + 1],
                        in_=self.modflat_d[b:b + 1, 0 * H + kt * P:0 * H + (kt + 1) * P]), 16)
            tok["modtiles"] = self.cdM.tok()
        self.on("sync", s_s2)

        def s_v3(e, m):
            m.wait(self.T("modtiles"))
            for b in range(B):
                cv.inc(e.tensor_scalar_add(self.ca1_t[b][:, :], self.ca1_t[b][:, :], 1.0))
                cv.inc(e.tensor_copy(self.sa_bf[b][:, :], self.AR1[:, b * NHT:(b + 1) * NHT]))
            tok["mod_done"] = cv.tok()
        self.on("vector", s_v3)

    def emit_clamp(self, b, off, scr, tok_in, tok_out):
        c, cv, ca, tok = self.c, self.cv, self.ca, self.tok
        H = c.H
        g = self.rows[0:1, off:off + H]
        S = self.rows[0:1, scr:scr + H]

        def s_sign(e, m):
            m.wait(tok_in() if callable(tok_in) else self.T(tok_in))
            tok[tok_out + "_sg"] = ca.inc(e.sign(S, g))
        self.on("scalar", s_sign)

        def s_v(e, m):
            m.wait(self.T(tok_out + "_sg"))
            t = cv.inc(e.tensor_mul(g, g, S))
            m.wait(t)
            t = cv.inc(e.tensor_scalar_max(g, g, 0.25))
            m.wait(t)
            tok[tok_out] = cv.inc(e.tensor_mul(g, g, S))
        self.on("vector", s_v)

    def ph_wprep(self):
        c, tok = self.c, self.tok
        cv, ca, cp = self.cv, self.ca, self.cp
        N, R, K, NIT, NHT, E = c.N, c.R, c.K, c.NIT, c.NHT, c.E

        def s_a(e, m):
            m.wait(self.T("init_in"))
            tok["A_exp"] = ca.inc(e.activation(self.A_sb[:, 0:NIT * N],
                                               self.A_sb[:, 0:NIT * N], AF.Exp))
        self.on("scalar", s_a)

        def s_v(e, m):
            m.wait(self.T("A_exp"))
            cv.inc(e.tensor_scalar_mul(self.A_sb[:, 0:NIT * N], self.A_sb[:, 0:NIT * N], -1.0))
            m.wait(self.T("init_in"))
            for it in range(NIT):
                cv.inc(e.tensor_copy(self.xpT_bf[it][:, :],
                                     self.xpT_f[:, it * (R + 2 * N):(it + 1) * (R + 2 * N)]))
            cv.inc(e.tensor_copy(self.dtpT_bf[:, :], self.dtpT_f[:, :]))
            cv.inc(e.tensor_copy(self.gwT_sb[:, :], self.gw_f[:, 0:NHT * E]))
            cv.inc(e.tensor_copy(self.sgwT_sb[:, :], self.gw_f[:, NHT * E:NHT * E + NHT]))
            m.wait(self.T("ones")); m.wait(self.T("ident"))
            for it in range(NIT):
                for k in range(K):
                    cv.inc(e.tensor_scalar(
                        self.diag[it][k][:, :], self.id_f32[:, :],
                        self.cw_sb[:, it * K + k:it * K + k + 1], None, ALU.mult))
            tok["diag"] = cv.tok()
        self.on("vector", s_v)

    def ph_A(self, b):
        c, tok = self.c, self.tok
        cd, cq, cv, ca, ct = self.cd, self.cq, self.cv, self.ca, self.ct
        B, L, H, N, R, K = c.B, c.L, c.H, c.N, c.R, c.K
        NIT, NHT, IC, ALC, NALC, NAT = c.NIT, c.NHT, c.IC, c.ALC, c.NALC, c.NAT
        psA, psB, psC, psD = self.psA, self.psB, self.psC, self.psD
        Wb = self.vbf(self.AR1, 0, NHT * 2 * IC)
        ipT_st = [self.v32(self.AR2, j * 2 * IC, 2 * IC) for j in range(2)]
        x_tm = self.v32(self.AR2, 0, NAT * H)
        xb_fm = self.vbf(self.AR6, 0, NHT * ALC)
        ssm_ts = self.v32(self.AR3, 0, L)
        ssm_bc = self.AR3[0:2 * N, L:2 * L]
        u_f = self.v32(self.AR4, 0, NIT * ALC)
        g_f = self.v32(self.AR4, NIT * ALC, NIT * ALC)
        u_bf = self.vbf(self.AR4, 2 * NIT * ALC, NIT * ALC)
        o = 2 * NIT * ALC + NIT * ALC // 2
        r_bc_s = [self.v32(self.AR4, o + j * 2 * ALC, ALC) for j in range(2)]
        nmur_bc_s = [self.v32(self.AR4, o + j * 2 * ALC + ALC, ALC) for j in range(2)]
        sq_bf = [self.vbf(self.AR5, 1536 + j * ALC // 2, ALC) for j in range(2)]
        self._ssm_views = (ssm_ts, ssm_bc)
        rows = self.rows
        RMU, RM2, RT, RR_, RNM = 0, ALC, 2 * ALC, 3 * ALC, 4 * ALC

        for kt in range(NHT):
            def s_ld(e, m, kt=kt):
                if kt == 0:
                    m.wait(self.T("mod_done"))
                    if b > 0:
                        m.wait(self.T(f"x2w{b - 1}_{c.NTT - 1}"))
                        m.wait(self.T("h2bw0"))
                        m.wait(self.T(f"cmbw{b - 1}"))
                        m.wait(self.T(f"sgbc{b - 1}"))
                        if c.debug:
                            m.wait((self.dbgc, self.dbgc.count))
                if kt >= 2:
                    m.wait(self.T(f"wbs{b}_{kt - 2}"))
                self.st[kt % 2].inc(e.dma_start(out=ipT_st[kt % 2][:, :],
                                                in_=self.ipT_d[kt * P:(kt + 1) * P, :]), 16)
                tok[f"ipld{b}_{kt}"] = self.st[kt % 2].tok()
            self.on("sync", s_ld)

            def s_sc(e, m, kt=kt):
                m.wait(self.T(f"ipld{b}_{kt}"))
                tok[f"wbs{b}_{kt}"] = cv.inc(e.tensor_scalar(
                    Wb[:, kt * 2 * IC:(kt + 1) * 2 * IC], ipT_st[kt % 2][:, :],
                    self.ca1_t[b][:, kt:kt + 1], None, ALU.mult))
            self.on("vector", s_sc)
        tok[f"Wb{b}"] = self.cv.tok()

        def s_t(e, m):
            m.wait(self.T(f"Wb{b}"))
            if b > 0:
                m.wait(self.T(f"cmbw{b - 1}"))
            for mt in range(2 * NIT):
                for kt in range(NHT):
                    ct.inc(e.matmul(psB[0:P, mt:mt + 1],
                                    Wb[:, kt * 2 * IC + mt * P:kt * 2 * IC + (mt + 1) * P],
                                    self.sa_bf[b][:, kt:kt + 1],
                                    start=(kt == 0), stop=(kt == NHT - 1)))
            for mt in range(2 * NIT):
                for kt in range(NHT):
                    ct.inc(e.matmul(psB[0:P, 512 + mt:512 + mt + 1],
                                    Wb[:, kt * 2 * IC + mt * P:kt * 2 * IC + (mt + 1) * P],
                                    self.onescol_bf[:, :],
                                    start=(kt == 0), stop=(kt == NHT - 1)))
            tok[f"bias_mm{b}"] = ct.tok()
        self.on("tensor", s_t)

        def s_v2(e, m):
            m.wait(self.T(f"bias_mm{b}"))
            cv.inc(e.tensor_copy(self.wsum[:, :], psB[:, 512:512 + 2 * NIT]))
            for it in range(NIT):
                t = cv.inc(e.reduce_sum(self.scol[:, 8 + it:9 + it],
                                        self.cw_sb[:, it * K:(it + 1) * K], axis=AXX))
                m.wait(t)
                cv.inc(e.scalar_tensor_tensor(
                    self.ubias[:, it:it + 1], psB[:, it:it + 1],
                    self.scol[:, 8 + it:9 + it], self.convb_sb[:, it:it + 1],
                    ALU.mult, ALU.add))
            tok[f"gb_ev{b}"] = cv.inc(e.tensor_copy(self.gbias[:, :], psB[:, NIT:2 * NIT]))
            for it in range(NIT):
                cv.inc(e.memset(self.hin[it][:, 0:3], 0.0))
            cv.inc(e.memset(self.carry[:, :], 0.0))
            tok[f"Aprep{b}"] = cv.tok()
        self.on("vector", s_v2)

        trps = [psA[:, 0:512], psA[:, 512:1024]]
        for lc in range(NALC):
            par = lc % 2
            lc0 = lc * ALC
            r_bc = r_bc_s[par]
            nmur_bc = nmur_bc_s[par]

            def s_xld(e, m, lc=lc, lc0=lc0):
                if lc == 0:
                    m.wait(self.T(f"Aprep{b}"))
                if lc >= 1:
                    m.wait(self.T(f"tr{b}_{lc - 1}"))
                for tt in range(NAT):
                    self.xs.inc(e.dma_start(out=x_tm[:, tt * H:(tt + 1) * H],
                                            in_=self.x_in[b, lc0 + tt * P:lc0 + (tt + 1) * P, :]), 16)
                tok[f"xld{b}_{lc}"] = self.xs.tok()
            self.on("sync", s_xld)

            def mk_st(lc, ht):
                def s_st(e, m, lc=lc, ht=ht):
                    if ht == 0:
                        if lc >= 1:
                            m.wait(self.T(f"var{b}_{lc - 1}"))
                            m.wait(self.T(f"xpev{b}_{lc - 1}"))
                    m.wait(self.T(f"xev{b}_{lc}_{ht}"))
                    ct.inc(e.matmul(psC[0:1, 0:ALC], self.onescol_bf[:, :],
                                    xb_fm[:, ht * ALC:(ht + 1) * ALC],
                                    start=(ht == 0), stop=(ht == NHT - 1)))
                    m.wait(self.T(f"sq{b}_{lc}_{ht}"))
                    ct.inc(e.matmul(psC[0:1, 512:512 + ALC], self.onescol_bf[:, :],
                                    sq_bf[ht % 2][:, :],
                                    start=(ht == 0), stop=(ht == NHT - 1)))
                    tok[f"sqmm{b}_{lc}_{ht}"] = ct.tok()
                self.on("tensor", s_st)

            for ht in range(NHT):
                g = lc * NHT + ht
                trp = trps[g % 2]

                def s_tr(e, m, lc=lc, ht=ht, trp=trp, g=g):
                    if ht == 0:
                        m.wait(self.T(f"xld{b}_{lc}"))
                    if g >= 2:
                        pl, ph = divmod(g - 2, NHT)
                        m.wait(self.T(f"xev{b}_{pl}_{ph}"))
                    for tt in range(NAT):
                        ct.inc(e.transpose(trp[:, tt * P:(tt + 1) * P],
                                           x_tm[:, tt * H + ht * P:tt * H + (ht + 1) * P],
                                           self.id_f32[:, :]))
                    tok[f"tr{b}_{lc}_{ht}"] = ct.tok()
                self.on("tensor", s_tr)

                def s_xev(e, m, lc=lc, ht=ht, trp=trp):
                    m.wait(self.T(f"tr{b}_{lc}_{ht}"))
                    if lc >= 1 and ht == 0:
                        m.wait(self.T(f"ipmm{b}_{lc - 1}_{2 * NIT - 1}"))
                    tok[f"xev{b}_{lc}_{ht}"] = ca.inc(e.activation(
                        xb_fm[:, ht * ALC:(ht + 1) * ALC], trp[:, 0:ALC], AF.Copy))
                    if ht >= 2:
                        m.wait(self.T(f"sqmm{b}_{lc}_{ht - 2}"))
                    m.wait(self.T(f"xev{b}_{lc}_{ht}"))
                    tok[f"sq{b}_{lc}_{ht}"] = ca.inc(e.activation(
                        sq_bf[ht % 2][:, :], xb_fm[:, ht * ALC:(ht + 1) * ALC], AF.Square))
                self.on("scalar", s_xev)

                if ht >= 1:
                    mk_st(lc, ht - 1)
            mk_st(lc, NHT - 1)
            tok[f"tr{b}_{lc}"] = self.ct.tok()
            tok[f"statmm{b}_{lc}"] = self.ct.tok()

            def s_row(e, m, lc=lc):
                m.wait(self.T(f"statmm{b}_{lc}"))
                cv.inc(e.tensor_scalar_mul(rows[0:1, RMU:RMU + ALC], psC[0:1, 0:ALC], 1.0 / H))
                t = cv.inc(e.tensor_scalar_mul(rows[0:1, RM2:RM2 + ALC],
                                               psC[0:1, 512:512 + ALC], 1.0 / H))
                m.wait(t)
                t = cv.inc(e.tensor_mul(rows[0:1, RT:RT + ALC],
                                        rows[0:1, RMU:RMU + ALC], rows[0:1, RMU:RMU + ALC]))
                m.wait(t)
                t = cv.inc(e.tensor_sub(
                    rows[0:1, RM2:RM2 + ALC], rows[0:1, RM2:RM2 + ALC], rows[0:1, RT:RT + ALC]))
                m.wait(t)
                tok[f"var{b}_{lc}"] = cv.inc(e.tensor_scalar_add(
                    rows[0:1, RM2:RM2 + ALC], rows[0:1, RM2:RM2 + ALC], 1e-5))
            self.on("vector", s_row)

            def s_ln(e, m, lc=lc):
                m.wait(self.T(f"var{b}_{lc}"))
                tok[f"lnv{b}_{lc}"] = ca.inc(e.activation(
                    rows[0:1, RT:RT + ALC], rows[0:1, RM2:RM2 + ALC], AF.Ln))
            self.on("scalar", s_ln)

            def s_row2(e, m, lc=lc):
                m.wait(self.T(f"lnv{b}_{lc}"))
                tok[f"nl{b}_{lc}"] = cv.inc(e.tensor_scalar_mul(
                    rows[0:1, RT:RT + ALC], rows[0:1, RT:RT + ALC], -0.5))
            self.on("vector", s_row2)

            def s_ex(e, m, lc=lc):
                m.wait(self.T(f"nl{b}_{lc}"))
                tok[f"rstd{b}_{lc}"] = ca.inc(e.activation(
                    rows[0:1, RR_:RR_ + ALC], rows[0:1, RT:RT + ALC], AF.Exp))
            self.on("scalar", s_ex)

            def s_row3(e, m, lc=lc):
                m.wait(self.T(f"rstd{b}_{lc}"))
                t = cv.inc(e.tensor_mul(rows[0:1, RNM:RNM + ALC],
                                        rows[0:1, RMU:RMU + ALC], rows[0:1, RR_:RR_ + ALC]))
                m.wait(t)
                tok[f"nmur{b}_{lc}"] = cv.inc(e.tensor_scalar_mul(
                    rows[0:1, RNM:RNM + ALC], rows[0:1, RNM:RNM + ALC], -1.0))
            self.on("vector", s_row3)

            RD0 = 8192 // 2 + (lc % 2) * 2 * ALC
            def s_rbc(e, m, lc=lc, RD0=RD0):
                m.wait(self.T(f"rstd{b}_{lc}"))
                if lc >= 2:
                    m.wait(self.T(f"ipev{b}_{lc - 2}_{2 * NIT - 1}"))
                self.rbc.inc(e.dma_start(out=self.rowsd[0:1, RD0:RD0 + ALC],
                                         in_=rows[0:1, RR_:RR_ + ALC]), 16)
                m.wait(self.T(f"nmur{b}_{lc}"))
                self.rbc.inc(e.dma_start(out=self.rowsd[0:1, RD0 + ALC:RD0 + 2 * ALC],
                                         in_=rows[0:1, RNM:RNM + ALC]), 16)
                t = (self.rbc, self.rbc.count)
                m.wait(t)
                self.rbc.inc(e.dma_start(out=r_bc[:, :], in_=self.bcast(
                    r_bc[:, :], self.rowsd[0:1, RD0:RD0 + ALC])), 16)
                self.rbc.inc(e.dma_start(out=nmur_bc[:, :], in_=self.bcast(
                    nmur_bc[:, :], self.rowsd[0:1, RD0 + ALC:RD0 + 2 * ALC])), 16)
                tok[f"rnbc{b}_{lc}"] = self.rbc.tok()
            self.on("sync", s_rbc)

            for mt in range(2 * NIT):
                pp = psB[:, 0:512] if mt % 2 == 0 else psB[:, 512:1024]

                def s_ip(e, m, lc=lc, mt=mt, pp=pp):
                    if lc == 0 and mt == 0:
                        m.wait(self.T(f"bias_mm{b}"))
                    if mt >= 2:
                        m.wait(self.T(f"ipev{b}_{lc}_{mt - 2}"))
                    elif lc >= 1:
                        m.wait(self.T(f"ipev{b}_{lc - 1}_{2 * NIT - 2 + mt}"))
                    for kt in range(NHT):
                        ct.inc(e.matmul(pp[:, 0:ALC],
                                        Wb[:, kt * 2 * IC + mt * P:kt * 2 * IC + (mt + 1) * P],
                                        xb_fm[:, kt * ALC:(kt + 1) * ALC],
                                        start=(kt == 0), stop=(kt == NHT - 1)))
                    tok[f"ipmm{b}_{lc}_{mt}"] = ct.tok()
                self.on("tensor", s_ip)

                def s_ipe(e, m, lc=lc, mt=mt, pp=pp, r_bc=r_bc, nmur_bc=nmur_bc):
                    m.wait(self.T(f"ipmm{b}_{lc}_{mt}"))
                    m.wait(self.T(f"rnbc{b}_{lc}"))
                    if mt >= 1:
                        m.wait(self.T(f"ipev{b}_{lc}_{mt - 1}"))
                    elif lc >= 1:
                        m.wait(self.T(f"ipev{b}_{lc - 1}_{2 * NIT - 1}"))
                    t = cv.inc(e.tensor_mul(self.tev[:, 0:ALC], pp[:, 0:ALC], r_bc[:, :]))
                    m.wait(t)
                    if mt < NIT:
                        it = mt
                        if lc >= 1:
                            m.wait(self.T(f"usil{b}_{lc - 1}_{it}"))
                            m.wait(self.T(f"halo{b}_{lc - 1}_{it}"))
                        tok[f"ipev{b}_{lc}_{mt}"] = cv.inc(e.scalar_tensor_tensor(
                            self.hin[it][:, 3:3 + ALC], nmur_bc[:, :],
                            self.wsum[:, mt:mt + 1], self.tev[:, 0:ALC],
                            ALU.mult, ALU.add))
                    else:
                        it = mt - NIT
                        if lc >= 1:
                            m.wait(self.T(f"gs{b}_{lc - 1}_{it}"))
                        if lc >= 1:
                            m.wait(self.T(f"gspill{b}_{lc - 1}"))
                        tok[f"ipev{b}_{lc}_{mt}"] = cv.inc(e.scalar_tensor_tensor(
                            g_f[:, it * ALC:(it + 1) * ALC], nmur_bc[:, :],
                            self.wsum[:, mt:mt + 1], self.tev[:, 0:ALC],
                            ALU.mult, ALU.add))
                        tok[f"gpre{b}_{lc}_{it}"] = tok[f"ipev{b}_{lc}_{mt}"]
                self.on("vector", s_ipe)

                if mt >= NIT:
                    it = mt - NIT

                    def s_gs(e, m, lc=lc, it=it):
                        if lc == 0 and it == 0:
                            m.wait(self.T(f"gb_ev{b}"))
                        m.wait(self.T(f"gpre{b}_{lc}_{it}"))
                        tok[f"gs{b}_{lc}_{it}"] = ca.inc(e.activation(
                            g_f[:, it * ALC:(it + 1) * ALC],
                            g_f[:, it * ALC:(it + 1) * ALC],
                            AF.Silu, bias=self.gbias[:, it:it + 1]))
                    self.on("scalar", s_gs)

            for it in range(NIT):
                gc = lc * NIT + it
                cpp = psD[:, 0:ALC] if gc % 2 == 0 else psD[:, 512:512 + ALC]

                def s_cv(e, m, lc=lc, it=it, gc=gc, cpp=cpp):
                    if lc == 0 and it == 0:
                        m.wait(self.T("diag"))
                    m.wait(self.T(f"ipev{b}_{lc}_{it}"))
                    if gc >= 2:
                        pl, pi = divmod(gc - 2, NIT)
                        m.wait(self.T(f"uev{b}_{pl}_{pi}"))
                    if gc % 2 == 1 and lc >= 1 and (it == 1 or NIT == 1):
                        m.wait(self.T(f"xpev{b}_{lc - 1}"))
                    for k in range(K):
                        ct.inc(e.matmul(cpp, self.diag[it][k][:, :],
                                        self.hin[it][:, k:k + ALC],
                                        start=(k == 0), stop=(k == K - 1)))
                    tok[f"conv{b}_{lc}_{it}"] = ct.tok()
                self.on("tensor", s_cv)

                def s_us(e, m, lc=lc, it=it, cpp=cpp):
                    m.wait(self.T(f"conv{b}_{lc}_{it}"))
                    if lc >= 1 and it == 0:
                        m.wait(self.T(f"uspill{b}_{lc - 1}"))
                    t = ca.inc(e.activation(u_f[:, it * ALC:(it + 1) * ALC],
                                            cpp, AF.Silu,
                                            bias=self.ubias[:, it:it + 1]))
                    tok[f"uev{b}_{lc}_{it}"] = t
                    tok[f"usil{b}_{lc}_{it}"] = t
                self.on("scalar", s_us)

                def s_ub(e, m, lc=lc, it=it):
                    m.wait(self.T(f"uev{b}_{lc}_{it}"))
                    tok[f"ubf{b}_{lc}_{it}"] = cv.inc(e.tensor_copy(
                        u_bf[:, it * ALC:(it + 1) * ALC], u_f[:, it * ALC:(it + 1) * ALC]))
                    tok[f"halo{b}_{lc}_{it}"] = cv.inc(e.tensor_copy(
                        self.hin[it][:, 0:3], self.hin[it][:, ALC:ALC + 3]))
                self.on("vector", s_ub)

            def s_xp(e, m, lc=lc):
                if lc >= 1:
                    m.wait(self.T(f"xpev{b}_{lc - 1}"))
                m.wait(self.T(f"var{b}_{lc}"))
                for it in range(NIT):
                    m.wait(self.T(f"ubf{b}_{lc}_{it}"))
                    m.wait(self.T(f"uev{b}_{lc}_{it}"))
                for it in range(NIT):
                    ct.inc(e.matmul(psD[:, 512:512 + ALC], self.xpT_bf[it][:, 0:R],
                                    u_bf[:, it * ALC:(it + 1) * ALC],
                                    start=(it == 0), stop=(it == NIT - 1)))
                for it in range(NIT):
                    ct.inc(e.matmul(psC[0:2 * N, 512:512 + ALC],
                                    self.xpT_bf[it][:, R:R + 2 * N],
                                    u_bf[:, it * ALC:(it + 1) * ALC],
                                    start=(it == 0), stop=(it == NIT - 1)))
                tok[f"xpmm{b}_{lc}"] = ct.tok()
            self.on("tensor", s_xp)


            def s_xpe(e, m, lc=lc, lc0=lc0):
                m.wait(self.T(f"xpmm{b}_{lc}"))
                cv.inc(e.tensor_copy(ssm_ts[:, lc0:lc0 + ALC], psD[:, 512:512 + ALC]))
                tok[f"xpev{b}_{lc}"] = cv.inc(e.tensor_copy(
                    ssm_bc[:, lc0:lc0 + ALC], psC[0:2 * N, 512:512 + ALC]))
            self.on("vector", s_xpe)

            def s_sp(e, m, lc=lc, lc0=lc0):
                for it in range(NIT):
                    m.wait(self.T(f"uev{b}_{lc}_{it}"))
                    self.spU.inc(e.dma_start(
                        out=self.usp_d[b][it * P:(it + 1) * P, lc0:lc0 + ALC],
                        in_=u_f[:, it * ALC:(it + 1) * ALC]), 16)
                    if c.debug:
                        self.spU.inc(e.dma_start(
                            out=self.dbg["u"][b, it * P:(it + 1) * P, lc0:lc0 + ALC],
                            in_=u_f[:, it * ALC:(it + 1) * ALC]), 16)
                tok[f"uspill{b}_{lc}"] = self.spU.tok()
                for it in range(NIT):
                    m.wait(self.T(f"gs{b}_{lc}_{it}"))
                    self.spG.inc(e.dma_start(
                        out=self.gsp_d[b][it * P:(it + 1) * P, lc0:lc0 + ALC],
                        in_=g_f[:, it * ALC:(it + 1) * ALC]), 16)
                    if c.debug:
                        self.spG.inc(e.dma_start(
                            out=self.dbg["gate"][b, it * P:(it + 1) * P, lc0:lc0 + ALC],
                            in_=g_f[:, it * ALC:(it + 1) * ALC]), 16)
                tok[f"gspill{b}_{lc}"] = self.spG.tok()
            self.on("sync", s_sp)

    def ph_AR(self, b):
        c, tok = self.c, self.tok
        cd, cg, cv = self.cd, self.cg, self.cv
        N, R, L = c.N, c.R, c.L
        ssm_ts, ssm_bc = self._ssm_views

        def s_s(e, m):
            m.wait(self.T(f"xpev{b}_{c.NALC - 1}"))
            self.ssmW.inc(e.dma_start(out=self.ssmp_d[b][0:R, :], in_=ssm_ts[:, :]), 16)
            self.ssmW.inc(e.dma_start(out=self.ssmp_d[b][R:R + 2 * N, :], in_=ssm_bc[:, :]), 16)
            tok[f"ssmp_w{b}"] = self.ssmW.tok()
        self.on("sync", s_s)

        def s_p(e, m):
            m.wait(self.T(f"ssmp_w{b}"))
            tok[f"AR{b}"] = cg.inc(e.collective_compute(
                "AllReduce", ALU.add, replica_groups=[list(range(NCORE))],
                ins=[self.ssmp_d[b].ap().opt()], outs=[self.ssm_d[b].ap().opt()]))
        self.on("gpsimd", s_p)

        def s_s2(e, m):
            m.wait(self.T(f"AR{b}"))
            self.ssmW.inc(e.dma_start(out=ssm_ts[:, :], in_=self.ssm_d[b][0:R, :]), 16)
            tok[f"ssm_ld{b}"] = self.ssmW.tok()
            if c.debug:
                self.dbgc.inc(e.dma_start(out=self.dbg["ssm"][b, :, :], in_=self.ssm_d[b][:, :]), 16)
        self.on("sync", s_s2)

        def s_v(e, m):
            m.wait(self.T(f"ssm_ld{b}"))
            tok[f"tsbf{b}"] = cv.inc(e.tensor_copy(self.ts_bf[:, :], ssm_ts[:, :]))
        self.on("vector", s_v)

    def ph_B(self, b):
        c, tok = self.c, self.tok
        cd, cq, cv, ca, ct, cg, cp = self.cd, self.cq, self.cv, self.ca, self.ct, self.cg, self.cp
        N, R, L, NIT, IC, SLC, NSLC, TC = c.N, c.R, c.L, c.NIT, c.IC, c.SLC, c.NSLC, c.TC
        psA, psB, psC, psD = self.psA, self.psB, self.psC, self.psD
        dt_ch = self.v32(self.AR2, 0, NIT * SLC)
        dtu_ch = self.v32(self.AR2, NIT * SLC, NIT * SLC)
        y_ch = self.v32(self.AR2, 2 * NIT * SLC, NIT * SLC)
        u_ch = self.v32(self.AR3, 0, NIT * SLC)
        g_ch = self.v32(self.AR3, NIT * SLC, NIT * SLC)
        Bbc = [self.v32(self.AR3, 2 * NIT * SLC + j * SLC, SLC) for j in range(2)]
        Cbc = [self.v32(self.AR3, 2 * NIT * SLC + 2 * SLC + j * SLC, SLC) for j in range(2)]
        h_t = self.v32(self.AR3, 2 * NIT * SLC + 4 * SLC, SLC)
        ytmp = self.v32(self.AR3, 2 * NIT * SLC + 5 * SLC, SLC)
        dA_t = [psA[:, 0:512], psA[:, 512:1024], psC[:, 0:512], psC[:, 512:1024]]
        dBx_t = [self.v32(self.AR5, j * SLC, SLC) for j in range(4)]
        yg_ch = self.vbf(self.AR5, 4 * SLC, NIT * SLC)
        csc = self.csc
        cscbase = csc.count

        def scw(slc, n, it):
            return (csc, cscbase + slc * N * NIT + n * NIT + it + 1)

        for slc in range(NSLC):
            l0 = slc * SLC

            def s_ld(e, m, slc=slc, l0=l0):
                if slc == 0:
                    m.wait(self.T(f"gspill{b}_{c.NALC - 1}"))
                    m.wait(self.T(f"uspill{b}_{c.NALC - 1}"))
                    m.wait(self.T(f"tsbf{b}"))
                if slc >= 1:
                    m.wait(self.T(f"ygv{b}_{slc - 1}"))
                for it in range(NIT):
                    self.ug.inc(e.dma_start(out=u_ch[:, it * SLC:(it + 1) * SLC],
                                            in_=self.usp_d[b][it * P:(it + 1) * P, l0:l0 + SLC]), 16)
                    self.ug.inc(e.dma_start(out=g_ch[:, it * SLC:(it + 1) * SLC],
                                            in_=self.gsp_d[b][it * P:(it + 1) * P, l0:l0 + SLC]), 16)
                tok[f"ugld{b}_{slc}"] = self.ug.tok()
            self.on("sync", s_ld)

            for it in range(NIT):
                jg0 = slc * NIT + it
                pp = psB[:, 0:512] if jg0 % 2 == 0 else psB[:, 512:1024]

                def s_dm(e, m, slc=slc, l0=l0, it=it, pp=pp):
                    if slc == 0 and it == 0:
                        m.wait(self.T(f"tsbf{b}"))
                    jg = slc * NIT + it
                    if jg >= 2:
                        ps_, pi_ = divmod(jg - 2, NIT)
                        m.wait(self.T(f"dte{b}_{ps_}_{pi_}"))
                    ct.inc(e.matmul(pp[:, 0:SLC],
                                    self.dtpT_bf[:, it * P:(it + 1) * P],
                                    self.ts_bf[:, l0:l0 + SLC],
                                    start=True, stop=True))
                    tok[f"dtmm{b}_{slc}_{it}"] = ct.tok()
                self.on("tensor", s_dm)

                def s_de(e, m, slc=slc, it=it, pp=pp):
                    m.wait(self.T(f"dtmm{b}_{slc}_{it}"))
                    if slc >= 1 and it == 0:
                        m.wait(self.T(f"dtln{b}_{slc - 1}"))
                        m.wait(scw(slc - 1, N - 1, NIT - 1))
                        if c.debug:
                            m.wait(self.T(f"dtdump{b}_{slc - 1}"))
                    tok[f"dte{b}_{slc}_{it}"] = cv.inc(e.tensor_scalar(
                        dt_ch[:, it * SLC:(it + 1) * SLC],
                        pp[:, 0:SLC], self.dtb_sb[:, it:it + 1], 85.0, ALU.add, ALU.min))
                self.on("vector", s_de)
            tok[f"dtev{b}_{slc}"] = self.cv.tok()

            def s_sp(e, m, slc=slc):
                m.wait(self.T(f"dtev{b}_{slc}"))
                if slc >= 1:
                    m.wait(scw(slc - 1, N - 1, NIT - 1))
                t = ca.inc(e.activation(dtu_ch[:, :], dt_ch[:, :], AF.Exp))
                m.wait(t)
                tok[f"dtln{b}_{slc}"] = ca.inc(e.activation(dt_ch[:, :], dtu_ch[:, :],
                                                            AF.Ln, bias=1.0))
            self.on("scalar", s_sp)

            def s_du(e, m, slc=slc):
                m.wait(self.T(f"dtln{b}_{slc}")); m.wait(self.T(f"ugld{b}_{slc}"))
                tok[f"dtu{b}_{slc}"] = cv.inc(e.tensor_mul(dtu_ch[:, :], dt_ch[:, :], u_ch[:, :]))
            self.on("vector", s_du)

            if c.debug:
                def s_dd(e, m, slc=slc, l0=l0):
                    m.wait(self.T(f"dtln{b}_{slc}"))
                    for it in range(NIT):
                        self.dbgD.inc(e.dma_start(
                            out=self.dbg["dt"][b, it * P:(it + 1) * P, l0:l0 + SLC],
                            in_=dt_ch[:, it * SLC:(it + 1) * SLC]), 16)
                    tok[f"dtdump{b}_{slc}"] = self.dbgD.tok()
                self.on("sync", s_dd)

            def s_bc(e, m, slc=slc, l0=l0):
                if slc == 0:
                    m.wait(self.T(f"AR{b}"))
                for n in range(N):
                    if n >= 2:
                        m.wait(scw(slc, n - 2, NIT - 1))
                    elif slc >= 1:
                        m.wait(scw(slc - 1, N - 2 + n, NIT - 1))
                    elif b > 0 and n < 2:
                        m.wait((csc, cscbase))
                    self.br[n % 2].inc(e.dma_start(
                        out=Bbc[n % 2][:, :],
                        in_=self.bcast(Bbc[n % 2][:, :],
                                       self.ssm_d[b][R + n:R + n + 1, l0:l0 + SLC])), 16)
                    self.br[n % 2].inc(e.dma_start(
                        out=Cbc[n % 2][:, :],
                        in_=self.bcast(Cbc[n % 2][:, :],
                                       self.ssm_d[b][R + N + n:R + N + n + 1, l0:l0 + SLC])), 16)
                    tok[f"bcd{b}_{slc}_{n}"] = (self.br[n % 2], self.br[n % 2].count)
            self.on("sync", s_bc)

            def s_dbx(e, m, slc=slc):
                m.wait(self.T(f"dtu{b}_{slc}"))
                for n in range(N):
                    m.wait(self.T(f"bcd{b}_{slc}_{n}"))
                    for it in range(NIT):
                        j = n * NIT + it
                        if j >= 4:
                            m.wait(scw(slc, (j - 4) // NIT, (j - 4) % NIT))
                        elif slc >= 1:
                            m.wait(scw(slc - 1, (N * NIT - 4 + j) // NIT, (N * NIT - 4 + j) % NIT))
                        tok[f"dbx{b}_{slc}_{n}_{it}"] = cp.inc(e.tensor_mul(
                            dBx_t[j % 4][:, :], dtu_ch[:, it * SLC:(it + 1) * SLC],
                            Bbc[n % 2][:, :]))
            self.on("gpsimd", s_dbx)

            def s_dA(e, m, slc=slc):
                m.wait(self.T(f"dtln{b}_{slc}"))
                if slc == 0:
                    m.wait(self.T(f"xev{b}_{c.NALC - 1}_{c.NHT - 1}"))
                    m.wait(self.T(f"var{b}_{c.NALC - 1}"))
                    m.wait(self.T(f"xpev{b}_{c.NALC - 1}"))
                for n in range(N):
                    for it in range(NIT):
                        j = n * NIT + it
                        if j >= 4:
                            m.wait(scw(slc, (j - 4) // NIT, (j - 4) % NIT))
                        elif slc >= 1:
                            m.wait(scw(slc - 1, (N * NIT - 4 + j) // NIT, (N * NIT - 4 + j) % NIT))
                        tok[f"dA{b}_{slc}_{n}_{it}"] = ca.inc(e.activation(
                            dA_t[j % 4][:, :], dt_ch[:, it * SLC:(it + 1) * SLC], AF.Exp,
                            scale=self.A_sb[:, it * N + n:it * N + n + 1]))
            self.on("scalar", s_dA)

            def s_sc(e, m, slc=slc):
                if c.debug and slc >= 1:
                    m.wait(self.T(f"ydump{b}_{slc - 1}"))
                for n in range(N):
                    for it in range(NIT):
                        j = n * NIT + it
                        m.wait(self.T(f"dbx{b}_{slc}_{n}_{it}"))
                        m.wait(self.T(f"dA{b}_{slc}_{n}_{it}"))
                        if j >= 1:
                            m.wait(self.T(f"scanh{b}_{slc}_{(j - 1) // NIT}_{(j - 1) % NIT}"))
                        elif slc >= 1:
                            m.wait(self.T(f"scanh{b}_{slc - 1}_{N - 1}_{NIT - 1}"))
                        t = cv.inc(e.tensor_tensor_scan(
                            h_t[:, :], dA_t[j % 4][:, :], dBx_t[j % 4][:, :],
                            self.carry[:, n * NIT + it:n * NIT + it + 1],
                            ALU.mult, ALU.add))
                        m.wait(t)
                        cv.inc(e.tensor_copy(self.carry[:, n * NIT + it:n * NIT + it + 1],
                                             h_t[:, SLC - 1:SLC]))
                        if n == 0:
                            ins = e.tensor_mul(y_ch[:, it * SLC:(it + 1) * SLC],
                                               h_t[:, :], Cbc[n % 2][:, :])
                        else:
                            t2 = cv.inc(e.tensor_mul(ytmp[:, :], h_t[:, :], Cbc[n % 2][:, :]))
                            m.wait(t2)
                            ins = e.tensor_add(y_ch[:, it * SLC:(it + 1) * SLC],
                                               y_ch[:, it * SLC:(it + 1) * SLC],
                                               ytmp[:, :])
                        tok[f"scanh{b}_{slc}_{n}_{it}"] = csc.inc(ins)
                m.wait(self.T(f"scanh{b}_{slc}_{N - 1}_{NIT - 1}"))
                for it in range(NIT):
                    t = cv.inc(e.scalar_tensor_tensor(
                        y_ch[:, it * SLC:(it + 1) * SLC], u_ch[:, it * SLC:(it + 1) * SLC],
                        self.D_sb[:, it:it + 1], y_ch[:, it * SLC:(it + 1) * SLC],
                        ALU.mult, ALU.add))
                    m.wait(t)
                    if slc >= 1 and it == 0:
                        m.wait(self.T(f"ygw{b}_{slc - 1}"))
                    cv.inc(e.tensor_mul(yg_ch[:, it * SLC:(it + 1) * SLC],
                                        y_ch[:, it * SLC:(it + 1) * SLC],
                                        g_ch[:, it * SLC:(it + 1) * SLC]))
                tok[f"ygv{b}_{slc}"] = cv.tok()
            self.on("vector", s_sc)

            def s_yw(e, m, slc=slc, l0=l0):
                m.wait(self.T(f"ygv{b}_{slc}"))
                for q in range(max(1, SLC // TC)):
                    rank = (l0 + q * TC) // TC
                    q0 = (l0 + q * TC) % TC
                    w = min(SLC, TC)
                    for it in range(NIT):
                        self.ygW.inc(e.dma_start(
                            out=self.a2ain_d[b][rank * IC + it * P:rank * IC + (it + 1) * P,
                                                q0:q0 + w],
                            in_=yg_ch[:, it * SLC + q * w:it * SLC + q * w + w]), 16)
                if c.debug:
                    for it in range(NIT):
                        self.dbgY.inc(e.dma_start(
                            out=self.dbg["y"][b, it * P:(it + 1) * P, l0:l0 + SLC],
                            in_=y_ch[:, it * SLC:(it + 1) * SLC]), 16)
                    tok[f"ydump{b}_{slc}"] = self.dbgY.tok()
                tok[f"ygw{b}_{slc}"] = self.ygW.tok()
            self.on("sync", s_yw)

        tok[f"yg_w{b}"] = self.ygW.tok()
        tok[f"dbgB{b}"] = self.dbgc.tok()
        tok[f"dbgBD{b}"] = self.dbgD.tok()
        tok[f"dbgBY{b}"] = self.dbgY.tok()

        def s_a2a(e, m):
            m.wait(self.T(f"yg_w{b}"))
            with allow_ici_all_to_all():
                tok[f"A2A{b}"] = cg.inc(e.collective_compute(
                    "AllToAll", ALU.bypass, replica_groups=[list(range(NCORE))],
                    ins=[self.a2ain_d[b].ap().opt()], outs=[self.a2aout_d[b].ap().opt()]))
        self.on("gpsimd", s_a2a)

    def ph_C(self, b):
        c, tok = self.c, self.tok
        cd, cq, cv, ca, ct = self.cd, self.cq, self.cv, self.ca, self.ct
        H, N, L, IC, TC, NHT, NTT, E = c.H, c.N, c.L, c.IC, c.TC, c.NHT, c.NTT, c.E
        NKI = c.I // P
        psA, psB, psC, psD = self.psA, self.psB, self.psC, self.psD
        PSL = self.PSL
        y_all = self.vbf(self.AR1, 0, NKI * TC)
        x2_tm = self.v32(self.AR2, 0, NTT * H)
        x_q = [self.v32(self.AR5, j * 1024, 1024) for j in range(2)]
        gabc = self.v32(self.AR3, 0, H)
        cbbc = self.v32(self.AR3, H, H)
        sbbc = self.v32(self.AR3, 2 * H, H)
        xsq = self.v32(self.AR3, 3 * H, H)
        h2_tm = self.v32(self.AR4, 0, H)
        h2b = self.vbf(self.AR6, 0, NHT * TC)
        h2b3 = h2b.rearrange("p (ht t) -> p ht t", t=TC)
        rows = self.rows
        R0, RS = 0, 2048

        def s_yld(e, m):
            m.wait(self.T(f"A2A{b}"))
            for kt in range(NKI):
                self.ya.inc(e.dma_start(out=y_all[:, kt * TC:(kt + 1) * TC],
                                        in_=self.a2aout_d[b][kt * P:(kt + 1) * P, :]), 16)
            tok[f"yall{b}"] = self.ya.tok()
        self.on("sync", s_yld)

        def s_r1(e, m):
            m.wait(self.T("modflat"))
            m.wait(self.T(f"ygw{b}_{c.NSLC - 1}"))
            if b > 0:
                m.wait(self.T(f"cbbc{b - 1}"))
            cd.inc(e.dma_start(out=rows[0:1, R0:R0 + H],
                               in_=self.modflat_d[b:b + 1, 2 * H:3 * H]), 16)
            tok[f"garow{b}"] = cd.tok()
        self.on("sync", s_r1)

        self.emit_clamp(b, R0, RS, f"garow{b}", f"gacl{b}")

        def s_gab(e, m):
            m.wait(self.T(f"gacl{b}"))
            m.wait((self.cbc, self.cbc.count))
            self.cbc.inc(e.dma_start(out=self.rowsd[0:1, 0:H], in_=rows[0:1, R0:R0 + H]), 16)
            t = (self.cbc, self.cbc.count)
            m.wait(t)
            self.cbc.inc(e.dma_start(out=gabc[:, :], in_=self.bcast(
                gabc[:, :], self.rowsd[0:1, 0:H])), 16)
            self.cbc.inc(e.dma_start(out=sbbc[:, :], in_=self.bcast(
                sbbc[:, :], self.modflat_d[b:b + 1, 3 * H:4 * H])), 16)
            tok[f"gabc{b}"] = self.cbc.tok()
        self.on("sync", s_gab)

        def s_cbr(e, m):
            m.wait(self.T(f"gabc{b}"))
            cd.inc(e.dma_start(out=rows[0:1, R0:R0 + H],
                               in_=self.modflat_d[b:b + 1, 4 * H:5 * H]), 16)
            tok[f"cbrow{b}"] = cd.tok()
        self.on("sync", s_cbr)

        def s_cb1(e, m):
            m.wait(self.T(f"cbrow{b}"))
            tok[f"cb1{b}"] = cv.inc(e.tensor_scalar_add(
                rows[0:1, R0:R0 + H], rows[0:1, R0:R0 + H], 1.0))
        self.on("vector", s_cb1)

        def s_cbb(e, m):
            m.wait(self.T(f"cb1{b}"))
            m.wait((self.cbc, self.cbc.count))
            self.cbc.inc(e.dma_start(out=self.rowsd[0:1, H:2 * H], in_=rows[0:1, R0:R0 + H]), 16)
            t = (self.cbc, self.cbc.count)
            m.wait(t)
            self.cbc.inc(e.dma_start(out=cbbc[:, :], in_=self.bcast(
                cbbc[:, :], self.rowsd[0:1, H:2 * H])), 16)
            tok[f"cbbc{b}"] = self.cbc.tok()
        self.on("sync", s_cbb)
        tok[f"bcC{b}"] = tok[f"cbbc{b}"]

        for g in range(2):
            for kt in range(NKI):
                j = g * NKI + kt

                def s_wl(e, m, g=g, kt=kt, j=j):
                    if j >= 2:
                        m.wait(self.T(f"opmm{b}_{j - 2}"))
                    self.stg[j % 2].inc(e.dma_start(
                        out=self.wst[j % 2][:, :],
                        in_=self.opT_d[kt * P:(kt + 1) * P, g * 1024:(g + 1) * 1024]), 16)
                    tok[f"opw{b}_{j}"] = self.stg[j % 2].tok()
                self.on("gpsimd", s_wl)

                def s_om(e, m, g=g, kt=kt, j=j):
                    if j == 0:
                        m.wait(self.T(f"yall{b}"))
                        m.wait(self.T(f"bcC{b}"))
                    if g == 1 and kt == 0:
                        for sl in range(2 * NTT):
                            m.wait(self.T(f"opev{b}_0_{sl}"))
                    m.wait(self.T(f"opw{b}_{j}"))
                    for lt in range(NTT):
                        for hc in range(2):
                            ct.inc(e.matmul(
                                PSL[lt * 2 + hc][:, 0:512],
                                y_all[:, kt * TC + lt * P:kt * TC + (lt + 1) * P],
                                self.wst[j % 2][:, hc * 512:(hc + 1) * 512],
                                start=(kt == 0), stop=(kt == NKI - 1)))
                    tok[f"opmm{b}_{j}"] = ct.tok()
                self.on("tensor", s_om)
            tok[f"opg{b}_{g}"] = self.ct.tok()

            for lt in range(NTT):
                def s_xq(e, m, g=g, lt=lt):
                    if lt >= 2:
                        m.wait(self.T(f"opev{b}_{g}_{(lt - 2) * 2 + 1}"))
                    elif g == 1 and lt == 0:
                        m.wait(self.T(f"opev{b}_0_{2 * NTT - 1}"))
                    self.xq[lt % 2].inc(e.dma_start(out=x_q[lt % 2][:, :],
                                        in_=self.xq_in[b, lt * P:(lt + 1) * P,
                                                       g * 1024:(g + 1) * 1024]), 16)
                    tok[f"xq{b}_{g}_{lt}"] = self.xq[lt % 2].tok()
                self.on("sync", s_xq)

                def s_oe(e, m, g=g, lt=lt):
                    m.wait(self.T(f"opg{b}_{g}"))
                    m.wait(self.T(f"xq{b}_{g}_{lt}"))
                    if c.debug:
                        m.wait(self.T(f"dbgB{b}")); m.wait(self.T(f"dbgBD{b}")); m.wait(self.T(f"dbgBY{b}"))
                    for hc in range(2):
                        sl = lt * 2 + hc
                        hcol = g * 1024 + hc * 512
                        if sl >= 1:
                            m.wait(self.T(f"opev{b}_{g}_{sl - 1}"))
                        elif g == 1:
                            m.wait(self.T(f"opev{b}_0_{2 * NTT - 1}"))
                        t = cv.inc(e.tensor_mul(self.tev[:, 0:512], PSL[sl][:, 0:512],
                                                gabc[:, hcol:hcol + 512]))
                        m.wait(t)
                        tok[f"opev{b}_{g}_{sl}"] = cv.inc(e.tensor_add(
                            x2_tm[:, lt * H + hcol:lt * H + hcol + 512],
                            self.tev[:, 0:512], x_q[lt % 2][:, hc * 512:(hc + 1) * 512]))
                self.on("vector", s_oe)
        tok[f"x2{b}"] = self.cv.tok()

        if c.debug:
            def s_xd(e, m):
                m.wait(self.T(f"x2{b}"))
                for lt in range(NTT):
                    self.dbgc.inc(e.dma_start(out=self.dbg["x2"][b, lt * P:(lt + 1) * P, :],
                                              in_=x2_tm[:, lt * H:(lt + 1) * H]), 16)
            self.on("sync", s_xd)

        for lt in range(NTT):
            def s_sq(e, m, lt=lt):
                m.wait(self.T(f"x2{b}"))
                if lt >= 1:
                    m.wait(self.T(f"sqred{b}_{lt - 1}"))
                tok[f"x2sq{b}_{lt}"] = ca.inc(e.activation(
                    xsq[:, :], x2_tm[:, lt * H:(lt + 1) * H], AF.Square))
            self.on("scalar", s_sq)

            def s_v1(e, m, lt=lt):
                m.wait(self.T(f"x2{b}"))
                if lt >= 1:
                    m.wait(self.T(f"h2tm{b}_{lt - 1}"))
                t0 = cv.inc(e.reduce_sum(self.scol[:, 0:1], x2_tm[:, lt * H:(lt + 1) * H], axis=AXX))
                m.wait(self.T(f"x2sq{b}_{lt}"))
                tok[f"sqred{b}_{lt}"] = cv.inc(e.reduce_sum(self.scol[:, 1:2], xsq[:, :], axis=AXX))
                m.wait(t0)
                t = cv.inc(e.tensor_scalar_mul(self.scol[:, 0:1], self.scol[:, 0:1], 1.0 / H))
                m.wait(self.T(f"sqred{b}_{lt}"))
                t = cv.inc(e.tensor_scalar_mul(self.scol[:, 1:2], self.scol[:, 1:2], 1.0 / H))
                m.wait(t)
                t = cv.inc(e.tensor_mul(self.scol[:, 2:3], self.scol[:, 0:1], self.scol[:, 0:1]))
                m.wait(t)
                t = cv.inc(e.tensor_sub(self.scol[:, 1:2],
                                        self.scol[:, 1:2], self.scol[:, 2:3]))
                m.wait(t)
                tok[f"varc{b}_{lt}"] = cv.inc(e.tensor_scalar_add(
                    self.scol[:, 1:2], self.scol[:, 1:2], 1e-5))
            self.on("vector", s_v1)

            def s_a1(e, m, lt=lt):
                m.wait(self.T(f"varc{b}_{lt}"))
                tok[f"lnc{b}_{lt}"] = ca.inc(e.activation(self.scol[:, 3:4],
                                                          self.scol[:, 1:2], AF.Ln))
            self.on("scalar", s_a1)

            def s_vh(e, m, lt=lt):
                m.wait(self.T(f"lnc{b}_{lt}"))
                tok[f"nlc{b}_{lt}"] = cv.inc(e.tensor_scalar_mul(
                    self.scol[:, 3:4], self.scol[:, 3:4], -0.5))
            self.on("vector", s_vh)

            def s_a1b(e, m, lt=lt):
                m.wait(self.T(f"nlc{b}_{lt}"))
                tok[f"rstdc{b}_{lt}"] = ca.inc(e.activation(self.scol[:, 3:4],
                                                            self.scol[:, 3:4], AF.Exp))
            self.on("scalar", s_a1b)

            def s_v2(e, m, lt=lt):
                m.wait(self.T(f"rstdc{b}_{lt}")); m.wait(self.T(f"bcC{b}"))
                if lt >= 1:
                    m.wait(self.T(f"h2tr{b}_{lt - 1}"))
                t = cv.inc(e.tensor_scalar(h2_tm[:, :], x2_tm[:, lt * H:(lt + 1) * H],
                                           self.scol[:, 0:1], self.scol[:, 3:4],
                                           ALU.subtract, ALU.mult))
                m.wait(t)
                t = cv.inc(e.tensor_mul(h2_tm[:, :], h2_tm[:, :], cbbc[:, :]))
                m.wait(t)
                tok[f"h2tm{b}_{lt}"] = cv.inc(e.tensor_add(h2_tm[:, :], h2_tm[:, :], sbbc[:, :]))
            self.on("vector", s_v2)

            def s_t1(e, m, lt=lt):
                m.wait(self.T(f"h2tm{b}_{lt}"))
                if lt == 0:
                    m.wait(self.T(f"x2{b}"))
                if lt >= 1:
                    m.wait(self.T(f"h2ev{b}_{lt - 1}"))
                    m.wait(self.T(f"rtc{b}_{lt - 1}_{NHT - 1}"))
                for ht in range(NHT):
                    dstp = psA if ht < 8 else psB
                    ct.inc(e.transpose(dstp[:, (ht % 8) * P:(ht % 8 + 1) * P],
                                       h2_tm[:, ht * P:(ht + 1) * P], self.id_f32[:, :]))
                tok[f"h2tr{b}_{lt}"] = ct.tok()
            self.on("tensor", s_t1)

            def s_a2(e, m, lt=lt):
                m.wait(self.T(f"h2tr{b}_{lt}"))
                ca.inc(e.activation(h2b3[:, 0:min(8, NHT), lt * P:(lt + 1) * P],
                                    psA[:, 0:min(8, NHT) * P], AF.Copy))
                if NHT > 8:
                    ca.inc(e.activation(h2b3[:, 8:NHT, lt * P:(lt + 1) * P],
                                        psB[:, 0:(NHT - 8) * P], AF.Copy))
                tok[f"h2ev{b}_{lt}"] = ca.tok()
            self.on("scalar", s_a2)

            for ht in range(NHT):
                def s_rc(e, m, lt=lt, ht=ht):
                    m.wait(self.T(f"h2tr{b}_{lt}"))
                    if lt * NHT + ht >= 4:
                        pj = lt * NHT + ht - 4
                        m.wait(self.T(f"rtm2{b}_{pj // NHT}_{pj % NHT}"))
                    src = psA if ht < 8 else psB
                    tok[f"rtc{b}_{lt}_{ht}"] = ca.inc(e.activation(
                        self.h2rt[ht % 4][:, :], src[:, (ht % 8) * P:(ht % 8 + 1) * P],
                        AF.Copy))
                self.on("scalar", s_rc)

                def s_rm(e, m, lt=lt, ht=ht):
                    m.wait(self.T(f"rtc{b}_{lt}_{ht}"))
                    ct.inc(e.matmul(psC[0:E, lt * P:(lt + 1) * P],
                                    self.gwT_sb[:, ht * E:(ht + 1) * E],
                                    self.h2rt[ht % 4][:, :],
                                    start=(ht == 0), stop=(ht == NHT - 1)))
                    ct.inc(e.matmul(psD[0:1, lt * P:(lt + 1) * P],
                                    self.sgwT_sb[:, ht:ht + 1],
                                    self.h2rt[ht % 4][:, :],
                                    start=(ht == 0), stop=(ht == NHT - 1)))
                    tok[f"rtm2{b}_{lt}_{ht}"] = ct.tok()
                self.on("tensor", s_rm)
                tok[f"h2done{b}"] = self.ca.tok()

        if c.debug:
            for ht in range(NHT):
                def s_h2c(e, m, ht=ht):
                    m.wait(self.T(f"h2done{b}"))
                    if ht >= 1:
                        m.wait(self.T(f"h2dd{b}_{ht - 1}"))
                    tok[f"h2dc{b}_{ht}"] = cv.inc(e.tensor_copy(
                        self.tev[:, 0:TC], h2b[:, ht * TC:(ht + 1) * TC]))
                self.on("vector", s_h2c)

                def s_h2w(e, m, ht=ht):
                    m.wait(self.T(f"h2dc{b}_{ht}"))
                    self.dbg2.inc(e.dma_start(
                        out=self.dbg["h2"][b, ht * P:(ht + 1) * P, :],
                        in_=self.tev[:, 0:TC]), 16)
                    tok[f"h2dd{b}_{ht}"] = self.dbg2.tok()
                self.on("sync", s_h2w)

        tok[f"rmm{b}"] = self.ct.tok()
        tok[f"sgmm{b}"] = self.ct.tok()

        def s_re(e, m):
            m.wait(self.T(f"rmm{b}"))
            tok[f"rsb{b}"] = cv.inc(e.tensor_copy(self.rsb[:, :], psC[0:E, 0:TC]))
        self.on("vector", s_re)

        def s_sg(e, m):
            m.wait(self.T(f"sgmm{b}")); m.wait(self.T(f"gacl{b}"))
            tok[f"sgrow{b}"] = ca.inc(e.activation(self.rows[0:1, 2048:2048 + TC],
                                                   psD[0:1, 0:TC], AF.Sigmoid))
        self.on("scalar", s_sg)

        def s_rtr(e, m):
            m.wait(self.T(f"rsb{b}"))
            m.wait(self.T(f"sgrow{b}"))
            for lt in range(NTT):
                ct.inc(e.transpose(psD[:, lt * E:(lt + 1) * E],
                                   self.rsb[0:E, lt * P:(lt + 1) * P], self.id_f32[0:E, 0:E]))
            tok[f"rtr{b}"] = ct.tok()
        self.on("tensor", s_rtr)

        def s_rc(e, m):
            m.wait(self.T(f"rtr{b}"))
            tok[f"rtm{b}"] = cv.inc(e.tensor_copy(self.r_tm[:, 0:NTT * E],
                                                  psD[:, 0:NTT * E]))
        self.on("vector", s_rc)

        for lt in range(NTT):
            sl = slice(lt * E, (lt + 1) * E)

            def s_m1(e, m, lt=lt, sl=sl):
                m.wait(self.T(f"rtm{b}"))
                tok[f"nmx{b}_{lt}"] = cv.inc(e.reduce_max(self.scol[:, 4:5],
                                                          self.r_tm[:, sl], axis=AXX,
                                                          negate=True))
            self.on("vector", s_m1)

            def s_ex(e, m, lt=lt, sl=sl):
                m.wait(self.T(f"nmx{b}_{lt}"))
                tok[f"smex{b}_{lt}"] = ca.inc(e.activation(
                    self.rt1[:, sl], self.r_tm[:, sl], AF.Exp, bias=self.scol[:, 4:5]))
            self.on("scalar", s_ex)

            def s_m2(e, m, lt=lt, sl=sl):
                m.wait(self.T(f"smex{b}_{lt}"))
                t = cv.inc(e.reduce_sum(self.scol[:, 5:6], self.rt1[:, sl], axis=AXX))
                m.wait(t)
                t = cv.inc(e.reciprocal(self.scol[:, 5:6], self.scol[:, 5:6]))
                m.wait(t)
                t = cv.inc(e.tensor_scalar_mul(self.r_tm[:, sl], self.rt1[:, sl],
                                               self.scol[:, 5:6]))
                m.wait(t)
                t = cv.inc(e.reduce_max(self.scol[:, 6:7], self.r_tm[:, sl], axis=AXX))
                m.wait(t)
                t = cv.inc(e.tensor_scalar(self.rt1[:, sl], self.r_tm[:, sl],
                                           self.scol[:, 6:7], None, ALU.is_equal))
                m.wait(t)
                t = cv.inc(e.scalar_tensor_tensor(self.rt1[:, sl], self.rt1[:, sl],
                                                  -1e30, self.r_tm[:, sl],
                                                  ALU.mult, ALU.add))
                m.wait(t)
                t = cv.inc(e.reduce_max(self.scol[:, 7:8], self.rt1[:, sl], axis=AXX))
                m.wait(t)
                t = cv.inc(e.tensor_scalar(self.rt1[:, sl], self.r_tm[:, sl],
                                           self.scol[:, 7:8], None, ALU.is_ge))
                m.wait(t)
                tok[f"cmb{b}_{lt}"] = cv.inc(e.tensor_mul(self.r_tm[:, sl],
                                                          self.r_tm[:, sl], self.rt1[:, sl]))
            self.on("vector", s_m2)
        tok[f"cmbv{b}"] = self.cv.tok()

        def s_ctr(e, m):
            m.wait(self.T(f"cmbv{b}"))
            for lt in range(NTT):
                ct.inc(e.transpose(psD[0:E, 512 + lt * P:512 + (lt + 1) * P],
                                   self.r_tm[:, lt * E:(lt + 1) * E], self.id_f32[:, :]))
            tok[f"ctr{b}"] = ct.tok()
        self.on("tensor", s_ctr)

        def s_ce(e, m):
            m.wait(self.T(f"ctr{b}"))
            tok[f"cmbsb{b}"] = cv.inc(e.tensor_copy(self.cmbsb[:, :],
                                                    psD[0:E, 512:512 + TC]))
        self.on("vector", s_ce)

        def s_cw(e, m):
            m.wait(self.T(f"cmbsb{b}"))
            self.cr[0].inc(e.dma_start(out=self.cmb_d[b][:, :], in_=self.cmbsb[:, :]), 16)
            tok[f"cmbw{b}"] = self.cr[0].tok()
            if c.debug:
                self.dbgc.inc(e.dma_start(out=self.dbg["cmb"][b, :, :], in_=self.cmbsb[:, :]), 16)
        self.on("sync", s_cw)

        def s_sb(e, m):
            m.wait(self.T(f"sgrow{b}"))
            m.wait((self.cbc, self.cbc.count))
            self.cbc.inc(e.dma_start(out=self.rowsd[0:1, 2 * H:2 * H + TC],
                                     in_=self.rows[0:1, 2048:2048 + TC]), 16)
            t = (self.cbc, self.cbc.count)
            m.wait(t)
            self.cbc.inc(e.dma_start(out=self.sgbc[:, b * TC:(b + 1) * TC], in_=self.bcast(
                self.sgbc[:, b * TC:(b + 1) * TC], self.rowsd[0:1, 2 * H:2 * H + TC])), 16)
            tok[f"sgbc{b}"] = self.cbc.tok()
        self.on("sync", s_sb)

        def s_x2w(e, m):
            m.wait(self.T(f"x2{b}"))
            for lt in range(NTT):
                self.cx2.inc(e.dma_start(out=self.x2_d[b][lt * P:(lt + 1) * P, :],
                                         in_=x2_tm[:, lt * H:(lt + 1) * H]), 16)
                tok[f"x2w{b}_{lt}"] = self.cx2.tok()
        self.on("sync", s_x2w)

        if b == 0:
            def s_h2w0(e, m):
                m.wait(self.T(f"h2ev0_{NTT - 1}"))
                if c.debug:
                    m.wait(self.T(f"h2dd0_{NHT - 1}"))
                self.ch2.inc(e.dma_start(out=self.h2b_d0[:, :], in_=h2b[:, :]), 16)
                tok["h2bw0"] = self.ch2.tok()
            self.on("sync", s_h2w0)

    def ph_D(self):
        c, tok = self.c, self.tok
        cd, cq, cv, ca, ct = self.cd, self.cq, self.cv, self.ca, self.ct
        B, H, TC, NHT, NTT, E = c.B, c.H, c.TC, c.NHT, c.NTT, c.E
        NFM, NFS = c.NFM, c.NFS
        psA, psB, psC, psD = self.psA, self.psB, self.psC, self.psD
        PSL = self.PSL
        NKI = c.I // P
        ymoe_b = [self.v32(self.AR1, 0, NHT * TC), self.v32(self.AR2, 0, NHT * TC)]
        h2b_b = [self.vbf(self.AR3, 0, NHT * TC), self.vbf(self.AR6, 0, NHT * TC)]
        s1_sb = self.v32(self.AR3, NHT * TC // 2, 8 * TC)
        he_b = [self.vbf(self.AR4, 0, NFM * TC),
                self.vbf(self.AR4, NFM * TC // 2 + 32, NFM * TC)]
        sho = NFM * TC + 64
        shf_b = [self.vbf(self.AR4, sho, 4 * TC),
                 self.vbf(self.AR4, sho + 2 * TC + 16, 4 * TC)]
        gbbc = self.v32(self.AR3, NHT * TC // 2, H)
        x2t = self.v32(self.AR3, NHT * TC // 2 + H, H)
        rows = self.rows
        mg_f = [list(range(g, min(g + 4, NFM))) for g in range(0, NFM, 4)]
        hgrp = [list(range(g, min(g + 4, NHT))) for g in range(0, NHT, 4)]
        NMG, NHG = len(mg_f), len(hgrp)
        lastC = B - 1

        def s_h2l(e, m):
            m.wait(self.T(f"h2tm{lastC}_{NTT - 1}"))
            m.wait(self.T("h2bw0"))
            self.ch2.inc(e.dma_start(out=h2b_b[0][:, :], in_=self.h2b_d0[:, :]), 16)
            tok["h2bl0"] = self.ch2.tok()
        self.on("sync", s_h2l)

        def s_init(e, m):
            m.wait(self.T(f"x2{lastC}"))
            cv.inc(e.memset(ymoe_b[0][:, :], 0.0))
            m.wait(self.T(f"x2w{lastC}_{NTT - 1}"))
            if c.debug:
                m.wait((self.dbgc, self.dbgc.count))
            cv.inc(e.memset(ymoe_b[1][:, :], 0.0))
            tok["ymz"] = cv.tok()
        self.on("vector", s_init)

        for b in range(B):
            def s_gr(e, m, b=b):
                m.wait(self.T("modflat"))
                m.wait(self.T(f"cbbc{lastC}"))
                m.wait(self.T(f"sgbc{lastC}"))
                if b > 0:
                    m.wait(self.T("gbw0"))
                self.cdM.inc(e.dma_start(out=rows[0:1, 0:H],
                                         in_=self.modflat_d[b:b + 1, 5 * H:6 * H]), 16)
                tok[f"gbrow{b}"] = self.cdM.tok()
            self.on("sync", s_gr)
            self.emit_clamp(b, 0, 2048, lambda b=b: tok[f"gbrow{b}"], f"gbcl{b}")

            def s_gw(e, m, b=b):
                m.wait(self.T(f"gbcl{b}"))
                self.cdM.inc(e.dma_start(out=self.rowsd[0:1, (3 + b) * H:(4 + b) * H],
                                         in_=rows[0:1, 0:H]), 16)
                tok[f"gbw{b}"] = self.cdM.tok()
            self.on("sync", s_gw)

        self.wstD = [self.wst[0], self.wst[1]] + \
                    [self.vbf(self.AR5, 512 * j, 1024) for j in range(4)]
        self.dring = []

        def wload(src_ap, width, cons_name):
            i = len(self.dring)
            sl = i % 6
            war = self.dring[i - 6] if i >= 6 else None
            self.dring.append(cons_name)

            def s_l(e, m, i=i, sl=sl, war=war, src_ap=src_ap, width=width):
                if war is not None:
                    m.wait(self.T(war))
                elif sl >= 2:
                    m.wait(self.T(f"x2{lastC}"))
                else:
                    m.wait(self.T(f"opmm{lastC}_{2 * NKI - 2 + sl}"))
                self.stg[sl].inc(e.dma_start(out=self.wstD[sl][:, 0:width],
                                             in_=src_ap), 16)
                tok[f"dwl{i}"] = self.stg[sl].tok()
            self.on("gpsimd", s_l)
            return i, sl

        def fc_psum_war(sl, groups):
            for gi in reversed(range(len(groups))):
                if sl < 2 * len(groups[gi]):
                    return gi
            return None

        for ee in range(E):
            def s_cbc2(e, m, ee=ee):
                for b in range(B):
                    m.wait(self.T(f"cmbw{b}"))
                if ee >= 1:
                    m.wait(self.T(f"hev{ee - 1}_{NMG - 1}"))
                for b in range(B):
                    self.cr[ee % 2].inc(e.dma_start(
                        out=self.cmbbc[:, b * TC:(b + 1) * TC],
                        in_=self.bcast(self.cmbbc[:, b * TC:(b + 1) * TC],
                                       self.cmb_d[b][ee:ee + 1, :])), 16)
                tok[f"cbc{ee}"] = (self.cr[ee % 2], self.cr[ee % 2].count)
            self.on("sync", s_cbc2)

            for gi, grp in enumerate(mg_f):
                g0, gn = grp[0], len(grp)
                for wi, wsrc in ((0, self.fc1T_d[ee]), (1, self.fc2T_d[ee])):
                    for kt in range(NHT):
                        li, sl_w = wload(
                            wsrc[kt * P:(kt + 1) * P, g0 * P:(g0 + gn) * P],
                            gn * P, f"fmm{ee}_{wi}_{gi}_{kt}")

                        def s_fm(e, m, ee=ee, wi=wi, gi=gi, gn=gn, kt=kt,
                                 li=li, sl_w=sl_w):
                            if kt == 0:
                                for b in range(B):
                                    m.wait(self.T(f"h2done{b}"))
                                m.wait(self.T("h2bl0"))
                                if wi == 0 and gi == 0:
                                    m.wait(self.T(f"cbc{ee}"))
                            m.wait(self.T(f"dwl{li}"))
                            for mi in range(gn):
                                for b in range(B):
                                    sl = mi * 2 + b
                                    if kt == 0:
                                        if wi == 1:
                                            m.wait(self.T(f"silt{ee}_{gi}_{sl}"))
                                        elif ee == 0 and gi == 0:
                                            m.wait(self.T(f"cmbw{lastC}"))
                                        elif gi == 0:
                                            m.wait(self.T(f"ymt{ee - 1}_{NHG - 1}_{sl}"))
                                        else:
                                            m.wait(self.T(f"hepf{ee}_{gi - 1}_{sl}"))
                                    ct.inc(e.matmul(PSL[sl][:, 0:TC],
                                                    self.wstD[sl_w][:, mi * P:(mi + 1) * P],
                                                    h2b_b[b][:, kt * TC:(kt + 1) * TC],
                                                    start=(kt == 0), stop=(kt == NHT - 1)))
                            tok[f"fmm{ee}_{wi}_{gi}_{kt}"] = ct.tok()
                        self.on("tensor", s_fm)

                    if wi == 0:
                        def s_si(e, m, ee=ee, gi=gi, gn=gn):
                            m.wait(self.T(f"fmm{ee}_0_{gi}_{NHT - 1}"))
                            for mi in range(gn):
                                for b in range(B):
                                    sl = mi * 2 + b
                                    tok[f"silt{ee}_{gi}_{sl}"] = ca.inc(
                                        e.activation(s1_sb[:, sl * TC:(sl + 1) * TC],
                                                     PSL[sl][:, 0:TC], AF.Silu))
                            tok[f"sil{ee}_{gi}"] = ca.tok()
                        self.on("scalar", s_si)
                    else:
                        def s_he(e, m, ee=ee, gi=gi, g0=g0, gn=gn):
                            if c.debug:
                                for b in range(B):
                                    m.wait(self.T(f"dbgB{b}")); m.wait(self.T(f"dbgBD{b}"))
                                    m.wait(self.T(f"dbgBY{b}")); m.wait(self.T(f"h2dd{b}_{NHT - 1}"))
                            m.wait(self.T(f"fmm{ee}_1_{gi}_{NHT - 1}"))
                            m.wait(self.T(f"sil{ee}_{gi}"))
                            m.wait(self.T(f"cbc{ee}"))
                            for mi in range(gn):
                                for b in range(B):
                                    sl = mi * 2 + b
                                    m.wait((cv, cv.count))
                                    tok[f"hepf{ee}_{gi}_{sl}"] = cv.inc(
                                        e.tensor_mul(self.tev[:, 0:TC],
                                                     s1_sb[:, sl * TC:(sl + 1) * TC],
                                                     PSL[sl][:, 0:TC]))
                                    m.wait(self.T(f"hepf{ee}_{gi}_{sl}"))
                                    tok[f"hew{ee}_{g0 + mi}_{b}"] = cv.inc(e.tensor_mul(
                                        he_b[b][:, (g0 + mi) * TC:(g0 + mi + 1) * TC],
                                        self.tev[:, 0:TC], self.cmbbc[:, b * TC:(b + 1) * TC]))
                            tok[f"hev{ee}_{gi}"] = cv.tok()
                        self.on("vector", s_he)

            for g2, grp2 in enumerate(hgrp):
                h0, hn = grp2[0], len(grp2)
                for kt in range(NFM):
                    li, sl_w = wload(
                        self.prT_d[ee][kt * P:(kt + 1) * P, h0 * P:(h0 + hn) * P],
                        hn * P, f"pmm{ee}_{g2}_{kt}")

                    def s_pm(e, m, ee=ee, g2=g2, hn=hn, kt=kt, li=li, sl_w=sl_w):
                        for b in range(B):
                            m.wait(self.T(f"hew{ee}_{kt}_{b}"))
                        m.wait(self.T(f"dwl{li}"))
                        for mi in range(hn):
                            for b in range(B):
                                sl = mi * 2 + b
                                if kt == 0:
                                    if g2 >= 1:
                                        m.wait(self.T(f"ymt{ee}_{g2 - 1}_{sl}"))
                                    else:
                                        lg = fc_psum_war(sl, mg_f)
                                        if lg is not None:
                                            m.wait(self.T(f"hepf{ee}_{lg}_{sl}"))
                                        elif ee >= 1:
                                            m.wait(self.T(f"ymt{ee - 1}_{NHG - 1}_{sl}"))
                                        else:
                                            m.wait(self.T(f"cmbw{lastC}"))
                                ct.inc(e.matmul(PSL[sl][:, 0:TC],
                                                self.wstD[sl_w][:, mi * P:(mi + 1) * P],
                                                he_b[b][:, kt * TC:(kt + 1) * TC],
                                                start=(kt == 0), stop=(kt == NFM - 1)))
                        tok[f"pmm{ee}_{g2}_{kt}"] = ct.tok()
                    self.on("tensor", s_pm)

                def s_pe(e, m, ee=ee, g2=g2, h0=h0, hn=hn):
                    m.wait(self.T(f"pmm{ee}_{g2}_{NFM - 1}"))
                    m.wait(self.T("ymz"))
                    for mi in range(hn):
                        for b in range(B):
                            mt = h0 + mi
                            tok[f"ymt{ee}_{g2}_{mi * 2 + b}"] = cv.inc(
                                e.tensor_add(ymoe_b[b][:, mt * TC:(mt + 1) * TC],
                                             ymoe_b[b][:, mt * TC:(mt + 1) * TC],
                                             PSL[mi * 2 + b][:, 0:TC]))
                    tok[f"ymadd{ee}_{g2}"] = cv.tok()
                self.on("vector", s_pe)

        nfc = (NFS + 3) // 4
        for fc in range(nfc):
            f0 = fc * 4
            fn = min(4, NFS - f0)
            sgrp = [list(range(fn))]
            for wi, wsrc in ((0, self.sf1T_d), (1, self.sf2T_d)):
                for kt in range(NHT):
                    li, sl_w = wload(
                        wsrc[kt * P:(kt + 1) * P, f0 * P:(f0 + fn) * P],
                        fn * P, f"smm{fc}_{wi}_{kt}")

                    def s_sm(e, m, fc=fc, fn=fn, wi=wi, kt=kt, li=li, sl_w=sl_w):
                        m.wait(self.T(f"dwl{li}"))
                        for mi in range(fn):
                            for b in range(B):
                                sl = mi * 2 + b
                                if kt == 0:
                                    if wi == 1:
                                        m.wait(self.T(f"ssilt{fc}_{sl}"))
                                    elif fc == 0:
                                        m.wait(self.T(f"ymt{E - 1}_{NHG - 1}_{sl}"))
                                    else:
                                        m.wait(self.T(f"yst{fc - 1}_{NHG - 1}_{sl}"))
                                ct.inc(e.matmul(PSL[sl][:, 0:TC],
                                                self.wstD[sl_w][:, mi * P:(mi + 1) * P],
                                                h2b_b[b][:, kt * TC:(kt + 1) * TC],
                                                start=(kt == 0), stop=(kt == NHT - 1)))
                        tok[f"smm{fc}_{wi}_{kt}"] = ct.tok()
                    self.on("tensor", s_sm)

                if wi == 0:
                    def s_ss(e, m, fc=fc, fn=fn):
                        m.wait(self.T(f"smm{fc}_0_{NHT - 1}"))
                        for mi in range(fn):
                            for b in range(B):
                                sl = mi * 2 + b
                                tok[f"ssilt{fc}_{sl}"] = ca.inc(
                                    e.activation(s1_sb[:, sl * TC:(sl + 1) * TC],
                                                 PSL[sl][:, 0:TC], AF.Silu))
                        tok[f"ssil{fc}"] = ca.tok()
                    self.on("scalar", s_ss)
                else:
                    def s_sh(e, m, fc=fc, fn=fn):
                        m.wait(self.T(f"smm{fc}_1_{NHT - 1}"))
                        m.wait(self.T(f"ssil{fc}"))
                        for mi in range(fn):
                            for b in range(B):
                                sl = mi * 2 + b
                                t = cv.inc(e.tensor_mul(shf_b[b][:, mi * TC:(mi + 1) * TC],
                                                        s1_sb[:, sl * TC:(sl + 1) * TC],
                                                        PSL[sl][:, 0:TC]))
                                tok[f"shpf{fc}_{sl}"] = t
                                tok[f"shw{fc}_{mi}_{b}"] = t
                        tok[f"shev{fc}"] = cv.tok()
                    self.on("vector", s_sh)

            for g2, grp2 in enumerate(hgrp):
                h0, hn = grp2[0], len(grp2)
                for ktc in range(fn):
                    li, sl_w = wload(
                        self.sprT_d[(f0 + ktc) * P:(f0 + ktc + 1) * P,
                                    h0 * P:(h0 + hn) * P],
                        hn * P, f"spm{fc}_{g2}_{ktc}")

                    def s_spm(e, m, fc=fc, fn=fn, g2=g2, hn=hn, ktc=ktc,
                              li=li, sl_w=sl_w):
                        for b in range(B):
                            m.wait(self.T(f"shw{fc}_{ktc}_{b}"))
                        m.wait(self.T(f"dwl{li}"))
                        for mi in range(hn):
                            for b in range(B):
                                sl = mi * 2 + b
                                if ktc == 0:
                                    if g2 >= 1:
                                        m.wait(self.T(f"yst{fc}_{g2 - 1}_{sl}"))
                                    else:
                                        lg = fc_psum_war(sl, [list(range(fn))])
                                        if lg is not None:
                                            m.wait(self.T(f"shpf{fc}_{sl}"))
                                        elif fc >= 1:
                                            m.wait(self.T(f"yst{fc - 1}_{NHG - 1}_{sl}"))
                                        else:
                                            m.wait(self.T(f"ymt{E - 1}_{NHG - 1}_{sl}"))
                                ct.inc(e.matmul(PSL[sl][:, 0:TC],
                                                self.wstD[sl_w][:, mi * P:(mi + 1) * P],
                                                shf_b[b][:, ktc * TC:(ktc + 1) * TC],
                                                start=(ktc == 0), stop=(ktc == fn - 1)))
                        tok[f"spm{fc}_{g2}_{ktc}"] = ct.tok()
                    self.on("tensor", s_spm)

                def s_spe(e, m, fc=fc, fn=fn, g2=g2, h0=h0, hn=hn):
                    m.wait(self.T(f"spm{fc}_{g2}_{fn - 1}"))
                    for b in range(B):
                        m.wait(self.T(f"sgbc{b}"))
                    m.wait(self.T("ymz"))
                    for mi in range(hn):
                        for b in range(B):
                            mt = h0 + mi
                            m.wait((cv, cv.count))
                            t = cv.inc(e.tensor_mul(self.tev[:, 0:TC],
                                                    PSL[mi * 2 + b][:, 0:TC],
                                                    self.sgbc[:, b * TC:(b + 1) * TC]))
                            m.wait(t)
                            tok[f"yst{fc}_{g2}_{mi * 2 + b}"] = cv.inc(
                                e.tensor_add(ymoe_b[b][:, mt * TC:(mt + 1) * TC],
                                             ymoe_b[b][:, mt * TC:(mt + 1) * TC],
                                             self.tev[:, 0:TC]))
                    tok[f"ysadd{fc}_{g2}"] = cv.tok()
                self.on("vector", s_spe)

        tok["ysadd_last"] = tok[f"ysadd{nfc - 1}_{NHG - 1}"]

        for b in range(B):
            def s_gbb(e, m, b=b):
                m.wait(self.T("ysadd_last"))
                m.wait(self.T(f"gbw{b}"))
                m.wait((self.cbc, self.cbc.count))
                self.cbc.inc(e.dma_start(out=gbbc[:, :], in_=self.bcast(
                    gbbc[:, :], self.rowsd[0:1, (3 + b) * H:(4 + b) * H])), 16)
                tok[f"gbbcf{b}"] = self.cbc.tok()
            self.on("sync", s_gbb)

            for lt in range(NTT):
                def s_xl(e, m, b=b, lt=lt):
                    m.wait(self.T("ysadd_last"))
                    if lt > 0:
                        m.wait(self.T(f"x3w{b}_{lt - 1}"))
                    elif b > 0:
                        m.wait(self.T(f"x3w{b - 1}_{NTT - 1}"))
                    self.cx2.inc(e.dma_start(out=x2t[:, :],
                                             in_=self.x2_d[b][lt * P:(lt + 1) * P, :]), 16)
                    tok[f"x2l{b}_{lt}"] = self.cx2.tok()
                self.on("sync", s_xl)

                def s_ft(e, m, b=b, lt=lt):
                    if b == 0 and lt == 0:
                        m.wait(self.T("ysadd_last"))
                    if lt >= 1:
                        m.wait(self.T(f"fev{b}_{lt - 1}"))
                    elif b >= 1:
                        m.wait(self.T(f"fev{b - 1}_{NTT - 1}"))
                    for ctile in range(NHT):
                        dstp = psA if ctile < 8 else psB
                        ct.inc(e.transpose(
                            dstp[:, (ctile % 8) * P:(ctile % 8 + 1) * P],
                            ymoe_b[b][:, ctile * TC + lt * P:ctile * TC + (lt + 1) * P],
                            self.id_f32[:, :]))
                    tok[f"ftr{b}_{lt}"] = ct.tok()
                self.on("tensor", s_ft)

                def s_fe(e, m, b=b, lt=lt):
                    m.wait(self.T(f"gbbcf{b}"))
                    m.wait(self.T(f"ftr{b}_{lt}"))
                    m.wait(self.T(f"x2l{b}_{lt}"))
                    for j in range(H // 512):
                        pp = psA[:, (j % 2) * 512:(j % 2) * 512 + 512] if j < 2 else \
                            psB[:, ((j - 2) % 2) * 512:((j - 2) % 2) * 512 + 512]
                        m.wait((cv, cv.count))
                        t = cv.inc(e.tensor_mul(self.tev[:, 0:512], pp,
                                                gbbc[:, j * 512:(j + 1) * 512]))
                        m.wait(t)
                        cv.inc(e.tensor_add(x2t[:, j * 512:(j + 1) * 512],
                                            self.tev[:, 0:512],
                                            x2t[:, j * 512:(j + 1) * 512]))
                    tok[f"fev{b}_{lt}"] = cv.tok()
                self.on("vector", s_fe)

                def s_fw(e, m, b=b, lt=lt):
                    m.wait(self.T(f"fev{b}_{lt}"))
                    self.outW.inc(e.dma_start(out=self.out_d[b, lt * P:(lt + 1) * P, :],
                                              in_=x2t[:, 0:H]), 16)
                    tok[f"x3w{b}_{lt}"] = self.outW.tok()
                self.on("sync", s_fw)
        tok["out_all"] = self.outW.tok()

    def ph_final(self):
        chains_sync = [self.cd, self.cdM, self.xs, self.spU, self.spG, self.ug,
                       self.ssmW, self.ygW, self.ya, self.outW, self.dbgc,
                       self.cr[0], self.cr[1], self.xq[0], self.xq[1], self.dbg2, self.dbgD, self.dbgY,
                       self.rbc, self.cbc, self.cx2, self.ch2]
        chains_gp = [self.cg, self.cgM] + self.stg
        chains_sc = [self.st[0], self.st[1], self.cq]

        def s_s(e, m):
            for ch in chains_sync:
                if ch.count:
                    e.wait_ge(ch.sem, ch.count)
        self.on("sync", s_s)

        def s_p(e, m):
            for ch in chains_gp:
                if ch.count:
                    e.wait_ge(ch.sem, ch.count)
        self.on("gpsimd", s_p)

        def s_q(e, m):
            for ch in chains_sc:
                if ch.count:
                    e.wait_ge(ch.sem, ch.count)
        self.on("scalar", s_q)


def build(cfg):
    kb1 = KB(cfg)
    kb1.alloc()
    kb1.emit()
    rec = {k: (v[0].name, v[1]) for k, v in kb1.tok.items() if v is not None}
    kb = KB(cfg, rec=rec)
    kb.alloc()
    nc = kb.emit()
    return nc, kb.dbg


def _stage(inputs, cfg, core):
    c = cfg
    ic0 = core * c.IC
    tc0 = core * c.TC
    T = lambda a: np.ascontiguousarray(np.swapaxes(np.asarray(a), -1, -2))
    ipw = np.asarray(inputs["L0_attn_in_proj_w"])
    ipT = np.ascontiguousarray(
        np.concatenate([ipw[ic0:ic0 + c.IC], ipw[c.I + ic0:c.I + ic0 + c.IC]], 0).T)
    x = np.ascontiguousarray(np.asarray(inputs["L0_attn_x"], dtype=np.float32))
    d = dict(
        x=x,
        xq=x[:, tc0:tc0 + c.TC, :],
        ipT=ipT,
        convw=np.asarray(inputs["L0_attn_conv_w"])[ic0:ic0 + c.IC, 0, :],
        convb=np.asarray(inputs["L0_attn_conv_b"])[ic0:ic0 + c.IC, None],
        xpT=np.asarray(inputs["L0_attn_x_proj_w"]).T[ic0:ic0 + c.IC, :],
        dtpT=np.asarray(inputs["L0_attn_dt_proj_w"]).T[:, ic0:ic0 + c.IC],
        dtb=np.asarray(inputs["L0_attn_dt_proj_b"])[ic0:ic0 + c.IC, None],
        alog=np.asarray(inputs["L0_attn_A_log"])[ic0:ic0 + c.IC, :],
        dvec=np.asarray(inputs["L0_attn_D"])[ic0:ic0 + c.IC, None],
        opT=np.asarray(inputs["L0_attn_out_proj_w"]).T,
        fc1T=T(inputs["L0_ffn_expert_fc1"]),
        fc2T=T(inputs["L0_ffn_expert_fc2"]),
        prT=T(inputs["L0_ffn_expert_proj"]),
        sf1T=np.asarray(inputs["L0_ffn_shared_fc1"]).T,
        sf2T=np.asarray(inputs["L0_ffn_shared_fc2"]).T,
        sprT=np.asarray(inputs["L0_ffn_shared_proj"]).T,
        gwT=np.asarray(inputs["L0_ffn_gate_w"]).T,
        sgwT=np.asarray(inputs["L0_ffn_shared_gate_w"]).T,
        c_in=np.asarray(inputs["L0_c"]),
        wmod=np.asarray(inputs["L0_wmod"])[:, core * c.MODSL:(core + 1) * c.MODSL],
    )
    return {k: np.ascontiguousarray(np.asarray(v), dtype=np.float32) for k, v in d.items()}


def kernel(**inputs):
    cfg = CFG()
    in_maps = [_stage(inputs, cfg, core) for core in range(NCORE)]
    nc, _ = build(cfg)
    res = run_bass_kernel_spmd(nc, in_maps, core_ids=list(range(NCORE)))
    out = np.zeros((cfg.B, cfg.L, cfg.H), dtype=np.float32)
    for core in range(NCORE):
        o = np.asarray(res.results[core]["out"])
        out[:, core * cfg.TC:(core + 1) * cfg.TC, :] = o
    return out
